# version 17
# baseline (speedup 1.0000x reference)
import contextlib
import numpy as np
import ml_dtypes

import concourse.bass as bass
import concourse.mybir as mybir
from concourse.bass_utils import run_bass_kernel_spmd

BF16 = ml_dtypes.bfloat16
F32 = mybir.dt.float32
BF = mybir.dt.bfloat16
AF = mybir.ActivationFunctionType
ALU = mybir.AluOpType
AX = mybir.AxisListType

NCORE = 8
D_MODEL = 1024
D_HID = 4096
H, HKV, D = 16, 4, 64
C = 64
GLN = 16.0

ENGINES = ("sync", "gpsimd", "tensor", "vector", "scalar")


SELF_EDGE_ENGINES = {"vector": "sEV", "scalar": "sSC", "gpsimd": "sCS"}


class Ctx:

    def __init__(self, self_edges=False):
        self.q = {e: [] for e in ENGINES}
        self.ctr = {s: 0 for s in ("sWD0", "sWD1", "sCN", "sXD0", "sXD1", "sVD0", "sVD1",
                                   "sCS", "sMM", "sSC", "sEV", "sCC")}
        self.sems = {}
        self.self_edges = self_edges

    def emit(self, eng, waits, build, incs, is_dma=False):
        waits = [(s, v) for (s, v) in waits if v > 0]
        incs = list(incs)
        if self.self_edges and eng in SELF_EDGE_ENGINES and not is_dma:
            sname = SELF_EDGE_ENGINES[eng]
            assert any(s == sname for s, _ in incs), (eng, incs)
            if self.ctr[sname] > 0:
                waits = waits + [(sname, self.ctr[sname])]
        for s, a in incs:
            self.ctr[s] += a
        sems = self.sems

        def fn(e, waits=tuple(waits), incs=tuple(incs), build=build):
            for s, v in waits:
                e.wait_ge(sems[s], v)
            ins = build(e)
            for s, a in incs:
                ins = ins.then_inc(sems[s], a)
        self.q[eng].append(fn)
        return {s: self.ctr[s] for s, _ in incs}

    def barrier_waits(self):
        return [(s, self.ctr[s]) for s in ("sMM", "sEV", "sSC", "sCS")]

    def guard(self, eng, waits):
        sems = self.sems

        def fn(e, waits=tuple([(s, v) for s, v in waits if v > 0])):
            for s, v in waits:
                e.wait_ge(sems[s], v)
        self.q[eng].append(fn)


class Banks:

    def __init__(self, ctx, tensors):
        self.ctx = ctx
        self.t = tensors
        self.free_w = [[] for _ in tensors]
        self.ready = [None] * len(tensors)
        self.in_use = [False] * len(tensors)
        self.rr = 0

    def take(self):
        for _ in range(len(self.t)):
            i = self.rr % len(self.t)
            self.rr += 1
            if not self.in_use[i]:
                self.in_use[i] = True
                return i
        raise RuntimeError("PSUM bank pool exhausted: all banks in use")

    def mm_waits(self, i):
        w = self.free_w[i]
        self.free_w[i] = []
        return w

    def set_ready(self, i, sem, val):
        self.ready[i] = (sem, val)

    def evac_wait(self, i):
        return [self.ready[i]] if self.ready[i] else []

    def add_free(self, i, sem, val):
        self.free_w[i].append((sem, val))
        self.in_use[i] = False


DEBUG = False
SIM_SELF_EDGES = True

def build_program(seg=2048):
    assert seg % 128 == 0
    NT = seg // 512
    NTT = seg // 128
    NCH = seg // C
    TP = seg // 2
    NT2 = TP // 512

    nc = bass.Bass("TRN2", target_bir_lowering=False)

    dx = nc.dram_tensor("x", [D_MODEL, seg], F32, kind="ExternalInput").ap()
    dwq = nc.dram_tensor("wq", [8, 128, 1024], F32, kind="ExternalInput").ap()
    dwk = nc.dram_tensor("wk", [8, 128, 256], F32, kind="ExternalInput").ap()
    dwv = nc.dram_tensor("wv", [8, 128, 256], F32, kind="ExternalInput").ap()
    dwg1 = nc.dram_tensor("wg1", [8, 128, 16], F32, kind="ExternalInput").ap()
    dwg2a = nc.dram_tensor("wg2a", [17, 256], F32, kind="ExternalInput").ap()
    dwo = nc.dram_tensor("wo", [8, 128, 1024], F32, kind="ExternalInput").ap()
    dw1 = nc.dram_tensor("w1t", [32, 128, 1024], F32, kind="ExternalInput").ap()
    dw2 = nc.dram_tensor("w2t", [32, 128, 1024], F32, kind="ExternalInput").ap()
    dh1 = nc.dram_tensor("h1t", [32, 128, 4096], F32, kind="ExternalInput").ap()
    dh2 = nc.dram_tensor("h2t", [32, 128, 4096], F32, kind="ExternalInput").ap()
    dw3 = nc.dram_tensor("w3t", [8, 128, 4096], F32, kind="ExternalInput").ap()
    dsc = nc.dram_tensor("scl", [128, 137], F32, kind="ExternalInput").ap()
    dcst = nc.dram_tensor("cst", [128, 8], F32, kind="ExternalInput").ap()
    dmcat = nc.dram_tensor("mcat", [128, 256], F32, kind="ExternalInput").ap()
    dmaskP0 = nc.dram_tensor("maskP0", [128, 256], BF, kind="ExternalInput").ap()
    dmaskP1 = nc.dram_tensor("maskP1", [128, 256], BF, kind="ExternalInput").ap()
    dident = nc.dram_tensor("ident", [128, 128], BF, kind="ExternalInput").ap()
    dones = nc.dram_tensor("onesv", [128, 4], F32, kind="ExternalInput").ap()
    donesb = nc.dram_tensor("onesb", [128, 1], BF, kind="ExternalInput").ap()
    donesr = nc.dram_tensor("onesr", [2, 128], F32, kind="ExternalInput").ap()
    dhbc = nc.dram_tensor("hbc", [2, 128], F32, kind="ExternalInput").ap()
    dmu = nc.dram_tensor("muv", [128, 8], F32, kind="ExternalInput").ap()
    dnu = nc.dram_tensor("nuv", [128, 64], F32, kind="ExternalInput").ap()

    dy = nc.dram_tensor("y", [D_MODEL, seg], F32, kind="ExternalOutput").ap()
    if DEBUG:
        ddbg = {n: nc.dram_tensor("dbg_" + n, s, d, kind="ExternalOutput").ap()
                for n, s, d in (
                    ("xn", [128, 8 * seg], BF), ("qg0", [128, 4 * seg], BF),
                    ("kfm", [128, 2 * seg], BF), ("vtm", [128, 2 * seg], BF),
                    ("g1aug", [32, seg], BF), ("og0", [128, 4 * seg], BF),
                    ("og1", [128, 4 * seg], BF),
                    ("sf", [128, 128], F32), ("eGg0", [128, seg], BF),
                    ("gath", [128, 8 * 132], F32), ("sin", [128, 128], BF),
                    ("g1e", [16, 2048], F32), ("g1a2", [32, seg], BF),
                    ("w8", [128, 16], F32), ("muv", [128, 8], F32),
                    ("w8h", [128, 16], F32), ("qg1", [128, 4 * seg], BF),
                    ("ch5", [128, 1152], F32),
                    ("resid", [D_MODEL, seg], F32), ("gtm0", [128, 256], F32))}

    dresid = nc.dram_tensor("resid", [D_MODEL, seg], F32, kind="Internal").ap()
    dccin = nc.dram_tensor("ccin", [128, 132], F32, kind="Internal").ap()
    dccout = nc.dram_tensor("ccout", [NCORE * 128, 132], F32, kind="Internal").ap()

    ctx = Ctx(self_edges=SIM_SELF_EDGES)
    es = contextlib.ExitStack()
    sb = lambda name, shape, dt: es.enter_context(nc.sbuf_tensor(name, shape, dt))

    xn = sb("xn", [128, 8 * seg], BF)
    SEGQ = seg * 4
    HW = max(32 * TP, 8 * 1024 + 2 * SEGQ + 2 * seg + 4096)
    ha = sb("ha", [128, HW], BF)
    hb = sb("hb", [128, HW], BF)
    wo_b = ha[:, 0:8 * 1024]
    og = [ha[:, 8 * 1024 + j * SEGQ: 8 * 1024 + (j + 1) * SEGQ] for j in range(2)]
    kfm = ha[:, 8 * 1024 + 2 * SEGQ: 8 * 1024 + 2 * SEGQ + 2 * seg]
    eGg = [ha[:, 8 * 1024 + 2 * SEGQ + 2 * seg + j * seg:
              8 * 1024 + 2 * SEGQ + 2 * seg + (j + 1) * seg] for j in range(2)]
    wq_b = hb[:, 0:8 * 1024]
    qg = [hb[:, 8 * 1024 + j * SEGQ: 8 * 1024 + (j + 1) * SEGQ] for j in range(2)]
    vtm = hb[:, 8 * 1024 + 2 * SEGQ: 8 * 1024 + 2 * SEGQ + 2 * seg]
    wk_b = hb[:, 8 * 1024 + 2 * SEGQ + 2 * seg: 8 * 1024 + 2 * SEGQ + 2 * seg + 2048]
    wv_b = hb[:, 8 * 1024 + 2 * SEGQ + 2 * seg + 2048: 8 * 1024 + 2 * SEGQ + 2 * seg + 4096]
    sbf_all = hb[:, 0:NCH * 128]
    EGH0 = NCH * 128
    eGhat = [hb[:, EGH0 + j * seg: EGH0 + (j + 1) * seg] for j in range(2)]

    af = sb("af", [128, 5120], F32)
    wst = [af[:, 0:1024], af[:, 1024:2048]]
    xstg = [af[:, 2048:2560], af[:, 2560:3072]]
    sqscr = af[:, 3072:3584]
    rowbuf = af[:, 3584:5120]
    gtm = [af[:, 2048 + i * 256: 2048 + (i + 1) * 256] for i in range(4)]
    gscr = [af[:, 3072 + i * 256: 3072 + (i + 1) * 256] for i in range(2)]
    exg = [[af[:, 3584 + p * 384 + i * 64: 3584 + p * 384 + (i + 1) * 64] for i in range(6)]
           for p in range(2)]
    pref = af[:, 4352:4360]
    sf32 = [af[:, 4360 + j * 64: 4360 + (j + 1) * 64] for j in range(2)]
    eprefs = af[:, 4488:4552]
    decays = af[:, 4552:4616]
    blob = af[:, 0:132]
    gath = af[:, 132:1188]
    sin = [af[:, 1188 + i * 64: 1188 + (i + 1) * 64] for i in range(4)]
    w8 = af[:, 1444:1460]
    sqo = af[:, 2048:2560]
    orow = af[:, 2560:3584]
    xwo = [af[:, 4096 + i * 512: 4096 + (i + 1) * 512] for i in range(2)]
    silu_s = [af[:, 2048 + i * 512: 2048 + (i + 1) * 512] for i in range(2)]
    ysb = [af[:, 3072 + i * 512: 3072 + (i + 1) * 512] for i in range(2)]
    rstg = [af[:, 4096 + i * 512: 4096 + (i + 1) * 512] for i in range(2)]

    ab = sb("ab", [128, 8192], BF)
    wbf = [ab[:, 0:4096], ab[:, 4096:8192]]
    g1aug = ab[0:32, 0:seg]
    wg1_b = ab[:, 2048:2176]
    wg2a_b = ab[0:32, 2176:2432]
    qs = [[ab[:, 2432 + p * 512 + j * 256: 2432 + p * 512 + (j + 1) * 256] for j in range(2)]
          for p in range(2)]
    ks = [[ab[:, 3456 + p * 256 + j * 128: 3456 + p * 256 + (j + 1) * 128] for j in range(2)]
          for p in range(2)]
    kppf = [[ab[:, 3968 + p * 256 + j * 128: 3968 + p * 256 + (j + 1) * 128] for j in range(2)]
            for p in range(2)]
    kppt = [[ab[:, 4480 + p * 256 + j * 128: 4480 + p * 256 + (j + 1) * 128] for j in range(2)]
            for p in range(2)]
    pscr = [[ab[:, 4992 + p * 1024 + g * 256: 4992 + p * 1024 + (g + 1) * 256] for g in range(4)]
            for p in range(2)]
    stot = [[ab[:, 7168 + p * 128 + j * 64: 7168 + p * 128 + (j + 1) * 64] for j in range(2)]
            for p in range(2)]
    maskPh = [ab[:, 7424 + h * 256: 7424 + (h + 1) * 256] for h in range(2)]

    scl = sb("scl_s", [128, 137], F32)
    cst = sb("cst_s", [128, 8], F32)
    mcat2 = sb("mcat2_s", [128, 256], F32)
    ident = sb("ident_s", [128, 128], BF)
    onesv = sb("onesv_s", [128, 4], F32)
    onesb = sb("onesb_s", [128, 1], BF)
    onesr = sb("onesr_s", [2, 128], F32)
    hbc = sb("hbc_s", [2, 128], F32)
    muv = sb("muv_s", [128, 8], F32)
    nuv = sb("nuv_s", [128, 64], F32)

    pb = [es.enter_context(nc.psum_tensor(f"p{i}", [128, 512], F32)) for i in range(6)]
    pTs = [es.enter_context(nc.psum_tensor(f"pT{i}", [128, 1024], BF)) for i in range(2)]

    for s in ("sWD0", "sWD1", "sCN", "sXD0", "sXD1", "sVD0", "sVD1",
              "sCS", "sMM", "sSC", "sEV", "sCC"):
        ctx.sems[s] = es.enter_context(nc.semaphore(s))

    banks = Banks(ctx, pb[:4])

    def dma(eng, sem, out, in_, waits=()):
        return ctx.emit(eng, list(waits), lambda e, o=out, i=in_: e.dma_start(out=o, in_=i), [(sem, 16)],
                        is_dma=True)

    def mm_group(parts, bank_i, extra_waits=(), region=None):
        w = banks.mm_waits(bank_i) + list(extra_waits)
        n = len(parts)
        for idx, (lh, rh) in enumerate(parts):
            incs = [("sMM", 1)] if idx == n - 1 else []
            out = region if region is not None else pb[bank_i][:, :]
            ctx.emit("tensor", w if idx == 0 else [],
                     lambda e, o=out, l=lh, r=rh, st=(idx == 0), sp=(idx == n - 1):
                     e.matmul(o, l, r, start=st, stop=sp), incs)
        banks.set_ready(bank_i, "sMM", ctx.ctr["sMM"])
        return ctx.ctr["sMM"]

    for out, i in ((scl, dsc), (cst, dcst), (mcat2, dmcat), (maskPh[0], dmaskP0),
                   (maskPh[1], dmaskP1), (ident, dident), (onesv, dones), (onesb, donesb),
                   (onesr, donesr), (hbc, dhbc), (muv, dmu), (nuv, dnu)):
        dma("sync", "sCN", out[:, :], i[:, :])
    consts_wd = ctx.ctr["sCN"]

    for kc in range(8):
        dma("gpsimd", "sWD0", wq_b[:, kc * 1024:(kc + 1) * 1024], dwq[kc, :, :])
        dma("gpsimd", "sWD0", wo_b[:, kc * 1024:(kc + 1) * 1024], dwo[kc, :, :])
        dma("gpsimd", "sWD0", wk_b[:, kc * 256:(kc + 1) * 256], dwk[kc, :, :])
        dma("gpsimd", "sWD0", wv_b[:, kc * 256:(kc + 1) * 256], dwv[kc, :, :])
        dma("gpsimd", "sWD0", wg1_b[:, kc * 16:(kc + 1) * 16], dwg1[kc, :, :])
    dma("gpsimd", "sWD0", wg2a_b[0:17, :], dwg2a[:, :])
    attn_w_wd = ctx.ctr["sWD0"]

    ctx.emit("vector", [], lambda e: e.memset(g1aug[:, :], 1.0), [("sEV", 1)])
    g1_memset_ev = ctx.ctr["sEV"]

    xb = [ha[:, 8 * 1024 + q * 4096: 8 * 1024 + (q + 1) * 4096] for q in range(2)]
    sqs = [sqscr, af[:, 640:1152]]
    xb_free = [[], []]
    last_xd = [0, 0]
    xdmas = {}
    S1 = {}
    sq_mm_prev = [0, 0]
    prev_tt_mm = [0]

    def p1_dma(tt):
        q = tt % 2
        xds = []
        for kc in range(8):
            sl = kc % 2
            sem = "sXD%d" % sl
            xd = dma("gpsimd", sem, xb[q][:, kc * 512:(kc + 1) * 512],
                     dx[kc * 128:(kc + 1) * 128, tt * 512:(tt + 1) * 512],
                     waits=[(sem, last_xd[sl])] + (xb_free[q] if kc < 2 else []) +
                           ([("sCN", consts_wd)] if (tt == 0 and kc == 0) else []))[sem]
            last_xd[sl] = xd
            xds.append((sem, xd))
        xdmas[tt] = xds

    def p1_stats(tt):
        q = tt % 2
        bA = banks.take()
        bB = banks.take()
        wA = banks.mm_waits(bA)
        wB = banks.mm_waits(bB)
        first = True
        for kc in range(8):
            sl = kc % 2
            sem, xd = xdmas[tt][kc]
            ctx.emit("tensor", [(sem, xd)] + (wA if first else []),
                     lambda e, q=q, kc=kc, st=first, sp=(kc == 7), bA=bA:
                     e.matmul(pb[bA][0:1, :], onesb[:, 0:1], xb[q][:, kc * 512:(kc + 1) * 512],
                              start=st, stop=sp), [])
            sc_ = ctx.emit("scalar", [(sem, xd), ("sMM", sq_mm_prev[sl])] +
                           ([("sEV", S1[tt - 1]["p2ev"])] if (kc == 0 and tt >= 1) else []),
                           lambda e, q=q, kc=kc, sl=sl:
                           e.activation(sqs[sl][:, :], xb[q][:, kc * 512:(kc + 1) * 512], AF.Square),
                           [("sSC", 1)])["sSC"]
            mmres = ctx.emit("tensor", [("sSC", sc_)] + (wB if first else []),
                             lambda e, sl=sl, st=first, sp=(kc == 7), bB=bB:
                             e.matmul(pb[bB][0:1, :], onesv[:, 0:1], sqs[sl][:, :], start=st, stop=sp),
                             [("sMM", 1)])["sMM"]
            sq_mm_prev[sl] = mmres
            first = False
        S1[tt] = dict(bA=bA, bB=bB, mm_stats=ctx.ctr["sMM"])

    def p1_rows(tt):
        d = S1[tt]
        bA, bB = d["bA"], d["bB"]
        ctx.emit("scalar", [("sMM", d["mm_stats"])],
                 lambda e, bA=bA: e.mul(rowbuf[0:1, 0:512], pb[bA][0:1, :], 1.0 / 1024.0), [("sSC", 1)])
        sc_mu = ctx.ctr["sSC"]
        ctx.emit("scalar", [],
                 lambda e, bB=bB: e.mul(rowbuf[0:1, 512:1024], pb[bB][0:1, :], 1.0 / 1024.0), [("sSC", 1)])
        sc_ex2 = ctx.ctr["sSC"]
        ctx.emit("vector", [("sSC", sc_ex2), ("sMM", prev_tt_mm[0])],
                 lambda e: e.tensor_mul(rowbuf[0:1, 1024:1536], rowbuf[0:1, 0:512], rowbuf[0:1, 0:512]),
                 [("sEV", 1)])
        ctx.emit("vector", [],
                 lambda e: e.tensor_sub(rowbuf[0:1, 512:1024], rowbuf[0:1, 512:1024], rowbuf[0:1, 1024:1536]),
                 [("sEV", 1)])
        ev_var = ctx.ctr["sEV"]
        banks.add_free(bA, "sSC", sc_mu)
        banks.add_free(bB, "sSC", sc_ex2)
        ctx.emit("scalar", [("sEV", ev_var)],
                 lambda e: e.activation(rowbuf[0:1, 512:1024], rowbuf[0:1, 512:1024], AF.Ln,
                                        bias=cst[0:1, 0:1]), [("sSC", 1)])
        ctx.emit("scalar", [],
                 lambda e: e.activation(rowbuf[0:1, 1024:1536], rowbuf[0:1, 512:1024], AF.Exp, scale=-0.5),
                 [("sSC", 1)])
        d["sc_a"] = ctx.ctr["sSC"]
        ctx.emit("vector", [("sSC", d["sc_a"])],
                 lambda e: e.tensor_mul(rowbuf[0:1, 0:512], rowbuf[0:1, 0:512], rowbuf[0:1, 1024:1536]),
                 [("sEV", 1)])
        d["ev_b"] = ctx.ctr["sEV"]

    def p1_bcast(tt):
        d = S1[tt]
        bA2 = banks.take()
        d["mm_a"] = mm_group([(onesr[0:1, :], rowbuf[0:1, 1024:1536])], bA2, extra_waits=[("sSC", d["sc_a"])])
        bB2 = banks.take()
        d["mm_b"] = mm_group([(onesr[0:1, :], rowbuf[0:1, 0:512])], bB2, extra_waits=[("sEV", d["ev_b"])])
        d["bA2"], d["bB2"] = bA2, bB2
        prev_tt_mm[0] = d["mm_b"]

    def p1_pass2(tt):
        q = tt % 2
        d = S1[tt]
        bA2, bB2 = d["bA2"], d["bB2"]
        for kc in range(8):
            ctx.emit("vector", [("sMM", d["mm_a"])] if kc == 0 else [],
                     lambda e, q=q, kc=kc, bA2=bA2:
                     e.tensor_tensor(sqscr[:, :], xb[q][:, kc * 512:(kc + 1) * 512],
                                     pb[bA2][:, :], ALU.mult), [("sEV", 1)])
            ctx.emit("vector", [("sMM", d["mm_b"])] if kc == 0 else [],
                     lambda e, kc=kc, tt=tt, bB2=bB2:
                     e.tensor_tensor(xn[:, kc * seg + tt * 512: kc * seg + tt * 512 + 512],
                                     sqscr[:, :], pb[bB2][:, :], ALU.subtract), [("sEV", 1)])
        d["p2ev"] = ctx.ctr["sEV"]
        xb_free[q] = [("sEV", d["p2ev"])]
        banks.add_free(bA2, "sEV", ctx.ctr["sEV"])
        banks.add_free(bB2, "sEV", ctx.ctr["sEV"])

    def p1_proj(tt):
        pw = [("sEV", S1[tt]["p2ev"]), ("sWD0", attn_w_wd), ("sEV", g1_memset_ev)]
        for m in range(8):
            j, hp = divmod(m, 4)
            b = banks.take()
            mm_group([(wq_b[:, kc * 1024 + m * 128: kc * 1024 + (m + 1) * 128],
                       xn[:, kc * seg + tt * 512: kc * seg + (tt) * 512 + 512]) for kc in range(8)], b,
                     extra_waits=pw)
            sc_ = ctx.emit("scalar", banks.evac_wait(b),
                           lambda e, j=j, hp=hp, tt=tt, b=b:
                           e.activation(qg[j][:, hp * seg + tt * 512: hp * seg + tt * 512 + 512],
                                        pb[b][:, :], AF.Relu, scale=0.125), [("sSC", 1)])["sSC"]
            banks.add_free(b, "sSC", sc_)
        for m in range(2):
            b = banks.take()
            mm_group([(wk_b[:, kc * 256 + m * 128: kc * 256 + (m + 1) * 128],
                       xn[:, kc * seg + tt * 512: kc * seg + tt * 512 + 512]) for kc in range(8)], b,
                     extra_waits=pw)
            sc_ = ctx.emit("scalar", banks.evac_wait(b),
                           lambda e, m=m, tt=tt, b=b:
                           e.activation(kfm[:, m * seg + tt * 512: m * seg + tt * 512 + 512],
                                        pb[b][:, :], AF.Relu), [("sSC", 1)])["sSC"]
            banks.add_free(b, "sSC", sc_)
        for ti in range(4 * tt, 4 * tt + 4):
            b = banks.take()
            mm_group([(xn[:, kc * seg + ti * 128: kc * seg + (ti + 1) * 128],
                       wv_b[:, kc * 256:(kc + 1) * 256]) for kc in range(8)], b,
                     region=pb[b][:, 0:256], extra_waits=pw)
            ev = ctx.emit("vector", banks.evac_wait(b),
                          lambda e, ti=ti, b=b: e.tensor_copy(vtm[:, ti * 256:(ti + 1) * 256], pb[b][:, 0:256]),
                          [("sEV", 1)])["sEV"]
            banks.add_free(b, "sEV", ev)
        b = banks.take()
        mm_group([(wg1_b[:, kc * 16:(kc + 1) * 16],
                   xn[:, kc * seg + tt * 512: kc * seg + tt * 512 + 512]) for kc in range(8)], b,
                 region=pb[b][0:16, :], extra_waits=pw)
        ev = ctx.emit("vector", banks.evac_wait(b),
                      lambda e, tt=tt, b=b: e.tensor_copy(g1aug[0:16, tt * 512:(tt + 1) * 512], pb[b][0:16, :]),
                      [("sEV", 1)])["sEV"]
        banks.add_free(b, "sEV", ev)

    p1_dma(0)
    for i in range(-1, NT):
        if i + 1 < NT:
            p1_stats(i + 1)
        if i + 2 < NT:
            p1_dma(i + 2)
        if i >= 0:
            p1_proj(i)
        if i + 1 < NT:
            p1_rows(i + 1)
            p1_bcast(i + 1)
            p1_pass2(i + 1)
    if DEBUG:
        ctx.guard("scalar", ctx.barrier_waits())
        dma("scalar", "sVD0", ddbg["xn"][:, :], xn[:, :])
    ln_done = ctx.barrier_waits()

    proj_done = ctx.barrier_waits()


    for e in ("tensor", "vector", "scalar", "gpsimd"):
        ctx.guard(e, proj_done)

    for j in range(2):
        ctx.emit("vector", [], lambda e, j=j: e.memset(sf32[j][:, :], 0.0), [("sEV", 1)])
    for p in range(2):
        for j in range(2):
            ctx.emit("vector", [], lambda e, p=p, j=j: e.memset(ks[p][j][:, :], 0.0), [("sEV", 1)])
    ctx.emit("scalar", [], lambda e: e.memzero(pref[:, :]), [("sSC", 1)])
    init_ev = ctx.ctr["sEV"]
    init_sc = ctx.ctr["sSC"]
    ctx.guard("tensor", [("sEV", init_ev)])
    ctx.guard("scalar", [("sEV", init_ev)])

    gslot_free = [[] for _ in range(4)]
    gslot_ready = [None] * 4
    glog_emitted = 0
    prev_gate_ev = 0

    def emit_gate_ttile(t):
        nonlocal glog_emitted, prev_gate_ev
        sl = t % 4
        b = banks.take()
        mm_group([(g1aug[0:17, t * 128:(t + 1) * 128], wg2a_b[0:17, :])], b,
                 region=pb[b][:, 0:256])
        w0 = banks.evac_wait(b) + gslot_free[sl] + [("sEV", prev_gate_ev)]
        ctx.emit("scalar", w0, lambda e, b=b: e.activation(gscr[0][:, :], pb[b][:, 0:256], AF.Abs),
                 [("sSC", 1)])
        ctx.emit("scalar", [], lambda e: e.activation(gscr[1][:, :], gscr[0][:, :], AF.Exp, scale=-1.0),
                 [("sSC", 1)])
        ctx.emit("scalar", [], lambda e: e.activation(gscr[0][:, :], gscr[1][:, :], AF.Ln, bias=cst[:, 2:3]),
                 [("sSC", 1)])
        sc1 = ctx.ctr["sSC"]
        ev = ctx.emit("vector", [("sSC", sc1)] + banks.evac_wait(b) + gslot_free[sl],
                      lambda e, b=b: e.tensor_scalar_min(gscr[1][:, :], pb[b][:, 0:256], 0.0),
                      [("sEV", 1)])["sEV"]
        banks.add_free(b, "sEV", ev)
        ev2 = ctx.emit("vector", [],
                       lambda e, sl=sl: e.tensor_sub(gtm[sl][:, :], gscr[1][:, :], gscr[0][:, :]),
                       [("sEV", 1)])["sEV"]
        gslot_ready[sl] = ("sEV", ev2)
        prev_gate_ev = ev2
        glog_emitted = t + 1

    for t in range(min(3, NTT)):
        emit_gate_ttile(t)

    factor_ev = {}
    factor_sc = {}
    q_cs = {}
    mm_after_o = {}
    pT_free = [[], []]
    pp = 0
    pgbs = [pb[4], pb[5]]
    pg_free = [[], []]
    pg_ready = [None, None]

    def apf(view, extra_off, dims):
        base = view[:, :] if hasattr(view, "tensor") is False else view
        return bass.AP(tensor=base.tensor, offset=base.offset + extra_off,
                       ap=[list(base.ap[0])] + [list(d) for d in dims])

    hbv = hb[:, :]
    hav = ha[:, :]
    abv = ab[:, :]
    afv = af[:, :]
    QG0 = 8 * 1024
    KFM0 = 8 * 1024 + 2 * SEGQ
    def emit_gate_mm(t):
        nonlocal glog_emitted
        while glog_emitted < min(t + 3, NTT):
            emit_gate_ttile(glog_emitted)
        tb = t % 2
        sl2 = t % 4
        pgb = pgbs[tb]
        for j in range(2):
            w = (pg_free[tb] if j == 0 else []) + [gslot_ready[sl2]]
            pg_free[tb] = []
            ctx.emit("tensor", w,
                     lambda e, j=j, sl2=sl2, pgb=pgb:
                     e.matmul(pgb[:, j * 256:(j + 1) * 256],
                              gtm[sl2][:, j * 128:(j + 1) * 128],
                              mcat2[:, :], start=True, stop=True), [("sMM", 1)])
        pg_ready_t[t] = ("sMM", ctx.ctr["sMM"])
        gslot_free[sl2] = [("sMM", ctx.ctr["sMM"])]

    def emit_pre_sv(c):
        nonlocal pp
        p = c % 2
        t, half = divmod(c, 2)
        tb = t % 2
        pgb = pgbs[tb]
        co = half * 128
        EX0 = 3584 + p * 384
        w_war = [pg_ready_t[t], ("sEV", factor_ev.get(c - 2, 0))]
        if c == 0:
            w_war = w_war + [("sSC", init_sc)]
        pgv = pgb[:, :]
        ctx.emit("scalar", w_war,
                 lambda e, co=co, EX0=EX0, pgv=pgv:
                 e.activation(afv[:, EX0: EX0 + 128], apf(pgv, co, [[256, 2], [1, 64]]),
                              AF.Exp, scale=1.0 / GLN), [("sSC", 1)])
        ctx.emit("scalar", [],
                 lambda e, c=c, EX0=EX0:
                 e.copy(decays[:, c * 2: c * 2 + 2],
                        apf(afv, EX0 + 63, [[64, 2], [1, 1]])), [("sSC", 1)])
        ctx.emit("scalar", [],
                 lambda e, co=co, EX0=EX0, pgv=pgv:
                 e.activation(afv[:, EX0 + 128: EX0 + 256], apf(pgv, co, [[256, 2], [1, 64]]),
                              AF.Exp, scale=-1.0 / GLN), [("sSC", 1)])
        ctx.emit("scalar", [],
                 lambda e, co=co, EX0=EX0, pgv=pgv:
                 e.activation(afv[:, EX0 + 256: EX0 + 384], apf(pgv, co + 64, [[256, 2], [1, 64]]),
                              AF.Exp, scale=1.0 / GLN), [("sSC", 1)])
        ctx.emit("scalar", [],
                 lambda e, c=c, pq=pp:
                 e.activation(eprefs[:, c * 2: c * 2 + 2], apf(afv, 4352 + pq, [[2, 2], [1, 1]]),
                              AF.Exp, scale=1.0 / GLN), [("sSC", 1)])
        for j in range(2):
            ctx.emit("scalar", [], lambda e, j=j, co=co, pq=pp, pgb=pgb:
                     e.add(pref[:, j * 2 + (1 - pq): j * 2 + (1 - pq) + 1],
                           pgb[:, j * 256 + co + 63: j * 256 + co + 64],
                           pref[:, j * 2 + pq: j * 2 + pq + 1]), [("sSC", 1)])
        ctx.emit("scalar", [], lambda e, c=c, EX0=EX0:
                 e.copy(apf(hbv, EGH0 + c * 64, [[seg, 2], [1, 64]]),
                        afv[:, EX0: EX0 + 128]), [("sSC", 1)])
        sc_all = ctx.ctr["sSC"]
        sc_alls[c] = sc_all
        if half == 1:
            pg_free[tb] = [("sSC", sc_all)]
        pp = 1 - pp
        for j in range(2):
            ctx.emit("vector", [("sSC", sc_all), ("sMM", mm_after_P.get(c - 2, 0))] if j == 0 else [],
                     lambda e, j=j, c=c, p=p, EX0=EX0:
                     e.tensor_tensor(apf(abv, 2432 + p * 512 + j * 256, [[64, 4], [1, 64]]),
                                     apf(hbv, QG0 + j * SEGQ + c * 64, [[seg, 4], [1, 64]]),
                                     apf(afv, EX0 + j * 64, [[0, 4], [1, 64]]),
                                     ALU.mult), [("sEV", 1)])
        ctx.emit("vector", [],
                 lambda e, c=c, half=half, p=p, EX0=EX0:
                 e.tensor_tensor(apf(abv, 3456 + p * 256 + half * 64, [[128, 2], [1, 64]]),
                                 apf(hav, KFM0 + c * 64, [[seg, 2], [1, 64]]),
                                 apf(afv, EX0 + 128, [[64, 2], [1, 64]]), ALU.mult),
                 [("sEV", 1)])
        ctx.emit("vector", [], lambda e, half=half, p=p:
                 e.memset(apf(abv, 3968 + p * 256 + (1 - half) * 64, [[128, 2], [1, 64]]), 0.0),
                 [("sEV", 1)])
        ctx.emit("vector", [], lambda e, c=c, half=half, p=p, EX0=EX0:
                 e.tensor_tensor(apf(abv, 3968 + p * 256 + half * 64, [[128, 2], [1, 64]]),
                                 apf(hav, KFM0 + c * 64, [[seg, 2], [1, 64]]),
                                 apf(afv, EX0 + 256, [[64, 2], [1, 64]]), ALU.mult),
                 [("sEV", 1)])
        factor_ev[c] = ctx.ctr["sEV"]

    def emit_pre_t(c):
        p = c % 2
        t, half = divmod(c, 2)
        if half == 0 and t + 1 < NTT:
            emit_gate_mm(t + 1)
        for j in range(2):
            ctx.emit("tensor", ([("sEV", factor_ev[c])] if j == 0 else []) + pT_free[j],
                     lambda e, j=j, p=p: e.transpose(pTs[j][:, 0:128], kppf[p][j][:, :], ident[:, :]),
                     [("sMM", 1)])
            trans_mm[c * 2 + j] = ctx.ctr["sMM"]
        pbanks = []
        for g in range(4):
            j, gh = divmod(g, 2)
            b = banks.take()
            w = banks.mm_waits(b)
            ctx.emit("tensor", w,
                     lambda e, j=j, gh=gh, b=b, p=p:
                     e.matmul(pb[b][:, 0:256], ks[p][j][gh * 64:(gh + 1) * 64, :],
                              qs[p][j][gh * 64:(gh + 1) * 64, :], start=True, stop=True),
                     [("sMM", 1)])
            banks.set_ready(b, "sMM", ctx.ctr["sMM"])
            pbanks.append(b)
        P_banks[c] = pbanks
        mm_after_P[c] = ctx.ctr["sMM"]

    def emit_kppt_evac(c):
        p = c % 2
        for j in range(2):
            sc = ctx.emit("scalar", [("sMM", trans_mm[c * 2 + j])],
                          lambda e, j=j, p=p: e.copy(kppt[p][j][:, :], pTs[j][:, 0:128]),
                          [("sSC", 1)])["sSC"]
            pT_free[j] = [("sSC", sc)]
        kppt_ev[c] = ctx.ctr["sSC"]

    def emit_mask(c):
        p = c % 2
        t, half = divmod(c, 2)
        for g in range(4):
            j, gh = divmod(g, 2)
            b = P_banks[c][g]
            w = banks.evac_wait(b) + ([("sMM", mm_after_o.get(c - 2, 0))] if g == 0 else [])
            ev = ctx.emit("vector", w,
                          lambda e, j=j, gh=gh, b=b, half=half, p=p:
                          e.tensor_tensor(abv[:, 4992 + p * 1024 + j * 512 + gh * 256:
                                              4992 + p * 1024 + j * 512 + (gh + 1) * 256],
                                          pb[b][:, 0:256],
                                          maskPh[half][:, :], ALU.mult), [("sEV", 1)])["sEV"]
            banks.add_free(b, "sEV", ev)
        mask_ev[c] = ctx.ctr["sEV"]

    def emit_tail_t(c):
        p = c % 2
        t, half = divmod(c, 2)
        obanks = []
        for j in range(2):
            b = banks.take()
            w = banks.mm_waits(b) + ([("sEV", mask_ev[c])] if j == 0 else [])
            for gh in range(2):
                g = 2 * j + gh
                vsl = vtm[:, t * 256 + g * 64: t * 256 + (g + 1) * 64]
                tp1 = (0, gh * 64) if gh else None
                ctx.emit("tensor", w if gh == 0 else [],
                         lambda e, b=b, vsl=vsl, j=j, gh=gh, tp1=tp1, p=p:
                         e.matmul(pb[b][gh * 64:(gh + 1) * 64, 0:256], vsl,
                                  abv[:, 4992 + p * 1024 + j * 512 + gh * 256:
                                       4992 + p * 1024 + j * 512 + (gh + 1) * 256],
                                  start=True, stop=True, tile_position=tp1),
                         [("sMM", 1)] if gh == 1 else [])
            banks.set_ready(b, "sMM", ctx.ctr["sMM"])
            obanks.append(b)
        sbanks = []
        for j in range(2):
            b = banks.take()
            w = banks.mm_waits(b) + ([("sSC", kppt_ev[c])] if j == 0 else [])
            for gh in range(2):
                g = 2 * j + gh
                vsl = vtm[:, t * 256 + g * 64: t * 256 + (g + 1) * 64]
                tp = (0, gh * 64) if gh else None
                ctx.emit("tensor", w if gh == 0 else [],
                         lambda e, b=b, j=j, gh=gh, vsl=vsl, tp=tp, p=p:
                         e.matmul(pb[b][gh * 64:(gh + 1) * 64, 0:64],
                                  kppt[p][j][:, gh * 64:(gh + 1) * 64],
                                  vsl, start=True, stop=True, tile_position=tp),
                         [("sMM", 1)] if gh == 1 else [])
            banks.set_ready(b, "sMM", ctx.ctr["sMM"])
            sbanks.append(b)
        mm_after_o[c] = ctx.ctr["sMM"]
        tail_banks[c] = (obanks, sbanks)

    def emit_tail_v(c):
        obanks, sbanks = tail_banks[c]
        for j in range(2):
            b = obanks[j]
            dst = og[j].rearrange("p (hp t) -> p hp t", hp=4)[:, :, c * 64:(c + 1) * 64]
            oeng, osem = (("scalar", "sSC") if j == 0 else ("vector", "sEV"))
            ov = ctx.emit(oeng, banks.evac_wait(b),
                          (lambda e, b=b, dst=dst:
                           e.copy(dst, pb[b][:, 0:256].rearrange("p (hp t) -> p hp t", hp=4)))
                          if j == 0 else
                          (lambda e, b=b, dst=dst:
                           e.tensor_copy(dst, pb[b][:, 0:256].rearrange("p (hp t) -> p hp t", hp=4))),
                          [(osem, 1)])[osem]
            banks.add_free(b, osem, ov)
        for j in range(2):
            b = sbanks[j]
            ev = ctx.emit("vector", banks.evac_wait(b) +
                          ([("sSC", sbf_sc.get(c - 1, 0))] if j == 0 else []),
                          lambda e, b=b, j=j, c=c:
                          e.scalar_tensor_tensor(sf32[j][:, :], sf32[j][:, :],
                                                 decays[:, c * 2 + j: c * 2 + j + 1],
                                                 pb[b][:, 0:64],
                                                 ALU.mult, ALU.add), [("sEV", 1)])["sEV"]
            banks.add_free(b, "sEV", ev)
        stt_ev[c] = ctx.ctr["sEV"]

    def emit_sbf_copy(c):
        ctx.emit("scalar", [("sEV", stt_ev[c])], lambda e, c=c:
                 e.copy(sbf_all[:, c * 128: (c + 1) * 128],
                        afv[:, 4360: 4360 + 128]), [("sSC", 1)])
        sbf_sc[c] = ctx.ctr["sSC"]

    sc_alls = {}
    sbf_sc = {}
    mm_after_P = {}
    kppt_ev = {}
    mask_ev = {}
    P_banks = {}
    pg_ready_t = {}
    tail_banks = {}
    stt_ev = {}
    q_cs = {}
    trans_mm = {}
    emit_gate_mm(0)
    for c in range(NCH):
        emit_pre_sv(c)
        if c >= 1:
            emit_kppt_evac(c - 1)
        if c >= 2:
            emit_sbf_copy(c - 2)
        if c >= 1:
            emit_tail_t(c - 1)
            emit_tail_v(c - 1)
        emit_pre_t(c)
        emit_mask(c)
    emit_kppt_evac(NCH - 1)
    emit_sbf_copy(NCH - 2)
    emit_tail_t(NCH - 1)
    emit_tail_v(NCH - 1)
    emit_sbf_copy(NCH - 1)

    chunk_done = ctx.barrier_waits()
    for e in ("vector", "gpsimd", "scalar"):
        ctx.guard(e, chunk_done)
    ctx.emit("vector", [], lambda e: e.memset(blob[:, :], 0.0), [("sEV", 1)])
    for j in range(2):
        ctx.emit("vector", [], lambda e, j=j: e.tensor_copy(blob[:, j * 64:(j + 1) * 64], sf32[j][:, :]),
                 [("sEV", 1)])
        ctx.emit("vector", [], lambda e, j=j, p=pp:
                 e.tensor_copy(blob[:, 128 + j:129 + j], pref[:, j * 2 + p: j * 2 + p + 1]), [("sEV", 1)])
    ev_blob = ctx.ctr["sEV"]
    xd = dma("gpsimd", "sXD0", dccin[:, :], blob[:, :], waits=[("sEV", ev_blob)])["sXD0"]
    cc = ctx.emit("gpsimd", [("sXD0", xd)],
                  lambda e: e.collective_compute("AllGather", ALU.bypass,
                                                 replica_groups=[list(range(NCORE))],
                                                 ins=[dccin[:, :]], outs=[dccout[:, :]]),
                  [("sCC", 1)], is_dma=True)["sCC"]
    xd = dma("gpsimd", "sXD0", gath[:, :].rearrange("p (r c) -> p r c", r=8),
             dccout.rearrange("(r p) c -> p r c", p=128), waits=[("sCC", cc)])["sXD0"]
    gath_xd = xd
    gath3 = gath[:, :].rearrange("p (r c) -> p r c", r=8)
    sumg_all = [gath3[:, :, 128 + j] for j in range(2)]
    for j in range(2):
        ctx.emit("vector", [("sXD0", gath_xd)],
                 lambda e, j=j: e.memset(sin[j * 2][:, :], 0.0), [("sEV", 1)])
    par = [0, 0]
    for i in range(NCORE):
        ctx.emit("vector", [("sXD0", gath_xd)],
                 lambda e, i=i: e.tensor_tensor(w8[:, 0:8], sumg_all[0],
                                                nuv[:, i * 8:(i + 1) * 8], ALU.mult), [("sEV", 1)])
        ctx.guard("vector", [("sEV", ctx.ctr["sEV"])])
        ctx.emit("vector", [],
                 lambda e: e.tensor_reduce(w8[:, 8:9], w8[:, 0:8], AX.X, ALU.add), [("sEV", 1)])
        ctx.guard("vector", [("sEV", ctx.ctr["sEV"])])
        ctx.emit("vector", [],
                 lambda e, i=i: e.tensor_tensor(w8[:, 0:8], sumg_all[1],
                                                nuv[:, i * 8:(i + 1) * 8], ALU.mult), [("sEV", 1)])
        ctx.guard("vector", [("sEV", ctx.ctr["sEV"])])
        ctx.emit("vector", [],
                 lambda e: e.tensor_reduce(w8[:, 9:10], w8[:, 0:8], AX.X, ALU.add), [("sEV", 1)])
        ctx.guard("vector", [("sEV", ctx.ctr["sEV"])])
        ev_z = ctx.ctr["sEV"]
        ctx.emit("scalar", [("sEV", ev_z)],
                 lambda e: e.activation(w8[:, 10:12], w8[:, 8:10], AF.Exp, scale=1.0 / GLN), [("sSC", 1)])
        sc_w = ctx.ctr["sSC"]
        ev_w = ctx.emit("vector", [("sSC", sc_w)],
                        lambda e, i=i: e.tensor_scalar_mul(w8[:, 12:14], w8[:, 10:12], muv[:, i:i + 1]),
                        [("sEV", 1)])["sEV"]
        ctx.guard("vector", [("sEV", ev_w)])
        for j in range(2):
            p = par[j]
            ctx.emit("vector", [],
                     lambda e, i=i, j=j, p=p:
                     e.scalar_tensor_tensor(sin[j * 2 + (1 - p)][:, :],
                                            gath[:, i * 132 + j * 64: i * 132 + (j + 1) * 64],
                                            w8[:, 12 + j:13 + j],
                                            sin[j * 2 + p][:, :], ALU.mult, ALU.add), [("sEV", 1)])
            ctx.guard("vector", [("sEV", ctx.ctr["sEV"])])
            par[j] = 1 - p
    sin_fin = [sin[0 + par[0]], sin[2 + par[1]]]
    if DEBUG:
        ctx.guard("scalar", ctx.barrier_waits())
        dma("scalar", "sVD0", ddbg["gath"][:, :], gath[:, :])
        dma("scalar", "sVD0", ddbg["w8"][:, :], w8[:, :])
        dma("scalar", "sVD0", ddbg["muv"][:, :], muv[:, :])

    comb_done = ctx.barrier_waits()

    for e in ("tensor", "vector", "gpsimd"):
        ctx.guard(e, comb_done)
    corr_mm_prev = 0
    for c in range(NCH):
        p = c % 2
        ctx.emit("vector", [("sMM", corr_mm_prev)],
                 lambda e, c=c, p=p:
                 e.tensor_tensor(apf(abv, 2432 + p * 512, [[256, 2], [64, 4], [1, 64]]),
                                 apf(hbv, QG0 + c * 64, [[SEGQ, 2], [seg, 4], [1, 64]]),
                                 apf(hbv, EGH0 + c * 64, [[seg, 2], [0, 4], [1, 64]]),
                                 ALU.mult), [("sEV", 1)])
        ev_q3 = ctx.ctr["sEV"]
        for j in range(2):
            if c == 0:
                ctx.emit("vector", [("sMM", corr_mm_prev)] if j == 0 else [],
                         lambda e, j=j: e.tensor_scalar_mul(stot[0][j][:, :], sin_fin[j][:, :],
                                                            eprefs[:, j: j + 1]), [("sEV", 1)])
            else:
                ctx.emit("vector", [("sMM", corr_mm_prev)] if j == 0 else [],
                         lambda e, j=j, c=c, p=p:
                         e.scalar_tensor_tensor(stot[p][j][:, :], sin_fin[j][:, :],
                                                eprefs[:, c * 2 + j: c * 2 + j + 1],
                                                sbf_all[:, (c - 1) * 128 + j * 64: (c - 1) * 128 + (j + 1) * 64],
                                                ALU.mult, ALU.add), [("sEV", 1)])
        ev_st = ctx.ctr["sEV"]
        for j in range(2):
            b = banks.take()
            w = banks.mm_waits(b) + ([("sEV", ev_st)] if j == 0 else [])
            for gh in range(2):
                tpc = (gh * 64, gh * 64) if gh else None
                ctx.emit("tensor", w if gh == 0 else [],
                         lambda e, b=b, j=j, gh=gh, tpc=tpc, p=p:
                         e.matmul(pb[b][gh * 64:(gh + 1) * 64, 0:256],
                                  stot[p][j][gh * 64:(gh + 1) * 64, :],
                                  qs[p][j][gh * 64:(gh + 1) * 64, :], start=True, stop=True,
                                  tile_position=tpc), [("sMM", 1)] if gh == 1 else [])
            banks.set_ready(b, "sMM", ctx.ctr["sMM"])
            corr_mm_prev = ctx.ctr["sMM"]
            dst = og[j].rearrange("p (hp t) -> p hp t", hp=4)[:, :, c * 64:(c + 1) * 64]
            ev = ctx.emit("vector", banks.evac_wait(b),
                          lambda e, b=b, dst=dst:
                          e.tensor_tensor(dst, dst, pb[b][:, 0:256].rearrange("p (hp t) -> p hp t", hp=4),
                                          ALU.add), [("sEV", 1)])["sEV"]
            banks.add_free(b, "sEV", ev)
    corr_done = ctx.barrier_waits()

    for e in ("tensor", "vector", "scalar"):
        ctx.guard(e, corr_done)
    prev_var_mm = 0
    prev_bc_mm = 0
    for j in range(2):
        for ti in range(4 * NT):
            sc_sq = ctx.emit("scalar", [("sMM", prev_var_mm)],
                             lambda e, j=j, ti=ti: e.activation(sqo[:, :], og[j][:, ti * 512:(ti + 1) * 512],
                                                                AF.Square), [("sSC", 1)])["sSC"]
            b = banks.take()
            w = banks.mm_waits(b) + [("sSC", sc_sq)]
            ctx.emit("tensor", w,
                     lambda e, b=b: e.matmul(pb[b][0:2, :], onesv[:, 1:3], sqo[:, :],
                                             start=True, stop=True), [("sMM", 1)])
            banks.set_ready(b, "sMM", ctx.ctr["sMM"])
            prev_var_mm = ctx.ctr["sMM"]
            ctx.emit("scalar", banks.evac_wait(b),
                     lambda e, b=b: e.activation(orow[0:2, 0:512], pb[b][0:2, :], AF.Ln,
                                                 scale=1.0 / 64.0, bias=cst[0:2, 1:2]), [("sSC", 1)])
            sc_ln = ctx.ctr["sSC"]
            banks.add_free(b, "sSC", sc_ln)
            ctx.emit("scalar", [("sMM", prev_bc_mm)],
                     lambda e: e.activation(orow[0:2, 512:1024], orow[0:2, 0:512], AF.Exp, scale=-0.5),
                     [("sSC", 1)])
            sc_r = ctx.ctr["sSC"]
            b2 = banks.take()
            w2 = banks.mm_waits(b2) + [("sSC", sc_r)]
            ctx.emit("tensor", w2,
                     lambda e, b2=b2: e.matmul(pb[b2][:, :], hbc[:, :], orow[0:2, 512:1024],
                                               start=True, stop=True), [("sMM", 1)])
            banks.set_ready(b2, "sMM", ctx.ctr["sMM"])
            prev_bc_mm = ctx.ctr["sMM"]
            ev = ctx.emit("vector", banks.evac_wait(b2),
                          lambda e, j=j, ti=ti, b2=b2:
                          e.scalar_tensor_tensor(og[j][:, ti * 512:(ti + 1) * 512],
                                                 og[j][:, ti * 512:(ti + 1) * 512],
                                                 scl[:, 136:137], pb[b2][:, :],
                                                 ALU.mult, ALU.mult), [("sEV", 1)])["sEV"]
            banks.add_free(b2, "sEV", ev)
    onorm_done = ctx.barrier_waits()

    for e in ("tensor", "vector", "gpsimd"):
        ctx.guard(e, onorm_done)
    xw_free = [[], []]
    yw_free = [[], []]
    for mb in range(8):
        for tt in range(NT):
            sl = (mb * NT + tt) % 2
            sem = "sXD%d" % sl
            xd = dma("gpsimd", sem, xwo[sl][:, :], dx[mb * 128:(mb + 1) * 128, tt * 512:(tt + 1) * 512],
                     waits=xw_free[sl])[sem]
            b = banks.take()
            parts = []
            for kc in range(8):
                j, hp = divmod(kc, 4)
                parts.append((wo_b[:, kc * 1024 + mb * 128: kc * 1024 + (mb + 1) * 128],
                              og[j][:, hp * seg + tt * 512: hp * seg + tt * 512 + 512]))
            mm_group(parts, b)
            ev = ctx.emit("vector", banks.evac_wait(b) + [(sem, xd)] + yw_free[sl],
                          lambda e, sl=sl, b=b: e.tensor_tensor(ysb[sl][:, :], xwo[sl][:, :], pb[b][:, :],
                                                                ALU.add), [("sEV", 1)])["sEV"]
            banks.add_free(b, "sEV", ev)
            xw_free[sl] = [("sEV", ev)]
            vsem = "sVD%d" % sl
            st = ctx.emit("scalar", [("sEV", ev)],
                          lambda e, sl=sl, mb=mb, tt=tt:
                          e.dma_start(out=dresid[mb * 128:(mb + 1) * 128, tt * 512:(tt + 1) * 512],
                                      in_=ysb[sl][:, :]), [(vsem, 16)], is_dma=True)[vsem]
            yw_free[sl] = [(vsem, st)]
    if DEBUG:
        ctx.guard("scalar", ctx.barrier_waits())
        dma("scalar", "sVD0", ddbg["resid"][:, :], dresid[:, :])
    wo_done = ctx.barrier_waits()
    resid_all_st = [("sVD0", ctx.ctr["sVD0"]), ("sVD1", ctx.ctr["sVD1"])]

    for e in ENGINES:
        ctx.guard(e, wo_done)

    wbf_free = [[], []]
    wbf_ready = [None, None]
    rs_free = [[], []]
    ysb_free = [list(resid_all_st), list(resid_all_st)]
    silu_free = [[], []]
    n_y = 0

    def stream_weight(dram, mb, nquart, slot):
        sem = "sWD%d" % slot
        wd = dma("gpsimd", sem, wbf[slot][:, 0:dram[mb].shape[1]], dram[mb],
                 waits=wbf_free[slot])[sem]
        wbf_free[slot] = []
        wbf_ready[slot] = (sem, wd)

    def stream_weight_pair(d1, d2, mb, slot):
        sem = "sWD%d" % slot
        dma("gpsimd", sem, wbf[slot][:, 0:1024], d1[mb], waits=wbf_free[slot])
        wd = dma("gpsimd", sem, wbf[slot][:, 1024:2048], d2[mb])[sem]
        wbf_free[slot] = []
        wbf_ready[slot] = (sem, wd)

    for p in range(2):
        tok0 = p * TP
        if p == 1:
            pass2 = [("sMM", ctx.ctr["sMM"])]
            ctx.guard("vector", pass2)
            ctx.guard("scalar", pass2)
        stA_ev = []
        stream_weight_pair(dw1, dw2, 0, 0)
        for mb in range(32):
            slot = mb % 2
            if mb + 1 < 32:
                stream_weight_pair(dw1, dw2, mb + 1, 1 - slot)
            last_mm_for_slot = None
            for t2 in range(NT2):
                bA = banks.take()
                wA = banks.mm_waits(bA) + ([wbf_ready[slot]] if t2 == 0 else [])
                for kc in range(8):
                    ctx.emit("tensor", wA if kc == 0 else [],
                             lambda e, bA=bA, kc=kc, t2=t2, slot=slot, tok0=tok0, st=(kc == 0):
                             e.matmul(pb[bA][:, :], wbf[slot][:, kc * 128:(kc + 1) * 128],
                                      xn[:, kc * seg + tok0 + t2 * 512: kc * seg + tok0 + t2 * 512 + 512],
                                      start=st, stop=(kc == 7)), [("sMM", 1)] if kc == 7 else [])
                banks.set_ready(bA, "sMM", ctx.ctr["sMM"])
                bB = banks.take()
                wB = banks.mm_waits(bB)
                for kc in range(8):
                    ctx.emit("tensor", wB if kc == 0 else [],
                             lambda e, bB=bB, kc=kc, t2=t2, slot=slot, tok0=tok0, st=(kc == 0):
                             e.matmul(pb[bB][:, :], wbf[slot][:, 1024 + kc * 128: 1024 + (kc + 1) * 128],
                                      xn[:, kc * seg + tok0 + t2 * 512: kc * seg + tok0 + t2 * 512 + 512],
                                      start=st, stop=(kc == 7)), [("sMM", 1)] if kc == 7 else [])
                banks.set_ready(bB, "sMM", ctx.ctr["sMM"])
                last_mm_for_slot = ctx.ctr["sMM"]
                ssl = (mb * NT2 + t2) % 2
                sc_si = ctx.emit("scalar", banks.evac_wait(bA) + silu_free[ssl],
                                 lambda e, bA=bA, mb=mb, ssl=ssl:
                                 e.activation(silu_s[ssl][:, :], pb[bA][:, :], AF.Silu,
                                              scale=scl[:, mb:mb + 1]), [("sSC", 1)])["sSC"]
                banks.add_free(bA, "sSC", sc_si)
                ev = ctx.emit("vector", banks.evac_wait(bB) + [("sSC", sc_si)],
                              lambda e, bB=bB, mb=mb, t2=t2, ssl=ssl:
                              e.scalar_tensor_tensor(ha[:, mb * TP + t2 * 512: mb * TP + t2 * 512 + 512],
                                                     pb[bB][:, :], scl[:, 32 + mb: 33 + mb],
                                                     silu_s[ssl][:, :], ALU.mult, ALU.mult),
                              [("sEV", 1)])["sEV"]
                banks.add_free(bB, "sEV", ev)
                silu_free[ssl] = [("sEV", ev)]
                stA_ev.append(ev)
            wbf_free[slot] = [("sMM", last_mm_for_slot)]
        evA = ctx.ctr["sEV"]
        stream_weight(dh1, 0, 4, 0)
        for mb in range(32):
            slot = mb % 2
            if mb + 1 < 32:
                stream_weight(dh1, mb + 1, 4, 1 - slot)
            for t2 in range(NT2):
                b = banks.take()
                w = banks.mm_waits(b) + ([wbf_ready[slot], ("sEV", evA)] if t2 == 0 else [])
                for kc in range(32):
                    ctx.emit("tensor", w if kc == 0 else [],
                             lambda e, b=b, kc=kc, t2=t2, slot=slot, st=(kc == 0):
                             e.matmul(pb[b][:, :], wbf[slot][:, kc * 128:(kc + 1) * 128],
                                      ha[:, kc * TP + t2 * 512: kc * TP + t2 * 512 + 512],
                                      start=st, stop=(kc == 31)), [("sMM", 1)] if kc == 31 else [])
                banks.set_ready(b, "sMM", ctx.ctr["sMM"])
                last_mm_for_slot = ctx.ctr["sMM"]
                sc_ = ctx.emit("scalar", banks.evac_wait(b),
                               lambda e, b=b, mb=mb, t2=t2:
                               e.activation(hb[:, mb * TP + t2 * 512: mb * TP + t2 * 512 + 512],
                                            pb[b][:, :], AF.Gelu, scale=scl[:, 64 + mb: 65 + mb]),
                               [("sSC", 1)])["sSC"]
                banks.add_free(b, "sSC", sc_)
            wbf_free[slot] = [("sMM", last_mm_for_slot)]
        scB = ctx.ctr["sSC"]
        stream_weight(dh2, 0, 4, 0)
        for mb in range(32):
            slot = mb % 2
            if mb + 1 < 32:
                stream_weight(dh2, mb + 1, 4, 1 - slot)
            for t2 in range(NT2):
                b = banks.take()
                w = banks.mm_waits(b) + ([wbf_ready[slot], ("sSC", scB)] if t2 == 0 else [])
                for kc in range(32):
                    ctx.emit("tensor", w if kc == 0 else [],
                             lambda e, b=b, kc=kc, t2=t2, slot=slot, st=(kc == 0):
                             e.matmul(pb[b][:, :], wbf[slot][:, kc * 128:(kc + 1) * 128],
                                      hb[:, kc * TP + t2 * 512: kc * TP + t2 * 512 + 512],
                                      start=st, stop=(kc == 31)), [("sMM", 1)] if kc == 31 else [])
                banks.set_ready(b, "sMM", ctx.ctr["sMM"])
                last_mm_for_slot = ctx.ctr["sMM"]
                sc_ = ctx.emit("scalar", banks.evac_wait(b),
                               lambda e, b=b, mb=mb, t2=t2:
                               e.activation(ha[:, mb * TP + t2 * 512: mb * TP + t2 * 512 + 512],
                                            pb[b][:, :], AF.Gelu, scale=scl[:, 96 + mb: 97 + mb]),
                               [("sSC", 1)])["sSC"]
                banks.add_free(b, "sSC", sc_)
            wbf_free[slot] = [("sMM", last_mm_for_slot)]
        scC = ctx.ctr["sSC"]
        stream_weight(dw3, 0, 4, 0)
        for mb in range(8):
            slot = mb % 2
            if mb + 1 < 8:
                stream_weight(dw3, mb + 1, 4, 1 - slot)
            for t2 in range(NT2):
                rsl = (mb * NT2 + t2) % 2
                rsem = "sXD%d" % rsl
                xd = dma("gpsimd", rsem, rstg[rsl][:, :],
                         dresid[mb * 128:(mb + 1) * 128, tok0 + t2 * 512: tok0 + t2 * 512 + 512],
                         waits=rs_free[rsl] + resid_all_st)[rsem]
                b = banks.take()
                w = banks.mm_waits(b) + ([wbf_ready[slot], ("sSC", scC)] if t2 == 0 else [])
                for kc in range(32):
                    ctx.emit("tensor", w if kc == 0 else [],
                             lambda e, b=b, kc=kc, t2=t2, slot=slot, st=(kc == 0):
                             e.matmul(pb[b][:, :], wbf[slot][:, kc * 128:(kc + 1) * 128],
                                      ha[:, kc * TP + t2 * 512: kc * TP + t2 * 512 + 512],
                                      start=st, stop=(kc == 31)), [("sMM", 1)] if kc == 31 else [])
                banks.set_ready(b, "sMM", ctx.ctr["sMM"])
                last_mm_for_slot = ctx.ctr["sMM"]
                ysl = (mb * NT2 + t2) % 2
                ev = ctx.emit("vector", banks.evac_wait(b) + [(rsem, xd)] + ysb_free[ysl],
                              lambda e, b=b, mb=mb, rsl=rsl, ysl=ysl:
                              e.scalar_tensor_tensor(ysb[ysl][:, :], pb[b][:, :],
                                                     scl[:, 128 + mb: 129 + mb], rstg[rsl][:, :],
                                                     ALU.mult, ALU.add), [("sEV", 1)])["sEV"]
                banks.add_free(b, "sEV", ev)
                rs_free[rsl] = [("sEV", ev)]
                ysem = "sVD%d" % ysl
                st = ctx.emit("scalar", [("sEV", ev)],
                              lambda e, ysl=ysl, mb=mb, t2=t2, tok0=tok0:
                              e.dma_start(out=dy[mb * 128:(mb + 1) * 128, tok0 + t2 * 512: tok0 + t2 * 512 + 512],
                                          in_=ysb[ysl][:, :]), [(ysem, 16)], is_dma=True)[ysem]
                n_y += 1
                ysb_free[ysl] = [("sEV", ev), (ysem, st)]
            wbf_free[slot] = [("sMM", last_mm_for_slot)]

    ctx.guard("scalar", [("sVD0", ctx.ctr["sVD0"]), ("sVD1", ctx.ctr["sVD1"])])

    with nc.Block() as blk:
        @blk.sync
        def _(e):
            for fn in ctx.q["sync"]:
                fn(e)

        @blk.gpsimd
        def _(e):
            for fn in ctx.q["gpsimd"]:
                fn(e)

        @blk.tensor
        def _(e):
            for fn in ctx.q["tensor"]:
                fn(e)

        @blk.vector
        def _(e):
            for fn in ctx.q["vector"]:
                fn(e)

        @blk.scalar
        def _(e):
            for fn in ctx.q["scalar"]:
                fn(e)

    es.close()
    return nc


def _prep_inputs(inputs, seg=2048):
    f = np.ascontiguousarray
    x = inputs["L0_attn_hidden_states"]
    Wq = inputs["L0_attn_Wq"]; Wk = inputs["L0_attn_Wk"]; Wv = inputs["L0_attn_Wv"]
    Wo = inputs["L0_attn_Wo"]; Wg1 = inputs["L0_attn_Wg1"]; Wg2 = inputs["L0_attn_Wg2"]
    bg2 = inputs["L0_attn_bg2"]; gnw = inputs["L0_attn_g_norm_weight"]
    w1 = inputs["L0_ffn_w1_w"]; s1 = inputs["L0_ffn_w1_s"]
    w2 = inputs["L0_ffn_w2_w"]; s2 = inputs["L0_ffn_w2_s"]
    h1 = inputs["L0_ffn_h1_w"]; hs1 = inputs["L0_ffn_h1_s"]
    h2 = inputs["L0_ffn_h2_w"]; hs2 = inputs["L0_ffn_h2_s"]
    w3 = inputs["L0_ffn_w3_w"]; s3 = inputs["L0_ffn_w3_s"]

    perm = []
    for j in range(2):
        for hp in range(4):
            for habs in (8 * j + hp, 8 * j + 4 + hp):
                perm.extend(range(habs * 64, habs * 64 + 64))
    perm = np.array(perm)

    wq_p = f(Wq[:, perm].reshape(8, 128, 1024))
    wo_p = f(Wo[perm, :].reshape(8, 128, 1024))
    wk_p = f(Wk.reshape(8, 128, 256))
    wv_p = f(Wv.reshape(8, 128, 256))
    wg1_p = f(Wg1.reshape(8, 128, 16))
    wg2a = f(np.vstack([Wg2, bg2[None, :]]))

    def tiles(wt, nmb):
        IN, OUT = wt.shape
        nkc = IN // 128
        return f(wt.reshape(nkc, 128, nmb, 128).transpose(2, 1, 0, 3).reshape(nmb, 128, nkc * 128))

    w1t = tiles(w1.T, 32)
    w2t = tiles(w2.T, 32)
    h1t = tiles(h1.T, 32)
    h2t = tiles(h2.T, 32)
    w3t = tiles(w3.T, 8)

    scl = np.zeros((128, 137), np.float32)
    scl[:, 0:32] = s1.reshape(32, 128).T
    scl[:, 32:64] = s2.reshape(32, 128).T
    scl[:, 64:96] = hs1.reshape(32, 128).T
    scl[:, 96:128] = hs2.reshape(32, 128).T
    scl[:, 128:136] = s3.reshape(8, 128).T
    scl[:, 136] = np.tile(gnw, 2)
    scl = f(scl)

    cstv = np.zeros((128, 8), np.float32)
    cstv[:, 0] = 1e-5
    cstv[:, 1] = 1e-6
    cstv[:, 2] = 1.0
    cstv = f(cstv)

    sidx = np.arange(C)
    m_incl = (sidx[:, None] <= sidx[None, :]).astype(np.float32)
    m_after = (sidx[:, None] > sidx[None, :]).astype(np.float32)
    z64 = np.zeros((C, C), np.float32)
    mcat = np.zeros((128, 256), np.float32)
    mcat[0:64, 0:64] = m_incl
    mcat[0:64, 64:128] = m_after
    mcat[64:128, 128:192] = m_incl
    mcat[64:128, 192:256] = m_after
    mcat = f(mcat)
    maskP0 = np.zeros((128, 256), np.float32)
    maskP0[0:64, :] = np.tile(m_incl, (1, 4))
    maskP1 = np.zeros((128, 256), np.float32)
    maskP1[64:128, :] = np.tile(m_incl, (1, 4))
    maskP0 = f(maskP0.astype(BF16))
    maskP1 = f(maskP1.astype(BF16))
    ident = f(np.eye(128, dtype=np.float32).astype(BF16))
    onesv = np.zeros((128, 4), np.float32)
    onesv[:, 0] = 1.0
    onesv[:64, 1] = 1.0
    onesv[64:, 2] = 1.0
    onesv = f(onesv)
    onesr = np.zeros((2, 128), np.float32)
    onesr[0, :] = 1.0
    onesr = f(onesr)
    hbcm = np.zeros((2, 128), np.float32)
    hbcm[0, 0:64] = 1.0
    hbcm[1, 64:128] = 1.0
    hbcm = f(hbcm)

    onesbv = np.ones((128, 1), BF16)
    shared = dict(onesb=onesbv,
                  wq=wq_p, wk=wk_p, wv=wv_p, wg1=wg1_p, wg2a=wg2a, wo=wo_p,
                  w1t=w1t, w2t=w2t, h1t=h1t, h2t=h2t, w3t=w3t,
                  scl=scl, cst=cstv, mcat=mcat, maskP0=maskP0, maskP1=maskP1, ident=ident,
                  onesv=onesv, onesr=onesr, hbc=hbcm)

    nseg = x.shape[1] // seg
    in_maps = []
    for core in range(NCORE):
        b, sgi = divmod(core, nseg)
        m = dict(shared)
        m["x"] = f(x[b, sgi * seg:(sgi + 1) * seg, :].T)
        mu = np.zeros(8, np.float32)
        nu = np.zeros((8, 8), np.float32)
        for i in range(8):
            if i // nseg == b and i < core:
                mu[i] = 1.0
                for l in range(8):
                    if l // nseg == b and i < l < core:
                        nu[i, l] = 1.0
        m["muv"] = f(np.tile(mu[None, :], (128, 1)))
        m["nuv"] = f(np.tile(nu.reshape(1, 64), (128, 1)))
        in_maps.append(m)
    return in_maps


def kernel(**inputs):
    seg = 2048
    in_maps = _prep_inputs(inputs, seg)
    nc = build_program(seg)
    res = run_bass_kernel_spmd(nc, in_maps, core_ids=list(range(NCORE)))
    out = np.empty((2, 8192, 1024), np.float32)
    nseg = 8192 // seg
    for core in range(NCORE):
        b, sgi = divmod(core, nseg)
        out[b, sgi * seg:(sgi + 1) * seg, :] = res.results[core]["y"].T
    return out


# version 18
# speedup vs baseline: 1.0305x; 1.0305x over previous
import contextlib
import numpy as np
import ml_dtypes

import concourse.bass as bass
import concourse.mybir as mybir
from concourse.bass_utils import run_bass_kernel_spmd

BF16 = ml_dtypes.bfloat16
F32 = mybir.dt.float32
BF = mybir.dt.bfloat16
AF = mybir.ActivationFunctionType
ALU = mybir.AluOpType
AX = mybir.AxisListType

NCORE = 8
D_MODEL = 1024
D_HID = 4096
H, HKV, D = 16, 4, 64
C = 64
GLN = 16.0

ENGINES = ("sync", "gpsimd", "tensor", "vector", "scalar")


SELF_EDGE_ENGINES = {"vector": "sEV", "scalar": "sSC", "gpsimd": "sCS"}


class Ctx:

    def __init__(self, self_edges=False):
        self.q = {e: [] for e in ENGINES}
        self.ctr = {s: 0 for s in ("sWD0", "sWD1", "sCN", "sXD0", "sXD1", "sVD0", "sVD1",
                                   "sCS", "sMM", "sSC", "sEV", "sCC")}
        self.sems = {}
        self.self_edges = self_edges

    def emit(self, eng, waits, build, incs, is_dma=False):
        waits = [(s, v) for (s, v) in waits if v > 0]
        incs = list(incs)
        if self.self_edges and eng in SELF_EDGE_ENGINES and not is_dma:
            sname = SELF_EDGE_ENGINES[eng]
            assert any(s == sname for s, _ in incs), (eng, incs)
            if self.ctr[sname] > 0:
                waits = waits + [(sname, self.ctr[sname])]
        for s, a in incs:
            self.ctr[s] += a
        sems = self.sems

        def fn(e, waits=tuple(waits), incs=tuple(incs), build=build):
            for s, v in waits:
                e.wait_ge(sems[s], v)
            ins = build(e)
            for s, a in incs:
                ins = ins.then_inc(sems[s], a)
        self.q[eng].append(fn)
        return {s: self.ctr[s] for s, _ in incs}

    def barrier_waits(self):
        return [(s, self.ctr[s]) for s in ("sMM", "sEV", "sSC", "sCS")]

    def guard(self, eng, waits):
        sems = self.sems

        def fn(e, waits=tuple([(s, v) for s, v in waits if v > 0])):
            for s, v in waits:
                e.wait_ge(sems[s], v)
        self.q[eng].append(fn)


class Banks:

    def __init__(self, ctx, tensors):
        self.ctx = ctx
        self.t = tensors
        self.free_w = [[] for _ in tensors]
        self.ready = [None] * len(tensors)
        self.in_use = [False] * len(tensors)
        self.rr = 0

    def take(self):
        for _ in range(len(self.t)):
            i = self.rr % len(self.t)
            self.rr += 1
            if not self.in_use[i]:
                self.in_use[i] = True
                return i
        raise RuntimeError("PSUM bank pool exhausted: all banks in use")

    def mm_waits(self, i):
        w = self.free_w[i]
        self.free_w[i] = []
        return w

    def set_ready(self, i, sem, val):
        self.ready[i] = (sem, val)

    def evac_wait(self, i):
        return [self.ready[i]] if self.ready[i] else []

    def add_free(self, i, sem, val):
        self.free_w[i].append((sem, val))
        self.in_use[i] = False


DEBUG = False
SIM_SELF_EDGES = True

def build_program(seg=2048):
    assert seg % 128 == 0
    NT = seg // 512
    NTT = seg // 128
    NCH = seg // C
    TP = seg // 2
    NT2 = TP // 512

    nc = bass.Bass("TRN2", target_bir_lowering=False)

    dx = nc.dram_tensor("x", [D_MODEL, seg], F32, kind="ExternalInput").ap()
    dwq = nc.dram_tensor("wq", [8, 128, 1024], F32, kind="ExternalInput").ap()
    dwk = nc.dram_tensor("wk", [8, 128, 256], F32, kind="ExternalInput").ap()
    dwv = nc.dram_tensor("wv", [8, 128, 256], F32, kind="ExternalInput").ap()
    dwg1 = nc.dram_tensor("wg1", [8, 128, 16], F32, kind="ExternalInput").ap()
    dwg2a = nc.dram_tensor("wg2a", [17, 256], F32, kind="ExternalInput").ap()
    dwo = nc.dram_tensor("wo", [8, 128, 1024], F32, kind="ExternalInput").ap()
    dw1 = nc.dram_tensor("w1t", [32, 128, 1024], F32, kind="ExternalInput").ap()
    dw2 = nc.dram_tensor("w2t", [32, 128, 1024], F32, kind="ExternalInput").ap()
    dh1 = nc.dram_tensor("h1t", [32, 128, 4096], F32, kind="ExternalInput").ap()
    dh2 = nc.dram_tensor("h2t", [32, 128, 4096], F32, kind="ExternalInput").ap()
    dw3 = nc.dram_tensor("w3t", [8, 128, 4096], F32, kind="ExternalInput").ap()
    dsc = nc.dram_tensor("scl", [128, 137], F32, kind="ExternalInput").ap()
    dcst = nc.dram_tensor("cst", [128, 8], F32, kind="ExternalInput").ap()
    dmcat = nc.dram_tensor("mcat", [128, 256], F32, kind="ExternalInput").ap()
    dmaskP0 = nc.dram_tensor("maskP0", [128, 256], BF, kind="ExternalInput").ap()
    dmaskP1 = nc.dram_tensor("maskP1", [128, 256], BF, kind="ExternalInput").ap()
    dident = nc.dram_tensor("ident", [128, 128], BF, kind="ExternalInput").ap()
    dones = nc.dram_tensor("onesv", [128, 4], F32, kind="ExternalInput").ap()
    donesb = nc.dram_tensor("onesb", [128, 1], BF, kind="ExternalInput").ap()
    donesr = nc.dram_tensor("onesr", [2, 128], F32, kind="ExternalInput").ap()
    dhbc = nc.dram_tensor("hbc", [2, 128], F32, kind="ExternalInput").ap()
    dmu = nc.dram_tensor("muv", [128, 8], F32, kind="ExternalInput").ap()
    dnu = nc.dram_tensor("nuv", [128, 64], F32, kind="ExternalInput").ap()

    dy = nc.dram_tensor("y", [D_MODEL, seg], F32, kind="ExternalOutput").ap()
    if DEBUG:
        ddbg = {n: nc.dram_tensor("dbg_" + n, s, d, kind="ExternalOutput").ap()
                for n, s, d in (
                    ("xn", [128, 8 * seg], BF), ("qg0", [128, 4 * seg], BF),
                    ("kfm", [128, 2 * seg], BF), ("vtm", [128, 2 * seg], BF),
                    ("g1aug", [32, seg], BF), ("og0", [128, 4 * seg], BF),
                    ("og1", [128, 4 * seg], BF),
                    ("sf", [128, 128], F32), ("eGg0", [128, seg], BF),
                    ("gath", [128, 8 * 132], F32), ("sin", [128, 128], BF),
                    ("g1e", [16, 2048], F32), ("g1a2", [32, seg], BF),
                    ("w8", [128, 16], F32), ("muv", [128, 8], F32),
                    ("w8h", [128, 16], F32), ("qg1", [128, 4 * seg], BF),
                    ("ch5", [128, 1152], F32),
                    ("resid", [D_MODEL, seg], F32), ("gtm0", [128, 256], F32))}

    dresid = nc.dram_tensor("resid", [D_MODEL, seg], F32, kind="Internal").ap()
    dccin = nc.dram_tensor("ccin", [128, 132], F32, kind="Internal").ap()
    dccout = nc.dram_tensor("ccout", [NCORE * 128, 132], F32, kind="Internal").ap()

    ctx = Ctx(self_edges=SIM_SELF_EDGES)
    es = contextlib.ExitStack()
    sb = lambda name, shape, dt: es.enter_context(nc.sbuf_tensor(name, shape, dt))

    xn = sb("xn", [128, 8 * seg], BF)
    SEGQ = seg * 4
    HW = max(32 * TP, 8 * 1024 + 2 * SEGQ + 2 * seg + 4096)
    ha = sb("ha", [128, HW], BF)
    hb = sb("hb", [128, HW], BF)
    wo_b = ha[:, 0:8 * 1024]
    og = [ha[:, 8 * 1024 + j * SEGQ: 8 * 1024 + (j + 1) * SEGQ] for j in range(2)]
    kfm = ha[:, 8 * 1024 + 2 * SEGQ: 8 * 1024 + 2 * SEGQ + 2 * seg]
    eGg = [ha[:, 8 * 1024 + 2 * SEGQ + 2 * seg + j * seg:
              8 * 1024 + 2 * SEGQ + 2 * seg + (j + 1) * seg] for j in range(2)]
    wq_b = hb[:, 0:8 * 1024]
    qg = [hb[:, 8 * 1024 + j * SEGQ: 8 * 1024 + (j + 1) * SEGQ] for j in range(2)]
    vtm = hb[:, 8 * 1024 + 2 * SEGQ: 8 * 1024 + 2 * SEGQ + 2 * seg]
    wk_b = hb[:, 8 * 1024 + 2 * SEGQ + 2 * seg: 8 * 1024 + 2 * SEGQ + 2 * seg + 2048]
    wv_b = hb[:, 8 * 1024 + 2 * SEGQ + 2 * seg + 2048: 8 * 1024 + 2 * SEGQ + 2 * seg + 4096]
    sbf_all = hb[:, 0:NCH * 128]
    EGH0 = NCH * 128
    eGhat = [hb[:, EGH0 + j * seg: EGH0 + (j + 1) * seg] for j in range(2)]

    af = sb("af", [128, 5120], F32)
    wst = [af[:, 0:1024], af[:, 1024:2048]]
    xstg = [af[:, 2048:2560], af[:, 2560:3072]]
    sqscr = af[:, 3072:3584]
    rowbuf = af[:, 3584:5120]
    gtm = [af[:, 2048 + i * 256: 2048 + (i + 1) * 256] for i in range(4)]
    gscr = [af[:, 3072 + i * 256: 3072 + (i + 1) * 256] for i in range(2)]
    exg = [[af[:, 3584 + p * 384 + i * 64: 3584 + p * 384 + (i + 1) * 64] for i in range(6)]
           for p in range(2)]
    pref = af[:, 4352:4360]
    sf32 = [af[:, 4360 + j * 64: 4360 + (j + 1) * 64] for j in range(2)]
    eprefs = af[:, 4488:4552]
    decays = af[:, 4552:4616]
    blob = af[:, 0:132]
    gath = af[:, 132:1188]
    sin = [af[:, 1188 + i * 64: 1188 + (i + 1) * 64] for i in range(4)]
    w8 = af[:, 1444:1460]
    sqo = af[:, 2048:2560]
    orow = af[:, 2560:3584]
    xwo = [af[:, 4096 + i * 512: 4096 + (i + 1) * 512] for i in range(2)]
    silu_s = [af[:, 2048 + i * 512: 2048 + (i + 1) * 512] for i in range(2)]
    ysb = [af[:, 3072 + i * 512: 3072 + (i + 1) * 512] for i in range(2)]
    rstg = [af[:, 4096 + i * 512: 4096 + (i + 1) * 512] for i in range(2)]

    ab = sb("ab", [128, 8192], BF)
    wbf = [ab[:, 0:4096], ab[:, 4096:8192]]
    g1aug = ab[0:32, 0:seg]
    wg1_b = ab[:, 2048:2176]
    wg2a_b = ab[0:32, 2176:2432]
    qs = [[ab[:, 2432 + p * 512 + j * 256: 2432 + p * 512 + (j + 1) * 256] for j in range(2)]
          for p in range(2)]
    ks = [[ab[:, 3456 + p * 256 + j * 128: 3456 + p * 256 + (j + 1) * 128] for j in range(2)]
          for p in range(2)]
    kppf = [[ab[:, 3968 + p * 256 + j * 128: 3968 + p * 256 + (j + 1) * 128] for j in range(2)]
            for p in range(2)]
    kppt = [[ab[:, 4480 + p * 256 + j * 128: 4480 + p * 256 + (j + 1) * 128] for j in range(2)]
            for p in range(2)]
    pscr = [[ab[:, 4992 + p * 1024 + g * 256: 4992 + p * 1024 + (g + 1) * 256] for g in range(4)]
            for p in range(2)]
    stot = [[ab[:, 7168 + p * 128 + j * 64: 7168 + p * 128 + (j + 1) * 64] for j in range(2)]
            for p in range(2)]
    maskPh = [ab[:, 7424 + h * 256: 7424 + (h + 1) * 256] for h in range(2)]

    scl = sb("scl_s", [128, 137], F32)
    cst = sb("cst_s", [128, 8], F32)
    mcat2 = sb("mcat2_s", [128, 256], F32)
    ident = sb("ident_s", [128, 128], BF)
    onesv = sb("onesv_s", [128, 4], F32)
    onesb = sb("onesb_s", [128, 1], BF)
    onesr = sb("onesr_s", [2, 128], F32)
    hbc = sb("hbc_s", [2, 128], F32)
    muv = sb("muv_s", [128, 8], F32)
    nuv = sb("nuv_s", [128, 64], F32)

    pb = [es.enter_context(nc.psum_tensor(f"p{i}", [128, 512], F32)) for i in range(6)]
    pTs = [es.enter_context(nc.psum_tensor(f"pT{i}", [128, 1024], BF)) for i in range(2)]

    for s in ("sWD0", "sWD1", "sCN", "sXD0", "sXD1", "sVD0", "sVD1",
              "sCS", "sMM", "sSC", "sEV", "sCC"):
        ctx.sems[s] = es.enter_context(nc.semaphore(s))

    banks = Banks(ctx, pb[:4])

    def dma(eng, sem, out, in_, waits=()):
        return ctx.emit(eng, list(waits), lambda e, o=out, i=in_: e.dma_start(out=o, in_=i), [(sem, 16)],
                        is_dma=True)

    def mm_group(parts, bank_i, extra_waits=(), region=None):
        w = banks.mm_waits(bank_i) + list(extra_waits)
        n = len(parts)
        for idx, (lh, rh) in enumerate(parts):
            incs = [("sMM", 1)] if idx == n - 1 else []
            out = region if region is not None else pb[bank_i][:, :]
            ctx.emit("tensor", w if idx == 0 else [],
                     lambda e, o=out, l=lh, r=rh, st=(idx == 0), sp=(idx == n - 1):
                     e.matmul(o, l, r, start=st, stop=sp), incs)
        banks.set_ready(bank_i, "sMM", ctx.ctr["sMM"])
        return ctx.ctr["sMM"]

    for out, i in ((scl, dsc), (cst, dcst), (mcat2, dmcat), (maskPh[0], dmaskP0),
                   (maskPh[1], dmaskP1), (ident, dident), (onesv, dones), (onesb, donesb),
                   (onesr, donesr), (hbc, dhbc), (muv, dmu), (nuv, dnu)):
        dma("sync", "sCN", out[:, :], i[:, :])
    consts_wd = ctx.ctr["sCN"]

    for kc in range(8):
        dma("gpsimd", "sWD0", wq_b[:, kc * 1024:(kc + 1) * 1024], dwq[kc, :, :])
        dma("gpsimd", "sWD0", wo_b[:, kc * 1024:(kc + 1) * 1024], dwo[kc, :, :])
        dma("gpsimd", "sWD0", wk_b[:, kc * 256:(kc + 1) * 256], dwk[kc, :, :])
        dma("gpsimd", "sWD0", wv_b[:, kc * 256:(kc + 1) * 256], dwv[kc, :, :])
        dma("gpsimd", "sWD0", wg1_b[:, kc * 16:(kc + 1) * 16], dwg1[kc, :, :])
    dma("gpsimd", "sWD0", wg2a_b[0:17, :], dwg2a[:, :])
    attn_w_wd = ctx.ctr["sWD0"]

    ctx.emit("vector", [], lambda e: e.memset(g1aug[:, :], 1.0), [("sEV", 1)])
    g1_memset_ev = ctx.ctr["sEV"]

    xb = [ha[:, 8 * 1024 + q * 4096: 8 * 1024 + (q + 1) * 4096] for q in range(2)]
    sqs = [sqscr, af[:, 640:1152]]
    xb_free = [[], []]
    last_xd = [0, 0]
    xdmas = {}
    S1 = {}
    sq_mm_prev = [0, 0]
    prev_tt_mm = [0]
    rows_guard = [0]

    def p1_dma(tt):
        q = tt % 2
        xds = []
        for kc in range(8):
            sl = kc % 2
            sem = "sXD%d" % sl
            xd = dma("gpsimd", sem, xb[q][:, kc * 512:(kc + 1) * 512],
                     dx[kc * 128:(kc + 1) * 128, tt * 512:(tt + 1) * 512],
                     waits=[(sem, last_xd[sl])] + (xb_free[q] if kc < 2 else []) +
                           ([("sCN", consts_wd)] if (tt == 0 and kc == 0) else []))[sem]
            last_xd[sl] = xd
            xds.append((sem, xd))
        xdmas[tt] = xds

    def p1_stats(tt):
        q = tt % 2
        first = True
        for kc in range(8):
            sl = kc % 2
            sem, xd = xdmas[tt][kc]
            ctx.emit("tensor", [(sem, xd)] + ([("sSC", rows_guard[0])] if first else []),
                     lambda e, q=q, kc=kc, st=first, sp=(kc == 7):
                     e.matmul(pb[4][0:1, :], onesb[:, 0:1], xb[q][:, kc * 512:(kc + 1) * 512],
                              start=st, stop=sp), [])
            sc_ = ctx.emit("scalar", [(sem, xd), ("sMM", sq_mm_prev[sl])] +
                           ([("sEV", S1[tt - 1]["p2ev"])] if (kc == 0 and tt >= 1) else []),
                           lambda e, q=q, kc=kc, sl=sl:
                           e.activation(sqs[sl][:, :], xb[q][:, kc * 512:(kc + 1) * 512], AF.Square),
                           [("sSC", 1)])["sSC"]
            mmres = ctx.emit("tensor", [("sSC", sc_)],
                             lambda e, sl=sl, st=first, sp=(kc == 7):
                             e.matmul(pb[5][0:1, :], onesv[:, 0:1], sqs[sl][:, :], start=st, stop=sp),
                             [("sMM", 1)])["sMM"]
            sq_mm_prev[sl] = mmres
            first = False
        S1[tt] = dict(mm_stats=ctx.ctr["sMM"])

    def p1_rows(tt):
        d = S1[tt]
        ctx.emit("scalar", [("sMM", d["mm_stats"])],
                 lambda e: e.mul(rowbuf[0:1, 0:512], pb[4][0:1, :], 1.0 / 1024.0), [("sSC", 1)])
        sc_mu = ctx.ctr["sSC"]
        ctx.emit("scalar", [],
                 lambda e: e.mul(rowbuf[0:1, 512:1024], pb[5][0:1, :], 1.0 / 1024.0), [("sSC", 1)])
        sc_ex2 = ctx.ctr["sSC"]
        rows_guard[0] = sc_ex2
        ctx.emit("vector", [("sSC", sc_ex2), ("sMM", prev_tt_mm[0])],
                 lambda e: e.tensor_mul(rowbuf[0:1, 1024:1536], rowbuf[0:1, 0:512], rowbuf[0:1, 0:512]),
                 [("sEV", 1)])
        ctx.emit("vector", [],
                 lambda e: e.tensor_sub(rowbuf[0:1, 512:1024], rowbuf[0:1, 512:1024], rowbuf[0:1, 1024:1536]),
                 [("sEV", 1)])
        ev_var = ctx.ctr["sEV"]
        ctx.emit("scalar", [("sEV", ev_var)],
                 lambda e: e.activation(rowbuf[0:1, 512:1024], rowbuf[0:1, 512:1024], AF.Ln,
                                        bias=cst[0:1, 0:1]), [("sSC", 1)])
        ctx.emit("scalar", [],
                 lambda e: e.activation(rowbuf[0:1, 1024:1536], rowbuf[0:1, 512:1024], AF.Exp, scale=-0.5),
                 [("sSC", 1)])
        d["sc_a"] = ctx.ctr["sSC"]
        ctx.emit("vector", [("sSC", d["sc_a"])],
                 lambda e: e.tensor_mul(rowbuf[0:1, 0:512], rowbuf[0:1, 0:512], rowbuf[0:1, 1024:1536]),
                 [("sEV", 1)])
        d["ev_b"] = ctx.ctr["sEV"]

    def p1_bcast(tt):
        d = S1[tt]
        bA2 = banks.take()
        d["mm_a"] = mm_group([(onesr[0:1, :], rowbuf[0:1, 1024:1536])], bA2, extra_waits=[("sSC", d["sc_a"])])
        bB2 = banks.take()
        d["mm_b"] = mm_group([(onesr[0:1, :], rowbuf[0:1, 0:512])], bB2, extra_waits=[("sEV", d["ev_b"])])
        d["bA2"], d["bB2"] = bA2, bB2
        prev_tt_mm[0] = d["mm_b"]

    def p1_pass2(tt):
        q = tt % 2
        d = S1[tt]
        bA2, bB2 = d["bA2"], d["bB2"]
        for kc in range(8):
            ctx.emit("vector", [("sMM", d["mm_a"])] if kc == 0 else [],
                     lambda e, q=q, kc=kc, bA2=bA2:
                     e.tensor_tensor(sqscr[:, :], xb[q][:, kc * 512:(kc + 1) * 512],
                                     pb[bA2][:, :], ALU.mult), [("sEV", 1)])
            ctx.emit("vector", [("sMM", d["mm_b"])] if kc == 0 else [],
                     lambda e, kc=kc, tt=tt, bB2=bB2:
                     e.tensor_tensor(xn[:, kc * seg + tt * 512: kc * seg + tt * 512 + 512],
                                     sqscr[:, :], pb[bB2][:, :], ALU.subtract), [("sEV", 1)])
        d["p2ev"] = ctx.ctr["sEV"]
        xb_free[q] = [("sEV", d["p2ev"])]
        banks.add_free(bA2, "sEV", ctx.ctr["sEV"])
        banks.add_free(bB2, "sEV", ctx.ctr["sEV"])

    def p1_proj(tt):
        pw = [("sEV", S1[tt]["p2ev"]), ("sWD0", attn_w_wd), ("sEV", g1_memset_ev)]
        for m in range(8):
            j, hp = divmod(m, 4)
            b = banks.take()
            mm_group([(wq_b[:, kc * 1024 + m * 128: kc * 1024 + (m + 1) * 128],
                       xn[:, kc * seg + tt * 512: kc * seg + (tt) * 512 + 512]) for kc in range(8)], b,
                     extra_waits=pw)
            sc_ = ctx.emit("scalar", banks.evac_wait(b),
                           lambda e, j=j, hp=hp, tt=tt, b=b:
                           e.activation(qg[j][:, hp * seg + tt * 512: hp * seg + tt * 512 + 512],
                                        pb[b][:, :], AF.Relu, scale=0.125), [("sSC", 1)])["sSC"]
            banks.add_free(b, "sSC", sc_)
        for m in range(2):
            b = banks.take()
            mm_group([(wk_b[:, kc * 256 + m * 128: kc * 256 + (m + 1) * 128],
                       xn[:, kc * seg + tt * 512: kc * seg + tt * 512 + 512]) for kc in range(8)], b,
                     extra_waits=pw)
            sc_ = ctx.emit("scalar", banks.evac_wait(b),
                           lambda e, m=m, tt=tt, b=b:
                           e.activation(kfm[:, m * seg + tt * 512: m * seg + tt * 512 + 512],
                                        pb[b][:, :], AF.Relu), [("sSC", 1)])["sSC"]
            banks.add_free(b, "sSC", sc_)
        for ti in range(4 * tt, 4 * tt + 4):
            b = banks.take()
            mm_group([(xn[:, kc * seg + ti * 128: kc * seg + (ti + 1) * 128],
                       wv_b[:, kc * 256:(kc + 1) * 256]) for kc in range(8)], b,
                     region=pb[b][:, 0:256], extra_waits=pw)
            ev = ctx.emit("vector", banks.evac_wait(b),
                          lambda e, ti=ti, b=b: e.tensor_copy(vtm[:, ti * 256:(ti + 1) * 256], pb[b][:, 0:256]),
                          [("sEV", 1)])["sEV"]
            banks.add_free(b, "sEV", ev)
        b = banks.take()
        mm_group([(wg1_b[:, kc * 16:(kc + 1) * 16],
                   xn[:, kc * seg + tt * 512: kc * seg + tt * 512 + 512]) for kc in range(8)], b,
                 region=pb[b][0:16, :], extra_waits=pw)
        ev = ctx.emit("vector", banks.evac_wait(b),
                      lambda e, tt=tt, b=b: e.tensor_copy(g1aug[0:16, tt * 512:(tt + 1) * 512], pb[b][0:16, :]),
                      [("sEV", 1)])["sEV"]
        banks.add_free(b, "sEV", ev)

    p1_dma(0)
    for i in range(-1, NT):
        if i + 1 < NT:
            p1_stats(i + 1)
        if i + 2 < NT:
            p1_dma(i + 2)
        if i >= 0:
            p1_proj(i)
        if i + 1 < NT:
            p1_rows(i + 1)
            p1_bcast(i + 1)
            p1_pass2(i + 1)
    if DEBUG:
        ctx.guard("scalar", ctx.barrier_waits())
        dma("scalar", "sVD0", ddbg["xn"][:, :], xn[:, :])
    ln_done = ctx.barrier_waits()

    proj_done = ctx.barrier_waits()


    for e in ("tensor", "vector", "scalar", "gpsimd"):
        ctx.guard(e, proj_done)

    for j in range(2):
        ctx.emit("vector", [], lambda e, j=j: e.memset(sf32[j][:, :], 0.0), [("sEV", 1)])
    for p in range(2):
        for j in range(2):
            ctx.emit("vector", [], lambda e, p=p, j=j: e.memset(ks[p][j][:, :], 0.0), [("sEV", 1)])
    ctx.emit("scalar", [], lambda e: e.memzero(pref[:, :]), [("sSC", 1)])
    init_ev = ctx.ctr["sEV"]
    init_sc = ctx.ctr["sSC"]
    ctx.guard("tensor", [("sEV", init_ev)])
    ctx.guard("scalar", [("sEV", init_ev)])

    gslot_free = [[] for _ in range(4)]
    gslot_ready = [None] * 4
    glog_emitted = 0
    prev_gate_ev = 0

    def emit_gate_ttile(t):
        nonlocal glog_emitted, prev_gate_ev
        sl = t % 4
        b = banks.take()
        mm_group([(g1aug[0:17, t * 128:(t + 1) * 128], wg2a_b[0:17, :])], b,
                 region=pb[b][:, 0:256])
        w0 = banks.evac_wait(b) + gslot_free[sl] + [("sEV", prev_gate_ev)]
        ctx.emit("scalar", w0, lambda e, b=b: e.activation(gscr[0][:, :], pb[b][:, 0:256], AF.Abs),
                 [("sSC", 1)])
        ctx.emit("scalar", [], lambda e: e.activation(gscr[1][:, :], gscr[0][:, :], AF.Exp, scale=-1.0),
                 [("sSC", 1)])
        ctx.emit("scalar", [], lambda e: e.activation(gscr[0][:, :], gscr[1][:, :], AF.Ln, bias=cst[:, 2:3]),
                 [("sSC", 1)])
        sc1 = ctx.ctr["sSC"]
        ev = ctx.emit("vector", [("sSC", sc1)] + banks.evac_wait(b) + gslot_free[sl],
                      lambda e, b=b: e.tensor_scalar_min(gscr[1][:, :], pb[b][:, 0:256], 0.0),
                      [("sEV", 1)])["sEV"]
        banks.add_free(b, "sEV", ev)
        ev2 = ctx.emit("vector", [],
                       lambda e, sl=sl: e.tensor_sub(gtm[sl][:, :], gscr[1][:, :], gscr[0][:, :]),
                       [("sEV", 1)])["sEV"]
        gslot_ready[sl] = ("sEV", ev2)
        prev_gate_ev = ev2
        glog_emitted = t + 1

    for t in range(min(3, NTT)):
        emit_gate_ttile(t)

    factor_ev = {}
    factor_sc = {}
    q_cs = {}
    mm_after_o = {}
    pT_free = [[], []]
    pp = 0
    pgbs = [pb[4], pb[5]]
    pg_free = [[("sSC", rows_guard[0])], [("sSC", rows_guard[0])]]
    pg_ready = [None, None]

    def apf(view, extra_off, dims):
        base = view[:, :] if hasattr(view, "tensor") is False else view
        return bass.AP(tensor=base.tensor, offset=base.offset + extra_off,
                       ap=[list(base.ap[0])] + [list(d) for d in dims])

    hbv = hb[:, :]
    hav = ha[:, :]
    abv = ab[:, :]
    afv = af[:, :]
    QG0 = 8 * 1024
    KFM0 = 8 * 1024 + 2 * SEGQ
    def emit_gate_mm(t):
        nonlocal glog_emitted
        while glog_emitted < min(t + 3, NTT):
            emit_gate_ttile(glog_emitted)
        tb = t % 2
        sl2 = t % 4
        pgb = pgbs[tb]
        for j in range(2):
            w = (pg_free[tb] if j == 0 else []) + [gslot_ready[sl2]]
            pg_free[tb] = []
            ctx.emit("tensor", w,
                     lambda e, j=j, sl2=sl2, pgb=pgb:
                     e.matmul(pgb[:, j * 256:(j + 1) * 256],
                              gtm[sl2][:, j * 128:(j + 1) * 128],
                              mcat2[:, :], start=True, stop=True), [("sMM", 1)])
        pg_ready_t[t] = ("sMM", ctx.ctr["sMM"])
        gslot_free[sl2] = [("sMM", ctx.ctr["sMM"])]

    def emit_pre_sv(c):
        nonlocal pp
        p = c % 2
        t, half = divmod(c, 2)
        tb = t % 2
        pgb = pgbs[tb]
        co = half * 128
        EX0 = 3584 + p * 384
        w_war = [pg_ready_t[t], ("sEV", factor_ev.get(c - 2, 0))]
        if c == 0:
            w_war = w_war + [("sSC", init_sc)]
        pgv = pgb[:, :]
        ctx.emit("scalar", w_war,
                 lambda e, co=co, EX0=EX0, pgv=pgv:
                 e.activation(afv[:, EX0: EX0 + 128], apf(pgv, co, [[256, 2], [1, 64]]),
                              AF.Exp, scale=1.0 / GLN), [("sSC", 1)])
        ctx.emit("scalar", [],
                 lambda e, c=c, EX0=EX0:
                 e.copy(decays[:, c * 2: c * 2 + 2],
                        apf(afv, EX0 + 63, [[64, 2], [1, 1]])), [("sSC", 1)])
        ctx.emit("scalar", [],
                 lambda e, co=co, EX0=EX0, pgv=pgv:
                 e.activation(afv[:, EX0 + 128: EX0 + 256], apf(pgv, co, [[256, 2], [1, 64]]),
                              AF.Exp, scale=-1.0 / GLN), [("sSC", 1)])
        ctx.emit("scalar", [],
                 lambda e, co=co, EX0=EX0, pgv=pgv:
                 e.activation(afv[:, EX0 + 256: EX0 + 384], apf(pgv, co + 64, [[256, 2], [1, 64]]),
                              AF.Exp, scale=1.0 / GLN), [("sSC", 1)])
        ctx.emit("scalar", [],
                 lambda e, c=c, pq=pp:
                 e.activation(eprefs[:, c * 2: c * 2 + 2], apf(afv, 4352 + pq, [[2, 2], [1, 1]]),
                              AF.Exp, scale=1.0 / GLN), [("sSC", 1)])
        for j in range(2):
            ctx.emit("scalar", [], lambda e, j=j, co=co, pq=pp, pgb=pgb:
                     e.add(pref[:, j * 2 + (1 - pq): j * 2 + (1 - pq) + 1],
                           pgb[:, j * 256 + co + 63: j * 256 + co + 64],
                           pref[:, j * 2 + pq: j * 2 + pq + 1]), [("sSC", 1)])
        ctx.emit("scalar", [], lambda e, c=c, EX0=EX0:
                 e.copy(apf(hbv, EGH0 + c * 64, [[seg, 2], [1, 64]]),
                        afv[:, EX0: EX0 + 128]), [("sSC", 1)])
        sc_all = ctx.ctr["sSC"]
        sc_alls[c] = sc_all
        if half == 1:
            pg_free[tb] = [("sSC", sc_all)]
        pp = 1 - pp
        for j in range(2):
            ctx.emit("vector", [("sSC", sc_all), ("sMM", mm_after_P.get(c - 2, 0))] if j == 0 else [],
                     lambda e, j=j, c=c, p=p, EX0=EX0:
                     e.tensor_tensor(apf(abv, 2432 + p * 512 + j * 256, [[64, 4], [1, 64]]),
                                     apf(hbv, QG0 + j * SEGQ + c * 64, [[seg, 4], [1, 64]]),
                                     apf(afv, EX0 + j * 64, [[0, 4], [1, 64]]),
                                     ALU.mult), [("sEV", 1)])
        ctx.emit("vector", [],
                 lambda e, c=c, half=half, p=p, EX0=EX0:
                 e.tensor_tensor(apf(abv, 3456 + p * 256 + half * 64, [[128, 2], [1, 64]]),
                                 apf(hav, KFM0 + c * 64, [[seg, 2], [1, 64]]),
                                 apf(afv, EX0 + 128, [[64, 2], [1, 64]]), ALU.mult),
                 [("sEV", 1)])
        ctx.emit("vector", [], lambda e, half=half, p=p:
                 e.memset(apf(abv, 3968 + p * 256 + (1 - half) * 64, [[128, 2], [1, 64]]), 0.0),
                 [("sEV", 1)])
        ctx.emit("vector", [], lambda e, c=c, half=half, p=p, EX0=EX0:
                 e.tensor_tensor(apf(abv, 3968 + p * 256 + half * 64, [[128, 2], [1, 64]]),
                                 apf(hav, KFM0 + c * 64, [[seg, 2], [1, 64]]),
                                 apf(afv, EX0 + 256, [[64, 2], [1, 64]]), ALU.mult),
                 [("sEV", 1)])
        factor_ev[c] = ctx.ctr["sEV"]

    def emit_pre_t(c):
        p = c % 2
        t, half = divmod(c, 2)
        if half == 0 and t + 1 < NTT:
            emit_gate_mm(t + 1)
        for j in range(2):
            ctx.emit("tensor", ([("sEV", factor_ev[c])] if j == 0 else []) + pT_free[j],
                     lambda e, j=j, p=p: e.transpose(pTs[j][:, 0:128], kppf[p][j][:, :], ident[:, :]),
                     [("sMM", 1)])
            trans_mm[c * 2 + j] = ctx.ctr["sMM"]
        pbanks = []
        for g in range(4):
            j, gh = divmod(g, 2)
            b = banks.take()
            w = banks.mm_waits(b)
            ctx.emit("tensor", w,
                     lambda e, j=j, gh=gh, b=b, p=p:
                     e.matmul(pb[b][:, 0:256], ks[p][j][gh * 64:(gh + 1) * 64, :],
                              qs[p][j][gh * 64:(gh + 1) * 64, :], start=True, stop=True),
                     [("sMM", 1)])
            banks.set_ready(b, "sMM", ctx.ctr["sMM"])
            pbanks.append(b)
        P_banks[c] = pbanks
        mm_after_P[c] = ctx.ctr["sMM"]

    def emit_kppt_evac(c):
        p = c % 2
        for j in range(2):
            sc = ctx.emit("scalar", [("sMM", trans_mm[c * 2 + j])],
                          lambda e, j=j, p=p: e.copy(kppt[p][j][:, :], pTs[j][:, 0:128]),
                          [("sSC", 1)])["sSC"]
            pT_free[j] = [("sSC", sc)]
        kppt_ev[c] = ctx.ctr["sSC"]

    def emit_mask(c):
        p = c % 2
        t, half = divmod(c, 2)
        for g in range(4):
            j, gh = divmod(g, 2)
            b = P_banks[c][g]
            w = banks.evac_wait(b) + ([("sMM", mm_after_o.get(c - 2, 0))] if g == 0 else [])
            ev = ctx.emit("vector", w,
                          lambda e, j=j, gh=gh, b=b, half=half, p=p:
                          e.tensor_tensor(abv[:, 4992 + p * 1024 + j * 512 + gh * 256:
                                              4992 + p * 1024 + j * 512 + (gh + 1) * 256],
                                          pb[b][:, 0:256],
                                          maskPh[half][:, :], ALU.mult), [("sEV", 1)])["sEV"]
            banks.add_free(b, "sEV", ev)
        mask_ev[c] = ctx.ctr["sEV"]

    def emit_tail_t(c):
        p = c % 2
        t, half = divmod(c, 2)
        obanks = []
        for j in range(2):
            b = banks.take()
            w = banks.mm_waits(b) + ([("sEV", mask_ev[c])] if j == 0 else [])
            for gh in range(2):
                g = 2 * j + gh
                vsl = vtm[:, t * 256 + g * 64: t * 256 + (g + 1) * 64]
                tp1 = (0, gh * 64) if gh else None
                ctx.emit("tensor", w if gh == 0 else [],
                         lambda e, b=b, vsl=vsl, j=j, gh=gh, tp1=tp1, p=p:
                         e.matmul(pb[b][gh * 64:(gh + 1) * 64, 0:256], vsl,
                                  abv[:, 4992 + p * 1024 + j * 512 + gh * 256:
                                       4992 + p * 1024 + j * 512 + (gh + 1) * 256],
                                  start=True, stop=True, tile_position=tp1),
                         [("sMM", 1)] if gh == 1 else [])
            banks.set_ready(b, "sMM", ctx.ctr["sMM"])
            obanks.append(b)
        sbanks = []
        for j in range(2):
            b = banks.take()
            w = banks.mm_waits(b) + ([("sSC", kppt_ev[c])] if j == 0 else [])
            for gh in range(2):
                g = 2 * j + gh
                vsl = vtm[:, t * 256 + g * 64: t * 256 + (g + 1) * 64]
                tp = (0, gh * 64) if gh else None
                ctx.emit("tensor", w if gh == 0 else [],
                         lambda e, b=b, j=j, gh=gh, vsl=vsl, tp=tp, p=p:
                         e.matmul(pb[b][gh * 64:(gh + 1) * 64, 0:64],
                                  kppt[p][j][:, gh * 64:(gh + 1) * 64],
                                  vsl, start=True, stop=True, tile_position=tp),
                         [("sMM", 1)] if gh == 1 else [])
            banks.set_ready(b, "sMM", ctx.ctr["sMM"])
            sbanks.append(b)
        mm_after_o[c] = ctx.ctr["sMM"]
        tail_banks[c] = (obanks, sbanks)

    def emit_tail_v(c):
        obanks, sbanks = tail_banks[c]
        for j in range(2):
            b = obanks[j]
            dst = og[j].rearrange("p (hp t) -> p hp t", hp=4)[:, :, c * 64:(c + 1) * 64]
            oeng, osem = (("scalar", "sSC") if j == 0 else ("vector", "sEV"))
            ov = ctx.emit(oeng, banks.evac_wait(b),
                          (lambda e, b=b, dst=dst:
                           e.copy(dst, pb[b][:, 0:256].rearrange("p (hp t) -> p hp t", hp=4)))
                          if j == 0 else
                          (lambda e, b=b, dst=dst:
                           e.tensor_copy(dst, pb[b][:, 0:256].rearrange("p (hp t) -> p hp t", hp=4))),
                          [(osem, 1)])[osem]
            banks.add_free(b, osem, ov)
        for j in range(2):
            b = sbanks[j]
            ev = ctx.emit("vector", banks.evac_wait(b) +
                          ([("sSC", sbf_sc.get(c - 1, 0))] if j == 0 else []),
                          lambda e, b=b, j=j, c=c:
                          e.scalar_tensor_tensor(sf32[j][:, :], sf32[j][:, :],
                                                 decays[:, c * 2 + j: c * 2 + j + 1],
                                                 pb[b][:, 0:64],
                                                 ALU.mult, ALU.add), [("sEV", 1)])["sEV"]
            banks.add_free(b, "sEV", ev)
        stt_ev[c] = ctx.ctr["sEV"]

    def emit_sbf_copy(c):
        ctx.emit("scalar", [("sEV", stt_ev[c])], lambda e, c=c:
                 e.copy(sbf_all[:, c * 128: (c + 1) * 128],
                        afv[:, 4360: 4360 + 128]), [("sSC", 1)])
        sbf_sc[c] = ctx.ctr["sSC"]

    sc_alls = {}
    sbf_sc = {}
    mm_after_P = {}
    kppt_ev = {}
    mask_ev = {}
    P_banks = {}
    pg_ready_t = {}
    tail_banks = {}
    stt_ev = {}
    q_cs = {}
    trans_mm = {}
    emit_gate_mm(0)
    for c in range(NCH):
        emit_pre_sv(c)
        if c >= 1:
            emit_kppt_evac(c - 1)
        if c >= 2:
            emit_sbf_copy(c - 2)
        if c >= 1:
            emit_tail_t(c - 1)
            emit_tail_v(c - 1)
        emit_pre_t(c)
        emit_mask(c)
    emit_kppt_evac(NCH - 1)
    emit_sbf_copy(NCH - 2)
    emit_tail_t(NCH - 1)
    emit_tail_v(NCH - 1)
    emit_sbf_copy(NCH - 1)

    chunk_done = ctx.barrier_waits()
    for e in ("vector", "gpsimd", "scalar"):
        ctx.guard(e, chunk_done)
    ctx.emit("vector", [], lambda e: e.memset(blob[:, :], 0.0), [("sEV", 1)])
    for j in range(2):
        ctx.emit("vector", [], lambda e, j=j: e.tensor_copy(blob[:, j * 64:(j + 1) * 64], sf32[j][:, :]),
                 [("sEV", 1)])
        ctx.emit("vector", [], lambda e, j=j, p=pp:
                 e.tensor_copy(blob[:, 128 + j:129 + j], pref[:, j * 2 + p: j * 2 + p + 1]), [("sEV", 1)])
    ev_blob = ctx.ctr["sEV"]
    xd = dma("gpsimd", "sXD0", dccin[:, :], blob[:, :], waits=[("sEV", ev_blob)])["sXD0"]
    cc = ctx.emit("gpsimd", [("sXD0", xd)],
                  lambda e: e.collective_compute("AllGather", ALU.bypass,
                                                 replica_groups=[list(range(NCORE))],
                                                 ins=[dccin[:, :]], outs=[dccout[:, :]]),
                  [("sCC", 1)], is_dma=True)["sCC"]
    xd = dma("gpsimd", "sXD0", gath[:, :].rearrange("p (r c) -> p r c", r=8),
             dccout.rearrange("(r p) c -> p r c", p=128), waits=[("sCC", cc)])["sXD0"]
    gath_xd = xd
    gath3 = gath[:, :].rearrange("p (r c) -> p r c", r=8)
    sumg_all = [gath3[:, :, 128 + j] for j in range(2)]
    for j in range(2):
        ctx.emit("vector", [("sXD0", gath_xd)],
                 lambda e, j=j: e.memset(sin[j * 2][:, :], 0.0), [("sEV", 1)])
    par = [0, 0]
    for i in range(NCORE):
        ctx.emit("vector", [("sXD0", gath_xd)],
                 lambda e, i=i: e.tensor_tensor(w8[:, 0:8], sumg_all[0],
                                                nuv[:, i * 8:(i + 1) * 8], ALU.mult), [("sEV", 1)])
        ctx.guard("vector", [("sEV", ctx.ctr["sEV"])])
        ctx.emit("vector", [],
                 lambda e: e.tensor_reduce(w8[:, 8:9], w8[:, 0:8], AX.X, ALU.add), [("sEV", 1)])
        ctx.guard("vector", [("sEV", ctx.ctr["sEV"])])
        ctx.emit("vector", [],
                 lambda e, i=i: e.tensor_tensor(w8[:, 0:8], sumg_all[1],
                                                nuv[:, i * 8:(i + 1) * 8], ALU.mult), [("sEV", 1)])
        ctx.guard("vector", [("sEV", ctx.ctr["sEV"])])
        ctx.emit("vector", [],
                 lambda e: e.tensor_reduce(w8[:, 9:10], w8[:, 0:8], AX.X, ALU.add), [("sEV", 1)])
        ctx.guard("vector", [("sEV", ctx.ctr["sEV"])])
        ev_z = ctx.ctr["sEV"]
        ctx.emit("scalar", [("sEV", ev_z)],
                 lambda e: e.activation(w8[:, 10:12], w8[:, 8:10], AF.Exp, scale=1.0 / GLN), [("sSC", 1)])
        sc_w = ctx.ctr["sSC"]
        ev_w = ctx.emit("vector", [("sSC", sc_w)],
                        lambda e, i=i: e.tensor_scalar_mul(w8[:, 12:14], w8[:, 10:12], muv[:, i:i + 1]),
                        [("sEV", 1)])["sEV"]
        ctx.guard("vector", [("sEV", ev_w)])
        for j in range(2):
            p = par[j]
            ctx.emit("vector", [],
                     lambda e, i=i, j=j, p=p:
                     e.scalar_tensor_tensor(sin[j * 2 + (1 - p)][:, :],
                                            gath[:, i * 132 + j * 64: i * 132 + (j + 1) * 64],
                                            w8[:, 12 + j:13 + j],
                                            sin[j * 2 + p][:, :], ALU.mult, ALU.add), [("sEV", 1)])
            ctx.guard("vector", [("sEV", ctx.ctr["sEV"])])
            par[j] = 1 - p
    sin_fin = [sin[0 + par[0]], sin[2 + par[1]]]
    if DEBUG:
        ctx.guard("scalar", ctx.barrier_waits())
        dma("scalar", "sVD0", ddbg["gath"][:, :], gath[:, :])
        dma("scalar", "sVD0", ddbg["w8"][:, :], w8[:, :])
        dma("scalar", "sVD0", ddbg["muv"][:, :], muv[:, :])

    comb_done = ctx.barrier_waits()

    for e in ("tensor", "vector", "gpsimd"):
        ctx.guard(e, comb_done)
    corr_mm_prev = 0
    for c in range(NCH):
        p = c % 2
        ctx.emit("vector", [("sMM", corr_mm_prev)],
                 lambda e, c=c, p=p:
                 e.tensor_tensor(apf(abv, 2432 + p * 512, [[256, 2], [64, 4], [1, 64]]),
                                 apf(hbv, QG0 + c * 64, [[SEGQ, 2], [seg, 4], [1, 64]]),
                                 apf(hbv, EGH0 + c * 64, [[seg, 2], [0, 4], [1, 64]]),
                                 ALU.mult), [("sEV", 1)])
        ev_q3 = ctx.ctr["sEV"]
        for j in range(2):
            if c == 0:
                ctx.emit("vector", [("sMM", corr_mm_prev)] if j == 0 else [],
                         lambda e, j=j: e.tensor_scalar_mul(stot[0][j][:, :], sin_fin[j][:, :],
                                                            eprefs[:, j: j + 1]), [("sEV", 1)])
            else:
                ctx.emit("vector", [("sMM", corr_mm_prev)] if j == 0 else [],
                         lambda e, j=j, c=c, p=p:
                         e.scalar_tensor_tensor(stot[p][j][:, :], sin_fin[j][:, :],
                                                eprefs[:, c * 2 + j: c * 2 + j + 1],
                                                sbf_all[:, (c - 1) * 128 + j * 64: (c - 1) * 128 + (j + 1) * 64],
                                                ALU.mult, ALU.add), [("sEV", 1)])
        ev_st = ctx.ctr["sEV"]
        for j in range(2):
            b = banks.take()
            w = banks.mm_waits(b) + ([("sEV", ev_st)] if j == 0 else [])
            for gh in range(2):
                tpc = (gh * 64, gh * 64) if gh else None
                ctx.emit("tensor", w if gh == 0 else [],
                         lambda e, b=b, j=j, gh=gh, tpc=tpc, p=p:
                         e.matmul(pb[b][gh * 64:(gh + 1) * 64, 0:256],
                                  stot[p][j][gh * 64:(gh + 1) * 64, :],
                                  qs[p][j][gh * 64:(gh + 1) * 64, :], start=True, stop=True,
                                  tile_position=tpc), [("sMM", 1)] if gh == 1 else [])
            banks.set_ready(b, "sMM", ctx.ctr["sMM"])
            corr_mm_prev = ctx.ctr["sMM"]
            dst = og[j].rearrange("p (hp t) -> p hp t", hp=4)[:, :, c * 64:(c + 1) * 64]
            ev = ctx.emit("vector", banks.evac_wait(b),
                          lambda e, b=b, dst=dst:
                          e.tensor_tensor(dst, dst, pb[b][:, 0:256].rearrange("p (hp t) -> p hp t", hp=4),
                                          ALU.add), [("sEV", 1)])["sEV"]
            banks.add_free(b, "sEV", ev)
    corr_done = ctx.barrier_waits()

    for e in ("tensor", "vector", "scalar"):
        ctx.guard(e, corr_done)
    prev_var_mm = 0
    prev_bc_mm = 0
    for j in range(2):
        for ti in range(4 * NT):
            sc_sq = ctx.emit("scalar", [("sMM", prev_var_mm)],
                             lambda e, j=j, ti=ti: e.activation(sqo[:, :], og[j][:, ti * 512:(ti + 1) * 512],
                                                                AF.Square), [("sSC", 1)])["sSC"]
            b = banks.take()
            w = banks.mm_waits(b) + [("sSC", sc_sq)]
            ctx.emit("tensor", w,
                     lambda e, b=b: e.matmul(pb[b][0:2, :], onesv[:, 1:3], sqo[:, :],
                                             start=True, stop=True), [("sMM", 1)])
            banks.set_ready(b, "sMM", ctx.ctr["sMM"])
            prev_var_mm = ctx.ctr["sMM"]
            ctx.emit("scalar", banks.evac_wait(b),
                     lambda e, b=b: e.activation(orow[0:2, 0:512], pb[b][0:2, :], AF.Ln,
                                                 scale=1.0 / 64.0, bias=cst[0:2, 1:2]), [("sSC", 1)])
            sc_ln = ctx.ctr["sSC"]
            banks.add_free(b, "sSC", sc_ln)
            ctx.emit("scalar", [("sMM", prev_bc_mm)],
                     lambda e: e.activation(orow[0:2, 512:1024], orow[0:2, 0:512], AF.Exp, scale=-0.5),
                     [("sSC", 1)])
            sc_r = ctx.ctr["sSC"]
            b2 = banks.take()
            w2 = banks.mm_waits(b2) + [("sSC", sc_r)]
            ctx.emit("tensor", w2,
                     lambda e, b2=b2: e.matmul(pb[b2][:, :], hbc[:, :], orow[0:2, 512:1024],
                                               start=True, stop=True), [("sMM", 1)])
            banks.set_ready(b2, "sMM", ctx.ctr["sMM"])
            prev_bc_mm = ctx.ctr["sMM"]
            ev = ctx.emit("vector", banks.evac_wait(b2),
                          lambda e, j=j, ti=ti, b2=b2:
                          e.scalar_tensor_tensor(og[j][:, ti * 512:(ti + 1) * 512],
                                                 og[j][:, ti * 512:(ti + 1) * 512],
                                                 scl[:, 136:137], pb[b2][:, :],
                                                 ALU.mult, ALU.mult), [("sEV", 1)])["sEV"]
            banks.add_free(b2, "sEV", ev)
    onorm_done = ctx.barrier_waits()

    for e in ("tensor", "vector", "gpsimd"):
        ctx.guard(e, onorm_done)
    xw_free = [[], []]
    yw_free = [[], []]
    for mb in range(8):
        for tt in range(NT):
            sl = (mb * NT + tt) % 2
            sem = "sXD%d" % sl
            xd = dma("gpsimd", sem, xwo[sl][:, :], dx[mb * 128:(mb + 1) * 128, tt * 512:(tt + 1) * 512],
                     waits=xw_free[sl])[sem]
            b = banks.take()
            parts = []
            for kc in range(8):
                j, hp = divmod(kc, 4)
                parts.append((wo_b[:, kc * 1024 + mb * 128: kc * 1024 + (mb + 1) * 128],
                              og[j][:, hp * seg + tt * 512: hp * seg + tt * 512 + 512]))
            mm_group(parts, b)
            ev = ctx.emit("vector", banks.evac_wait(b) + [(sem, xd)] + yw_free[sl],
                          lambda e, sl=sl, b=b: e.tensor_tensor(ysb[sl][:, :], xwo[sl][:, :], pb[b][:, :],
                                                                ALU.add), [("sEV", 1)])["sEV"]
            banks.add_free(b, "sEV", ev)
            xw_free[sl] = [("sEV", ev)]
            vsem = "sVD%d" % sl
            st = ctx.emit("scalar", [("sEV", ev)],
                          lambda e, sl=sl, mb=mb, tt=tt:
                          e.dma_start(out=dresid[mb * 128:(mb + 1) * 128, tt * 512:(tt + 1) * 512],
                                      in_=ysb[sl][:, :]), [(vsem, 16)], is_dma=True)[vsem]
            yw_free[sl] = [(vsem, st)]
    if DEBUG:
        ctx.guard("scalar", ctx.barrier_waits())
        dma("scalar", "sVD0", ddbg["resid"][:, :], dresid[:, :])
    wo_done = ctx.barrier_waits()
    resid_all_st = [("sVD0", ctx.ctr["sVD0"]), ("sVD1", ctx.ctr["sVD1"])]

    for e in ENGINES:
        ctx.guard(e, wo_done)

    wbf_free = [[], []]
    wbf_ready = [None, None]
    rs_free = [[], []]
    ysb_free = [list(resid_all_st), list(resid_all_st)]
    silu_free = [[], []]
    n_y = 0

    def stream_weight(dram, mb, nquart, slot):
        sem = "sWD%d" % slot
        wd = dma("gpsimd", sem, wbf[slot][:, 0:dram[mb].shape[1]], dram[mb],
                 waits=wbf_free[slot])[sem]
        wbf_free[slot] = []
        wbf_ready[slot] = (sem, wd)

    def stream_weight_pair(d1, d2, mb, slot):
        sem = "sWD%d" % slot
        dma("gpsimd", sem, wbf[slot][:, 0:1024], d1[mb], waits=wbf_free[slot])
        wd = dma("gpsimd", sem, wbf[slot][:, 1024:2048], d2[mb])[sem]
        wbf_free[slot] = []
        wbf_ready[slot] = (sem, wd)

    for p in range(2):
        tok0 = p * TP
        if p == 1:
            pass2 = [("sMM", ctx.ctr["sMM"])]
            ctx.guard("vector", pass2)
            ctx.guard("scalar", pass2)
        stA_ev = []
        stream_weight_pair(dw1, dw2, 0, 0)
        for mb in range(32):
            slot = mb % 2
            if mb + 1 < 32:
                stream_weight_pair(dw1, dw2, mb + 1, 1 - slot)
            last_mm_for_slot = None
            for t2 in range(NT2):
                bA = banks.take()
                wA = banks.mm_waits(bA) + ([wbf_ready[slot]] if t2 == 0 else [])
                for kc in range(8):
                    ctx.emit("tensor", wA if kc == 0 else [],
                             lambda e, bA=bA, kc=kc, t2=t2, slot=slot, tok0=tok0, st=(kc == 0):
                             e.matmul(pb[bA][:, :], wbf[slot][:, kc * 128:(kc + 1) * 128],
                                      xn[:, kc * seg + tok0 + t2 * 512: kc * seg + tok0 + t2 * 512 + 512],
                                      start=st, stop=(kc == 7)), [("sMM", 1)] if kc == 7 else [])
                banks.set_ready(bA, "sMM", ctx.ctr["sMM"])
                bB = banks.take()
                wB = banks.mm_waits(bB)
                for kc in range(8):
                    ctx.emit("tensor", wB if kc == 0 else [],
                             lambda e, bB=bB, kc=kc, t2=t2, slot=slot, tok0=tok0, st=(kc == 0):
                             e.matmul(pb[bB][:, :], wbf[slot][:, 1024 + kc * 128: 1024 + (kc + 1) * 128],
                                      xn[:, kc * seg + tok0 + t2 * 512: kc * seg + tok0 + t2 * 512 + 512],
                                      start=st, stop=(kc == 7)), [("sMM", 1)] if kc == 7 else [])
                banks.set_ready(bB, "sMM", ctx.ctr["sMM"])
                last_mm_for_slot = ctx.ctr["sMM"]
                ssl = (mb * NT2 + t2) % 2
                sc_si = ctx.emit("scalar", banks.evac_wait(bA) + silu_free[ssl],
                                 lambda e, bA=bA, mb=mb, ssl=ssl:
                                 e.activation(silu_s[ssl][:, :], pb[bA][:, :], AF.Silu,
                                              scale=scl[:, mb:mb + 1]), [("sSC", 1)])["sSC"]
                banks.add_free(bA, "sSC", sc_si)
                ev = ctx.emit("vector", banks.evac_wait(bB) + [("sSC", sc_si)],
                              lambda e, bB=bB, mb=mb, t2=t2, ssl=ssl:
                              e.scalar_tensor_tensor(ha[:, mb * TP + t2 * 512: mb * TP + t2 * 512 + 512],
                                                     pb[bB][:, :], scl[:, 32 + mb: 33 + mb],
                                                     silu_s[ssl][:, :], ALU.mult, ALU.mult),
                              [("sEV", 1)])["sEV"]
                banks.add_free(bB, "sEV", ev)
                silu_free[ssl] = [("sEV", ev)]
                stA_ev.append(ev)
            wbf_free[slot] = [("sMM", last_mm_for_slot)]
        evA = ctx.ctr["sEV"]
        stream_weight(dh1, 0, 4, 0)
        for mb in range(32):
            slot = mb % 2
            if mb + 1 < 32:
                stream_weight(dh1, mb + 1, 4, 1 - slot)
            for t2 in range(NT2):
                b = banks.take()
                w = banks.mm_waits(b) + ([wbf_ready[slot], ("sEV", evA)] if t2 == 0 else [])
                for kc in range(32):
                    ctx.emit("tensor", w if kc == 0 else [],
                             lambda e, b=b, kc=kc, t2=t2, slot=slot, st=(kc == 0):
                             e.matmul(pb[b][:, :], wbf[slot][:, kc * 128:(kc + 1) * 128],
                                      ha[:, kc * TP + t2 * 512: kc * TP + t2 * 512 + 512],
                                      start=st, stop=(kc == 31)), [("sMM", 1)] if kc == 31 else [])
                banks.set_ready(b, "sMM", ctx.ctr["sMM"])
                last_mm_for_slot = ctx.ctr["sMM"]
                sc_ = ctx.emit("scalar", banks.evac_wait(b),
                               lambda e, b=b, mb=mb, t2=t2:
                               e.activation(hb[:, mb * TP + t2 * 512: mb * TP + t2 * 512 + 512],
                                            pb[b][:, :], AF.Gelu, scale=scl[:, 64 + mb: 65 + mb]),
                               [("sSC", 1)])["sSC"]
                banks.add_free(b, "sSC", sc_)
            wbf_free[slot] = [("sMM", last_mm_for_slot)]
        scB = ctx.ctr["sSC"]
        stream_weight(dh2, 0, 4, 0)
        for mb in range(32):
            slot = mb % 2
            if mb + 1 < 32:
                stream_weight(dh2, mb + 1, 4, 1 - slot)
            for t2 in range(NT2):
                b = banks.take()
                w = banks.mm_waits(b) + ([wbf_ready[slot], ("sSC", scB)] if t2 == 0 else [])
                for kc in range(32):
                    ctx.emit("tensor", w if kc == 0 else [],
                             lambda e, b=b, kc=kc, t2=t2, slot=slot, st=(kc == 0):
                             e.matmul(pb[b][:, :], wbf[slot][:, kc * 128:(kc + 1) * 128],
                                      hb[:, kc * TP + t2 * 512: kc * TP + t2 * 512 + 512],
                                      start=st, stop=(kc == 31)), [("sMM", 1)] if kc == 31 else [])
                banks.set_ready(b, "sMM", ctx.ctr["sMM"])
                last_mm_for_slot = ctx.ctr["sMM"]
                sc_ = ctx.emit("scalar", banks.evac_wait(b),
                               lambda e, b=b, mb=mb, t2=t2:
                               e.activation(ha[:, mb * TP + t2 * 512: mb * TP + t2 * 512 + 512],
                                            pb[b][:, :], AF.Gelu, scale=scl[:, 96 + mb: 97 + mb]),
                               [("sSC", 1)])["sSC"]
                banks.add_free(b, "sSC", sc_)
            wbf_free[slot] = [("sMM", last_mm_for_slot)]
        scC = ctx.ctr["sSC"]
        stream_weight(dw3, 0, 4, 0)
        for mb in range(8):
            slot = mb % 2
            if mb + 1 < 8:
                stream_weight(dw3, mb + 1, 4, 1 - slot)
            for t2 in range(NT2):
                rsl = (mb * NT2 + t2) % 2
                rsem = "sXD%d" % rsl
                xd = dma("gpsimd", rsem, rstg[rsl][:, :],
                         dresid[mb * 128:(mb + 1) * 128, tok0 + t2 * 512: tok0 + t2 * 512 + 512],
                         waits=rs_free[rsl] + resid_all_st)[rsem]
                b = banks.take()
                w = banks.mm_waits(b) + ([wbf_ready[slot], ("sSC", scC)] if t2 == 0 else [])
                for kc in range(32):
                    ctx.emit("tensor", w if kc == 0 else [],
                             lambda e, b=b, kc=kc, t2=t2, slot=slot, st=(kc == 0):
                             e.matmul(pb[b][:, :], wbf[slot][:, kc * 128:(kc + 1) * 128],
                                      ha[:, kc * TP + t2 * 512: kc * TP + t2 * 512 + 512],
                                      start=st, stop=(kc == 31)), [("sMM", 1)] if kc == 31 else [])
                banks.set_ready(b, "sMM", ctx.ctr["sMM"])
                last_mm_for_slot = ctx.ctr["sMM"]
                ysl = (mb * NT2 + t2) % 2
                ev = ctx.emit("vector", banks.evac_wait(b) + [(rsem, xd)] + ysb_free[ysl],
                              lambda e, b=b, mb=mb, rsl=rsl, ysl=ysl:
                              e.scalar_tensor_tensor(ysb[ysl][:, :], pb[b][:, :],
                                                     scl[:, 128 + mb: 129 + mb], rstg[rsl][:, :],
                                                     ALU.mult, ALU.add), [("sEV", 1)])["sEV"]
                banks.add_free(b, "sEV", ev)
                rs_free[rsl] = [("sEV", ev)]
                ysem = "sVD%d" % ysl
                st = ctx.emit("scalar", [("sEV", ev)],
                              lambda e, ysl=ysl, mb=mb, t2=t2, tok0=tok0:
                              e.dma_start(out=dy[mb * 128:(mb + 1) * 128, tok0 + t2 * 512: tok0 + t2 * 512 + 512],
                                          in_=ysb[ysl][:, :]), [(ysem, 16)], is_dma=True)[ysem]
                n_y += 1
                ysb_free[ysl] = [("sEV", ev), (ysem, st)]
            wbf_free[slot] = [("sMM", last_mm_for_slot)]

    ctx.guard("scalar", [("sVD0", ctx.ctr["sVD0"]), ("sVD1", ctx.ctr["sVD1"])])

    with nc.Block() as blk:
        @blk.sync
        def _(e):
            for fn in ctx.q["sync"]:
                fn(e)

        @blk.gpsimd
        def _(e):
            for fn in ctx.q["gpsimd"]:
                fn(e)

        @blk.tensor
        def _(e):
            for fn in ctx.q["tensor"]:
                fn(e)

        @blk.vector
        def _(e):
            for fn in ctx.q["vector"]:
                fn(e)

        @blk.scalar
        def _(e):
            for fn in ctx.q["scalar"]:
                fn(e)

    es.close()
    return nc


def _prep_inputs(inputs, seg=2048):
    f = np.ascontiguousarray
    x = inputs["L0_attn_hidden_states"]
    Wq = inputs["L0_attn_Wq"]; Wk = inputs["L0_attn_Wk"]; Wv = inputs["L0_attn_Wv"]
    Wo = inputs["L0_attn_Wo"]; Wg1 = inputs["L0_attn_Wg1"]; Wg2 = inputs["L0_attn_Wg2"]
    bg2 = inputs["L0_attn_bg2"]; gnw = inputs["L0_attn_g_norm_weight"]
    w1 = inputs["L0_ffn_w1_w"]; s1 = inputs["L0_ffn_w1_s"]
    w2 = inputs["L0_ffn_w2_w"]; s2 = inputs["L0_ffn_w2_s"]
    h1 = inputs["L0_ffn_h1_w"]; hs1 = inputs["L0_ffn_h1_s"]
    h2 = inputs["L0_ffn_h2_w"]; hs2 = inputs["L0_ffn_h2_s"]
    w3 = inputs["L0_ffn_w3_w"]; s3 = inputs["L0_ffn_w3_s"]

    perm = []
    for j in range(2):
        for hp in range(4):
            for habs in (8 * j + hp, 8 * j + 4 + hp):
                perm.extend(range(habs * 64, habs * 64 + 64))
    perm = np.array(perm)

    wq_p = f(Wq[:, perm].reshape(8, 128, 1024))
    wo_p = f(Wo[perm, :].reshape(8, 128, 1024))
    wk_p = f(Wk.reshape(8, 128, 256))
    wv_p = f(Wv.reshape(8, 128, 256))
    wg1_p = f(Wg1.reshape(8, 128, 16))
    wg2a = f(np.vstack([Wg2, bg2[None, :]]))

    def tiles(wt, nmb):
        IN, OUT = wt.shape
        nkc = IN // 128
        return f(wt.reshape(nkc, 128, nmb, 128).transpose(2, 1, 0, 3).reshape(nmb, 128, nkc * 128))

    w1t = tiles(w1.T, 32)
    w2t = tiles(w2.T, 32)
    h1t = tiles(h1.T, 32)
    h2t = tiles(h2.T, 32)
    w3t = tiles(w3.T, 8)

    scl = np.zeros((128, 137), np.float32)
    scl[:, 0:32] = s1.reshape(32, 128).T
    scl[:, 32:64] = s2.reshape(32, 128).T
    scl[:, 64:96] = hs1.reshape(32, 128).T
    scl[:, 96:128] = hs2.reshape(32, 128).T
    scl[:, 128:136] = s3.reshape(8, 128).T
    scl[:, 136] = np.tile(gnw, 2)
    scl = f(scl)

    cstv = np.zeros((128, 8), np.float32)
    cstv[:, 0] = 1e-5
    cstv[:, 1] = 1e-6
    cstv[:, 2] = 1.0
    cstv = f(cstv)

    sidx = np.arange(C)
    m_incl = (sidx[:, None] <= sidx[None, :]).astype(np.float32)
    m_after = (sidx[:, None] > sidx[None, :]).astype(np.float32)
    z64 = np.zeros((C, C), np.float32)
    mcat = np.zeros((128, 256), np.float32)
    mcat[0:64, 0:64] = m_incl
    mcat[0:64, 64:128] = m_after
    mcat[64:128, 128:192] = m_incl
    mcat[64:128, 192:256] = m_after
    mcat = f(mcat)
    maskP0 = np.zeros((128, 256), np.float32)
    maskP0[0:64, :] = np.tile(m_incl, (1, 4))
    maskP1 = np.zeros((128, 256), np.float32)
    maskP1[64:128, :] = np.tile(m_incl, (1, 4))
    maskP0 = f(maskP0.astype(BF16))
    maskP1 = f(maskP1.astype(BF16))
    ident = f(np.eye(128, dtype=np.float32).astype(BF16))
    onesv = np.zeros((128, 4), np.float32)
    onesv[:, 0] = 1.0
    onesv[:64, 1] = 1.0
    onesv[64:, 2] = 1.0
    onesv = f(onesv)
    onesr = np.zeros((2, 128), np.float32)
    onesr[0, :] = 1.0
    onesr = f(onesr)
    hbcm = np.zeros((2, 128), np.float32)
    hbcm[0, 0:64] = 1.0
    hbcm[1, 64:128] = 1.0
    hbcm = f(hbcm)

    onesbv = np.ones((128, 1), BF16)
    shared = dict(onesb=onesbv,
                  wq=wq_p, wk=wk_p, wv=wv_p, wg1=wg1_p, wg2a=wg2a, wo=wo_p,
                  w1t=w1t, w2t=w2t, h1t=h1t, h2t=h2t, w3t=w3t,
                  scl=scl, cst=cstv, mcat=mcat, maskP0=maskP0, maskP1=maskP1, ident=ident,
                  onesv=onesv, onesr=onesr, hbc=hbcm)

    nseg = x.shape[1] // seg
    in_maps = []
    for core in range(NCORE):
        b, sgi = divmod(core, nseg)
        m = dict(shared)
        m["x"] = f(x[b, sgi * seg:(sgi + 1) * seg, :].T)
        mu = np.zeros(8, np.float32)
        nu = np.zeros((8, 8), np.float32)
        for i in range(8):
            if i // nseg == b and i < core:
                mu[i] = 1.0
                for l in range(8):
                    if l // nseg == b and i < l < core:
                        nu[i, l] = 1.0
        m["muv"] = f(np.tile(mu[None, :], (128, 1)))
        m["nuv"] = f(np.tile(nu.reshape(1, 64), (128, 1)))
        in_maps.append(m)
    return in_maps


def kernel(**inputs):
    seg = 2048
    in_maps = _prep_inputs(inputs, seg)
    nc = build_program(seg)
    res = run_bass_kernel_spmd(nc, in_maps, core_ids=list(range(NCORE)))
    out = np.empty((2, 8192, 1024), np.float32)
    nseg = 8192 // seg
    for core in range(NCORE):
        b, sgi = divmod(core, nseg)
        out[b, sgi * seg:(sgi + 1) * seg, :] = res.results[core]["y"].T
    return out


# version 19
# speedup vs baseline: 1.0901x; 1.0579x over previous
import contextlib
import numpy as np
import ml_dtypes

import concourse.bass as bass
import concourse.mybir as mybir
from concourse.bass_utils import run_bass_kernel_spmd

BF16 = ml_dtypes.bfloat16
F32 = mybir.dt.float32
BF = mybir.dt.bfloat16
AF = mybir.ActivationFunctionType
ALU = mybir.AluOpType
AX = mybir.AxisListType

NCORE = 8
D_MODEL = 1024
D_HID = 4096
H, HKV, D = 16, 4, 64
C = 64
GLN = 16.0

ENGINES = ("sync", "gpsimd", "tensor", "vector", "scalar")


SELF_EDGE_ENGINES = {"vector": "sEV", "scalar": "sSC", "gpsimd": "sCS"}


class Ctx:

    def __init__(self, self_edges=False):
        self.q = {e: [] for e in ENGINES}
        self.ctr = {s: 0 for s in ("sWD0", "sWD1", "sCN", "sXD0", "sXD1", "sVD0", "sVD1",
                                   "sCS", "sMM", "sSC", "sEV", "sCC")}
        self.sems = {}
        self.self_edges = self_edges

    def emit(self, eng, waits, build, incs, is_dma=False):
        waits = [(s, v) for (s, v) in waits if v > 0]
        incs = list(incs)
        if self.self_edges and eng in SELF_EDGE_ENGINES and not is_dma:
            sname = SELF_EDGE_ENGINES[eng]
            assert any(s == sname for s, _ in incs), (eng, incs)
            if self.ctr[sname] > 0:
                waits = waits + [(sname, self.ctr[sname])]
        for s, a in incs:
            self.ctr[s] += a
        sems = self.sems

        def fn(e, waits=tuple(waits), incs=tuple(incs), build=build):
            for s, v in waits:
                e.wait_ge(sems[s], v)
            ins = build(e)
            for s, a in incs:
                ins = ins.then_inc(sems[s], a)
        self.q[eng].append(fn)
        return {s: self.ctr[s] for s, _ in incs}

    def barrier_waits(self):
        return [(s, self.ctr[s]) for s in ("sMM", "sEV", "sSC", "sCS")]

    def guard(self, eng, waits):
        sems = self.sems

        def fn(e, waits=tuple([(s, v) for s, v in waits if v > 0])):
            for s, v in waits:
                e.wait_ge(sems[s], v)
        self.q[eng].append(fn)


class Banks:

    def __init__(self, ctx, tensors):
        self.ctx = ctx
        self.t = tensors
        self.free_w = [[] for _ in tensors]
        self.ready = [None] * len(tensors)
        self.in_use = [False] * len(tensors)
        self.rr = 0

    def take(self):
        for _ in range(len(self.t)):
            i = self.rr % len(self.t)
            self.rr += 1
            if not self.in_use[i]:
                self.in_use[i] = True
                return i
        raise RuntimeError("PSUM bank pool exhausted: all banks in use")

    def mm_waits(self, i):
        w = self.free_w[i]
        self.free_w[i] = []
        return w

    def set_ready(self, i, sem, val):
        self.ready[i] = (sem, val)

    def evac_wait(self, i):
        return [self.ready[i]] if self.ready[i] else []

    def add_free(self, i, sem, val):
        self.free_w[i].append((sem, val))
        self.in_use[i] = False


DEBUG = False
SIM_SELF_EDGES = True

def build_program(seg=2048):
    assert seg % 128 == 0
    NT = seg // 512
    NTT = seg // 128
    NCH = seg // C
    TP = seg // 2
    NT2 = TP // 512

    nc = bass.Bass("TRN2", target_bir_lowering=False)

    dx = nc.dram_tensor("x", [D_MODEL, seg], F32, kind="ExternalInput").ap()
    dwq = nc.dram_tensor("wq", [8, 128, 1024], F32, kind="ExternalInput").ap()
    dwk = nc.dram_tensor("wk", [8, 128, 256], F32, kind="ExternalInput").ap()
    dwv = nc.dram_tensor("wv", [8, 128, 256], F32, kind="ExternalInput").ap()
    dwg1 = nc.dram_tensor("wg1", [8, 128, 16], F32, kind="ExternalInput").ap()
    dwg2a = nc.dram_tensor("wg2a", [17, 256], F32, kind="ExternalInput").ap()
    dwo = nc.dram_tensor("wo", [8, 128, 1024], F32, kind="ExternalInput").ap()
    dw1 = nc.dram_tensor("w1t", [32, 128, 1024], F32, kind="ExternalInput").ap()
    dw2 = nc.dram_tensor("w2t", [32, 128, 1024], F32, kind="ExternalInput").ap()
    dh1 = nc.dram_tensor("h1t", [32, 128, 4096], F32, kind="ExternalInput").ap()
    dh2 = nc.dram_tensor("h2t", [32, 128, 4096], F32, kind="ExternalInput").ap()
    dw3 = nc.dram_tensor("w3t", [8, 128, 4096], F32, kind="ExternalInput").ap()
    dsc = nc.dram_tensor("scl", [128, 137], F32, kind="ExternalInput").ap()
    dcst = nc.dram_tensor("cst", [128, 8], F32, kind="ExternalInput").ap()
    dmcat = nc.dram_tensor("mcat", [128, 256], F32, kind="ExternalInput").ap()
    dmaskP0 = nc.dram_tensor("maskP0", [128, 256], BF, kind="ExternalInput").ap()
    dmaskP1 = nc.dram_tensor("maskP1", [128, 256], BF, kind="ExternalInput").ap()
    dident = nc.dram_tensor("ident", [128, 128], BF, kind="ExternalInput").ap()
    dones = nc.dram_tensor("onesv", [128, 4], F32, kind="ExternalInput").ap()
    donesb = nc.dram_tensor("onesb", [128, 1], BF, kind="ExternalInput").ap()
    donesr = nc.dram_tensor("onesr", [2, 128], F32, kind="ExternalInput").ap()
    dhbc = nc.dram_tensor("hbc", [2, 128], F32, kind="ExternalInput").ap()
    dmu = nc.dram_tensor("muv", [128, 8], F32, kind="ExternalInput").ap()
    dnu = nc.dram_tensor("nuv", [128, 64], F32, kind="ExternalInput").ap()

    dy = nc.dram_tensor("y", [D_MODEL, seg], F32, kind="ExternalOutput").ap()
    if DEBUG:
        ddbg = {n: nc.dram_tensor("dbg_" + n, s, d, kind="ExternalOutput").ap()
                for n, s, d in (
                    ("xn", [128, 8 * seg], BF), ("qg0", [128, 4 * seg], BF),
                    ("kfm", [128, 2 * seg], BF), ("vtm", [128, 2 * seg], BF),
                    ("g1aug", [32, seg], BF), ("og0", [128, 4 * seg], BF),
                    ("og1", [128, 4 * seg], BF),
                    ("sf", [128, 128], F32), ("eGg0", [128, seg], BF),
                    ("gath", [128, 8 * 132], F32), ("sin", [128, 128], BF),
                    ("g1e", [16, 2048], F32), ("g1a2", [32, seg], BF),
                    ("w8", [128, 16], F32), ("muv", [128, 8], F32),
                    ("w8h", [128, 16], F32), ("qg1", [128, 4 * seg], BF),
                    ("ch5", [128, 1152], F32),
                    ("resid", [D_MODEL, seg], F32), ("gtm0", [128, 256], F32))}

    dresid = nc.dram_tensor("resid", [D_MODEL, seg], F32, kind="Internal").ap()
    dccin = nc.dram_tensor("ccin", [128, 132], F32, kind="Internal").ap()
    dccout = nc.dram_tensor("ccout", [NCORE * 128, 132], F32, kind="Internal").ap()

    ctx = Ctx(self_edges=SIM_SELF_EDGES)
    es = contextlib.ExitStack()
    sb = lambda name, shape, dt: es.enter_context(nc.sbuf_tensor(name, shape, dt))

    xn = sb("xn", [128, 8 * seg], BF)
    SEGQ = seg * 4
    HW = max(32 * TP, 8 * 1024 + 2 * SEGQ + 2 * seg + 4096)
    ha = sb("ha", [128, HW], BF)
    hb = sb("hb", [128, HW], BF)
    wo_b = ha[:, 0:8 * 1024]
    og = [ha[:, 8 * 1024 + j * SEGQ: 8 * 1024 + (j + 1) * SEGQ] for j in range(2)]
    kfm = ha[:, 8 * 1024 + 2 * SEGQ: 8 * 1024 + 2 * SEGQ + 2 * seg]
    eGg = [ha[:, 8 * 1024 + 2 * SEGQ + 2 * seg + j * seg:
              8 * 1024 + 2 * SEGQ + 2 * seg + (j + 1) * seg] for j in range(2)]
    wq_b = hb[:, 0:8 * 1024]
    qg = [hb[:, 8 * 1024 + j * SEGQ: 8 * 1024 + (j + 1) * SEGQ] for j in range(2)]
    vtm = hb[:, 8 * 1024 + 2 * SEGQ: 8 * 1024 + 2 * SEGQ + 2 * seg]
    wk_b = hb[:, 8 * 1024 + 2 * SEGQ + 2 * seg: 8 * 1024 + 2 * SEGQ + 2 * seg + 2048]
    wv_b = hb[:, 8 * 1024 + 2 * SEGQ + 2 * seg + 2048: 8 * 1024 + 2 * SEGQ + 2 * seg + 4096]
    sbf_all = hb[:, 0:NCH * 128]
    EGH0 = NCH * 128
    eGhat = [hb[:, EGH0 + j * seg: EGH0 + (j + 1) * seg] for j in range(2)]

    af = sb("af", [128, 5120], F32)
    wst = [af[:, 0:1024], af[:, 1024:2048]]
    xstg = [af[:, 2048:2560], af[:, 2560:3072]]
    sqscr = af[:, 3072:3584]
    rowbuf = af[:, 3584:5120]
    gtm = [af[:, 2048 + i * 256: 2048 + (i + 1) * 256] for i in range(4)]
    gscr = [af[:, 3072 + i * 256: 3072 + (i + 1) * 256] for i in range(2)]
    exg = [[af[:, 3584 + p * 384 + i * 64: 3584 + p * 384 + (i + 1) * 64] for i in range(6)]
           for p in range(2)]
    pref = af[:, 4352:4360]
    sf32 = [af[:, 4360 + j * 64: 4360 + (j + 1) * 64] for j in range(2)]
    eprefs = af[:, 4488:4552]
    decays = af[:, 4552:4616]
    blob = af[:, 0:132]
    gath = af[:, 132:1188]
    sin = [af[:, 1188 + i * 64: 1188 + (i + 1) * 64] for i in range(4)]
    w8 = af[:, 1444:1460]
    sqo = af[:, 2048:2560]
    orow = af[:, 2560:3584]
    xwo = [af[:, 4096 + i * 512: 4096 + (i + 1) * 512] for i in range(2)]
    silu_s = [af[:, 2048 + i * 512: 2048 + (i + 1) * 512] for i in range(2)]
    ysb = [af[:, 3072 + i * 512: 3072 + (i + 1) * 512] for i in range(2)]
    rstg = [af[:, 4096 + i * 512: 4096 + (i + 1) * 512] for i in range(2)]

    ab = sb("ab", [128, 8192], BF)
    wbf = [ab[:, 0:4096], ab[:, 4096:8192]]
    g1aug = ab[0:32, 0:seg]
    wg1_b = ab[:, 2048:2176]
    wg2a_b = ab[0:32, 2176:2432]
    qs = [[ab[:, 2432 + p * 512 + j * 256: 2432 + p * 512 + (j + 1) * 256] for j in range(2)]
          for p in range(2)]
    ks = [[ab[:, 3456 + p * 256 + j * 128: 3456 + p * 256 + (j + 1) * 128] for j in range(2)]
          for p in range(2)]
    kppf = [[ab[:, 3968 + p * 256 + j * 128: 3968 + p * 256 + (j + 1) * 128] for j in range(2)]
            for p in range(2)]
    kppt = [[ab[:, 4480 + p * 256 + j * 128: 4480 + p * 256 + (j + 1) * 128] for j in range(2)]
            for p in range(2)]
    pscr = [[ab[:, 4992 + p * 1024 + g * 256: 4992 + p * 1024 + (g + 1) * 256] for g in range(4)]
            for p in range(2)]
    stot = [[ab[:, 7168 + p * 128 + j * 64: 7168 + p * 128 + (j + 1) * 64] for j in range(2)]
            for p in range(2)]
    maskPh = [ab[:, 7424 + h * 256: 7424 + (h + 1) * 256] for h in range(2)]

    scl = sb("scl_s", [128, 137], F32)
    cst = sb("cst_s", [128, 8], F32)
    mcat2 = sb("mcat2_s", [128, 256], F32)
    ident = sb("ident_s", [128, 128], BF)
    onesv = sb("onesv_s", [128, 4], F32)
    onesb = sb("onesb_s", [128, 1], BF)
    onesr = sb("onesr_s", [2, 128], F32)
    hbc = sb("hbc_s", [2, 128], F32)
    muv = sb("muv_s", [128, 8], F32)
    nuv = sb("nuv_s", [128, 64], F32)

    pb = [es.enter_context(nc.psum_tensor(f"p{i}", [128, 512], F32)) for i in range(6)]
    pTs = [es.enter_context(nc.psum_tensor(f"pT{i}", [128, 1024], BF)) for i in range(2)]

    for s in ("sWD0", "sWD1", "sCN", "sXD0", "sXD1", "sVD0", "sVD1",
              "sCS", "sMM", "sSC", "sEV", "sCC"):
        ctx.sems[s] = es.enter_context(nc.semaphore(s))

    banks = Banks(ctx, pb[:4])

    def dma(eng, sem, out, in_, waits=()):
        return ctx.emit(eng, list(waits), lambda e, o=out, i=in_: e.dma_start(out=o, in_=i), [(sem, 16)],
                        is_dma=True)

    def mm_group(parts, bank_i, extra_waits=(), region=None):
        w = banks.mm_waits(bank_i) + list(extra_waits)
        n = len(parts)
        for idx, (lh, rh) in enumerate(parts):
            incs = [("sMM", 1)] if idx == n - 1 else []
            out = region if region is not None else pb[bank_i][:, :]
            ctx.emit("tensor", w if idx == 0 else [],
                     lambda e, o=out, l=lh, r=rh, st=(idx == 0), sp=(idx == n - 1):
                     e.matmul(o, l, r, start=st, stop=sp), incs)
        banks.set_ready(bank_i, "sMM", ctx.ctr["sMM"])
        return ctx.ctr["sMM"]

    for out, i in ((scl, dsc), (cst, dcst), (mcat2, dmcat), (maskPh[0], dmaskP0),
                   (maskPh[1], dmaskP1), (ident, dident), (onesv, dones), (onesb, donesb),
                   (onesr, donesr), (hbc, dhbc), (muv, dmu), (nuv, dnu)):
        dma("sync", "sCN", out[:, :], i[:, :])
    consts_wd = ctx.ctr["sCN"]

    for kc in range(8):
        dma("gpsimd", "sWD0", wq_b[:, kc * 1024:(kc + 1) * 1024], dwq[kc, :, :])
        dma("gpsimd", "sWD0", wo_b[:, kc * 1024:(kc + 1) * 1024], dwo[kc, :, :])
        dma("gpsimd", "sWD0", wk_b[:, kc * 256:(kc + 1) * 256], dwk[kc, :, :])
        dma("gpsimd", "sWD0", wv_b[:, kc * 256:(kc + 1) * 256], dwv[kc, :, :])
        dma("gpsimd", "sWD0", wg1_b[:, kc * 16:(kc + 1) * 16], dwg1[kc, :, :])
    dma("gpsimd", "sWD0", wg2a_b[0:17, :], dwg2a[:, :])
    attn_w_wd = ctx.ctr["sWD0"]

    ctx.emit("vector", [], lambda e: e.memset(g1aug[:, :], 1.0), [("sEV", 1)])
    g1_memset_ev = ctx.ctr["sEV"]

    xb = [ha[:, 8 * 1024 + q * 4096: 8 * 1024 + (q + 1) * 4096] for q in range(2)]
    sqs = [sqscr, af[:, 640:1152]]
    xb_free = [[], []]
    last_xd = [0, 0]
    xdmas = {}
    S1 = {}
    sq_mm_prev = [0, 0]
    prev_tt_mm = [0]
    rows_guard = [0]

    def p1_dma(tt):
        q = tt % 2
        xds = []
        for kc in range(8):
            sl = kc % 2
            sem = "sXD%d" % sl
            xd = dma("gpsimd", sem, xb[q][:, kc * 512:(kc + 1) * 512],
                     dx[kc * 128:(kc + 1) * 128, tt * 512:(tt + 1) * 512],
                     waits=[(sem, last_xd[sl])] + (xb_free[q] if kc < 2 else []) +
                           ([("sCN", consts_wd)] if (tt == 0 and kc == 0) else []))[sem]
            last_xd[sl] = xd
            xds.append((sem, xd))
        xdmas[tt] = xds

    def p1_stats(tt):
        q = tt % 2
        first = True
        for kc in range(8):
            sl = kc % 2
            sem, xd = xdmas[tt][kc]
            ctx.emit("tensor", [(sem, xd)] + ([("sSC", rows_guard[0])] if first else []),
                     lambda e, q=q, kc=kc, st=first, sp=(kc == 7):
                     e.matmul(pb[4][0:1, :], onesb[:, 0:1], xb[q][:, kc * 512:(kc + 1) * 512],
                              start=st, stop=sp), [])
            sc_ = ctx.emit("scalar", [(sem, xd), ("sMM", sq_mm_prev[sl])] +
                           ([("sEV", S1[tt - 1]["p2ev"])] if (kc == 0 and tt >= 1) else []),
                           lambda e, q=q, kc=kc, sl=sl:
                           e.activation(sqs[sl][:, :], xb[q][:, kc * 512:(kc + 1) * 512], AF.Square),
                           [("sSC", 1)])["sSC"]
            mmres = ctx.emit("tensor", [("sSC", sc_)],
                             lambda e, sl=sl, st=first, sp=(kc == 7):
                             e.matmul(pb[5][0:1, :], onesv[:, 0:1], sqs[sl][:, :], start=st, stop=sp),
                             [("sMM", 1)])["sMM"]
            sq_mm_prev[sl] = mmres
            first = False
        S1[tt] = dict(mm_stats=ctx.ctr["sMM"])

    def p1_rows(tt):
        d = S1[tt]
        ctx.emit("scalar", [("sMM", d["mm_stats"])],
                 lambda e: e.mul(rowbuf[0:1, 0:512], pb[4][0:1, :], 1.0 / 1024.0), [("sSC", 1)])
        sc_mu = ctx.ctr["sSC"]
        ctx.emit("scalar", [],
                 lambda e: e.mul(rowbuf[0:1, 512:1024], pb[5][0:1, :], 1.0 / 1024.0), [("sSC", 1)])
        sc_ex2 = ctx.ctr["sSC"]
        rows_guard[0] = sc_ex2
        ctx.emit("vector", [("sSC", sc_ex2), ("sMM", prev_tt_mm[0])],
                 lambda e: e.tensor_mul(rowbuf[0:1, 1024:1536], rowbuf[0:1, 0:512], rowbuf[0:1, 0:512]),
                 [("sEV", 1)])
        ctx.emit("vector", [],
                 lambda e: e.tensor_sub(rowbuf[0:1, 512:1024], rowbuf[0:1, 512:1024], rowbuf[0:1, 1024:1536]),
                 [("sEV", 1)])
        ev_var = ctx.ctr["sEV"]
        ctx.emit("scalar", [("sEV", ev_var)],
                 lambda e: e.activation(rowbuf[0:1, 512:1024], rowbuf[0:1, 512:1024], AF.Ln,
                                        bias=cst[0:1, 0:1]), [("sSC", 1)])
        ctx.emit("scalar", [],
                 lambda e: e.activation(rowbuf[0:1, 1024:1536], rowbuf[0:1, 512:1024], AF.Exp, scale=-0.5),
                 [("sSC", 1)])
        d["sc_a"] = ctx.ctr["sSC"]
        ctx.emit("vector", [("sSC", d["sc_a"])],
                 lambda e: e.tensor_mul(rowbuf[0:1, 0:512], rowbuf[0:1, 0:512], rowbuf[0:1, 1024:1536]),
                 [("sEV", 1)])
        d["ev_b"] = ctx.ctr["sEV"]

    def p1_bcast(tt):
        d = S1[tt]
        bA2 = banks.take()
        d["mm_a"] = mm_group([(onesr[0:1, :], rowbuf[0:1, 1024:1536])], bA2, extra_waits=[("sSC", d["sc_a"])])
        bB2 = banks.take()
        d["mm_b"] = mm_group([(onesr[0:1, :], rowbuf[0:1, 0:512])], bB2, extra_waits=[("sEV", d["ev_b"])])
        d["bA2"], d["bB2"] = bA2, bB2
        prev_tt_mm[0] = d["mm_b"]

    def p1_pass2(tt):
        q = tt % 2
        d = S1[tt]
        bA2, bB2 = d["bA2"], d["bB2"]
        for kc in range(8):
            ctx.emit("vector", [("sMM", d["mm_a"])] if kc == 0 else [],
                     lambda e, q=q, kc=kc, bA2=bA2:
                     e.tensor_tensor(sqscr[:, :], xb[q][:, kc * 512:(kc + 1) * 512],
                                     pb[bA2][:, :], ALU.mult), [("sEV", 1)])
            ctx.emit("vector", [("sMM", d["mm_b"])] if kc == 0 else [],
                     lambda e, kc=kc, tt=tt, bB2=bB2:
                     e.tensor_tensor(xn[:, kc * seg + tt * 512: kc * seg + tt * 512 + 512],
                                     sqscr[:, :], pb[bB2][:, :], ALU.subtract), [("sEV", 1)])
        d["p2ev"] = ctx.ctr["sEV"]
        xb_free[q] = [("sEV", d["p2ev"])]
        banks.add_free(bA2, "sEV", ctx.ctr["sEV"])
        banks.add_free(bB2, "sEV", ctx.ctr["sEV"])

    def p1_proj(tt):
        pw = [("sEV", S1[tt]["p2ev"]), ("sWD0", attn_w_wd), ("sEV", g1_memset_ev)]
        for m in range(8):
            j, hp = divmod(m, 4)
            b = banks.take()
            mm_group([(wq_b[:, kc * 1024 + m * 128: kc * 1024 + (m + 1) * 128],
                       xn[:, kc * seg + tt * 512: kc * seg + (tt) * 512 + 512]) for kc in range(8)], b,
                     extra_waits=pw)
            sc_ = ctx.emit("scalar", banks.evac_wait(b),
                           lambda e, j=j, hp=hp, tt=tt, b=b:
                           e.activation(qg[j][:, hp * seg + tt * 512: hp * seg + tt * 512 + 512],
                                        pb[b][:, :], AF.Relu, scale=0.125), [("sSC", 1)])["sSC"]
            banks.add_free(b, "sSC", sc_)
        for m in range(2):
            b = banks.take()
            mm_group([(wk_b[:, kc * 256 + m * 128: kc * 256 + (m + 1) * 128],
                       xn[:, kc * seg + tt * 512: kc * seg + tt * 512 + 512]) for kc in range(8)], b,
                     extra_waits=pw)
            sc_ = ctx.emit("scalar", banks.evac_wait(b),
                           lambda e, m=m, tt=tt, b=b:
                           e.activation(kfm[:, m * seg + tt * 512: m * seg + tt * 512 + 512],
                                        pb[b][:, :], AF.Relu), [("sSC", 1)])["sSC"]
            banks.add_free(b, "sSC", sc_)
        for ti in range(4 * tt, 4 * tt + 4):
            b = banks.take()
            mm_group([(xn[:, kc * seg + ti * 128: kc * seg + (ti + 1) * 128],
                       wv_b[:, kc * 256:(kc + 1) * 256]) for kc in range(8)], b,
                     region=pb[b][:, 0:256], extra_waits=pw)
            ev = ctx.emit("vector", banks.evac_wait(b),
                          lambda e, ti=ti, b=b: e.tensor_copy(vtm[:, ti * 256:(ti + 1) * 256], pb[b][:, 0:256]),
                          [("sEV", 1)])["sEV"]
            banks.add_free(b, "sEV", ev)
        b = banks.take()
        mm_group([(wg1_b[:, kc * 16:(kc + 1) * 16],
                   xn[:, kc * seg + tt * 512: kc * seg + tt * 512 + 512]) for kc in range(8)], b,
                 region=pb[b][0:16, :], extra_waits=pw)
        ev = ctx.emit("vector", banks.evac_wait(b),
                      lambda e, tt=tt, b=b: e.tensor_copy(g1aug[0:16, tt * 512:(tt + 1) * 512], pb[b][0:16, :]),
                      [("sEV", 1)])["sEV"]
        banks.add_free(b, "sEV", ev)

    p1_dma(0)
    for i in range(-1, NT):
        if i + 1 < NT:
            p1_stats(i + 1)
        if i + 2 < NT:
            p1_dma(i + 2)
        if i >= 0:
            p1_proj(i)
        if i + 1 < NT:
            p1_rows(i + 1)
            p1_bcast(i + 1)
            p1_pass2(i + 1)
    if DEBUG:
        ctx.guard("scalar", ctx.barrier_waits())
        dma("scalar", "sVD0", ddbg["xn"][:, :], xn[:, :])
    ln_done = ctx.barrier_waits()

    proj_done = ctx.barrier_waits()


    for e in ("tensor", "vector", "scalar", "gpsimd"):
        ctx.guard(e, proj_done)

    for j in range(2):
        ctx.emit("vector", [], lambda e, j=j: e.memset(sf32[j][:, :], 0.0), [("sEV", 1)])
    for p in range(2):
        for j in range(2):
            ctx.emit("vector", [], lambda e, p=p, j=j: e.memset(ks[p][j][:, :], 0.0), [("sEV", 1)])
    ctx.emit("scalar", [], lambda e: e.memzero(pref[:, :]), [("sSC", 1)])
    init_ev = ctx.ctr["sEV"]
    init_sc = ctx.ctr["sSC"]
    ctx.guard("tensor", [("sEV", init_ev)])
    ctx.guard("scalar", [("sEV", init_ev)])

    gslot_free = [[] for _ in range(4)]
    gslot_ready = [None] * 4
    glog_emitted = 0
    prev_gate_ev = 0

    def emit_gate_ttile(t):
        nonlocal glog_emitted, prev_gate_ev
        sl = t % 4
        b = banks.take()
        mm_group([(g1aug[0:17, t * 128:(t + 1) * 128], wg2a_b[0:17, :])], b,
                 region=pb[b][:, 0:256])
        w0 = banks.evac_wait(b) + gslot_free[sl] + [("sEV", prev_gate_ev)]
        ctx.emit("scalar", w0, lambda e, b=b: e.activation(gscr[0][:, :], pb[b][:, 0:256], AF.Abs),
                 [("sSC", 1)])
        ctx.emit("scalar", [], lambda e: e.activation(gscr[1][:, :], gscr[0][:, :], AF.Exp, scale=-1.0),
                 [("sSC", 1)])
        ctx.emit("scalar", [], lambda e: e.activation(gscr[0][:, :], gscr[1][:, :], AF.Ln, bias=cst[:, 2:3]),
                 [("sSC", 1)])
        sc1 = ctx.ctr["sSC"]
        ev = ctx.emit("vector", [("sSC", sc1)] + banks.evac_wait(b) + gslot_free[sl],
                      lambda e, b=b: e.tensor_scalar_min(gscr[1][:, :], pb[b][:, 0:256], 0.0),
                      [("sEV", 1)])["sEV"]
        banks.add_free(b, "sEV", ev)
        ev2 = ctx.emit("vector", [],
                       lambda e, sl=sl: e.tensor_sub(gtm[sl][:, :], gscr[1][:, :], gscr[0][:, :]),
                       [("sEV", 1)])["sEV"]
        gslot_ready[sl] = ("sEV", ev2)
        prev_gate_ev = ev2
        glog_emitted = t + 1

    for t in range(min(3, NTT)):
        emit_gate_ttile(t)

    factor_ev = {}
    factor_sc = {}
    q_cs = {}
    mm_after_o = {}
    pT_free = [[], []]
    pp = 0
    pgbs = [pb[4], pb[5]]
    pg_free = [[("sSC", rows_guard[0])], [("sSC", rows_guard[0])]]
    pg_ready = [None, None]

    def apf(view, extra_off, dims):
        base = view[:, :] if hasattr(view, "tensor") is False else view
        return bass.AP(tensor=base.tensor, offset=base.offset + extra_off,
                       ap=[list(base.ap[0])] + [list(d) for d in dims])

    hbv = hb[:, :]
    hav = ha[:, :]
    abv = ab[:, :]
    afv = af[:, :]
    QG0 = 8 * 1024
    KFM0 = 8 * 1024 + 2 * SEGQ
    def emit_gate_mm(t):
        nonlocal glog_emitted
        while glog_emitted < min(t + 3, NTT):
            emit_gate_ttile(glog_emitted)
        tb = t % 2
        sl2 = t % 4
        pgb = pgbs[tb]
        for j in range(2):
            w = (pg_free[tb] if j == 0 else []) + [gslot_ready[sl2]]
            pg_free[tb] = []
            ctx.emit("tensor", w,
                     lambda e, j=j, sl2=sl2, pgb=pgb:
                     e.matmul(pgb[:, j * 256:(j + 1) * 256],
                              gtm[sl2][:, j * 128:(j + 1) * 128],
                              mcat2[:, :], start=True, stop=True), [("sMM", 1)])
        pg_ready_t[t] = ("sMM", ctx.ctr["sMM"])
        gslot_free[sl2] = [("sMM", ctx.ctr["sMM"])]

    def emit_pre_sv(c):
        nonlocal pp
        p = c % 2
        t, half = divmod(c, 2)
        tb = t % 2
        pgb = pgbs[tb]
        co = half * 128
        EX0 = 3584 + p * 384
        w_war = [pg_ready_t[t], ("sEV", factor_ev.get(c - 2, 0))]
        if c == 0:
            w_war = w_war + [("sSC", init_sc)]
        pgv = pgb[:, :]
        ctx.emit("scalar", w_war,
                 lambda e, co=co, EX0=EX0, pgv=pgv:
                 e.activation(afv[:, EX0: EX0 + 128], apf(pgv, co, [[256, 2], [1, 64]]),
                              AF.Exp, scale=1.0 / GLN), [("sSC", 1)])
        ctx.emit("scalar", [],
                 lambda e, c=c, EX0=EX0:
                 e.copy(decays[:, c * 2: c * 2 + 2],
                        apf(afv, EX0 + 63, [[64, 2], [1, 1]])), [("sSC", 1)])
        ctx.emit("scalar", [],
                 lambda e, co=co, EX0=EX0, pgv=pgv:
                 e.activation(afv[:, EX0 + 128: EX0 + 256], apf(pgv, co, [[256, 2], [1, 64]]),
                              AF.Exp, scale=-1.0 / GLN), [("sSC", 1)])
        ctx.emit("scalar", [],
                 lambda e, co=co, EX0=EX0, pgv=pgv:
                 e.activation(afv[:, EX0 + 256: EX0 + 384], apf(pgv, co + 64, [[256, 2], [1, 64]]),
                              AF.Exp, scale=1.0 / GLN), [("sSC", 1)])
        ctx.emit("scalar", [],
                 lambda e, c=c, pq=pp:
                 e.activation(eprefs[:, c * 2: c * 2 + 2], apf(afv, 4352 + pq, [[2, 2], [1, 1]]),
                              AF.Exp, scale=1.0 / GLN), [("sSC", 1)])
        for j in range(2):
            ctx.emit("scalar", [], lambda e, j=j, co=co, pq=pp, pgb=pgb:
                     e.add(pref[:, j * 2 + (1 - pq): j * 2 + (1 - pq) + 1],
                           pgb[:, j * 256 + co + 63: j * 256 + co + 64],
                           pref[:, j * 2 + pq: j * 2 + pq + 1]), [("sSC", 1)])
        ctx.emit("scalar", [], lambda e, c=c, EX0=EX0:
                 e.copy(apf(hbv, EGH0 + c * 64, [[seg, 2], [1, 64]]),
                        afv[:, EX0: EX0 + 128]), [("sSC", 1)])
        sc_all = ctx.ctr["sSC"]
        sc_alls[c] = sc_all
        if half == 1:
            pg_free[tb] = [("sSC", sc_all)]
        pp = 1 - pp
        for j in range(2):
            ctx.emit("vector", [("sSC", sc_all), ("sMM", mm_after_P.get(c - 2, 0))] if j == 0 else [],
                     lambda e, j=j, c=c, p=p, EX0=EX0:
                     e.tensor_tensor(apf(abv, 2432 + p * 512 + j * 256, [[64, 4], [1, 64]]),
                                     apf(hbv, QG0 + j * SEGQ + c * 64, [[seg, 4], [1, 64]]),
                                     apf(afv, EX0 + j * 64, [[0, 4], [1, 64]]),
                                     ALU.mult), [("sEV", 1)])
        ctx.emit("vector", [],
                 lambda e, c=c, half=half, p=p, EX0=EX0:
                 e.tensor_tensor(apf(abv, 3456 + p * 256 + half * 64, [[128, 2], [1, 64]]),
                                 apf(hav, KFM0 + c * 64, [[seg, 2], [1, 64]]),
                                 apf(afv, EX0 + 128, [[64, 2], [1, 64]]), ALU.mult),
                 [("sEV", 1)])
        ctx.emit("vector", [], lambda e, half=half, p=p:
                 e.memset(apf(abv, 3968 + p * 256 + (1 - half) * 64, [[128, 2], [1, 64]]), 0.0),
                 [("sEV", 1)])
        ctx.emit("vector", [], lambda e, c=c, half=half, p=p, EX0=EX0:
                 e.tensor_tensor(apf(abv, 3968 + p * 256 + half * 64, [[128, 2], [1, 64]]),
                                 apf(hav, KFM0 + c * 64, [[seg, 2], [1, 64]]),
                                 apf(afv, EX0 + 256, [[64, 2], [1, 64]]), ALU.mult),
                 [("sEV", 1)])
        factor_ev[c] = ctx.ctr["sEV"]

    def emit_pre_t(c):
        p = c % 2
        t, half = divmod(c, 2)
        if half == 0 and t + 1 < NTT:
            emit_gate_mm(t + 1)
        for j in range(2):
            ctx.emit("tensor", ([("sEV", factor_ev[c])] if j == 0 else []) + pT_free[j],
                     lambda e, j=j, p=p: e.transpose(pTs[j][:, 0:128], kppf[p][j][:, :], ident[:, :]),
                     [("sMM", 1)])
            trans_mm[c * 2 + j] = ctx.ctr["sMM"]
        pbanks = []
        for g in range(4):
            j, gh = divmod(g, 2)
            b = banks.take()
            w = banks.mm_waits(b)
            ctx.emit("tensor", w,
                     lambda e, j=j, gh=gh, b=b, p=p:
                     e.matmul(pb[b][:, 0:256], ks[p][j][gh * 64:(gh + 1) * 64, :],
                              qs[p][j][gh * 64:(gh + 1) * 64, :], start=True, stop=True),
                     [("sMM", 1)])
            banks.set_ready(b, "sMM", ctx.ctr["sMM"])
            pbanks.append(b)
        P_banks[c] = pbanks
        mm_after_P[c] = ctx.ctr["sMM"]

    def emit_kppt_evac(c):
        p = c % 2
        for j in range(2):
            sc = ctx.emit("scalar", [("sMM", trans_mm[c * 2 + j])],
                          lambda e, j=j, p=p: e.copy(kppt[p][j][:, :], pTs[j][:, 0:128]),
                          [("sSC", 1)])["sSC"]
            pT_free[j] = [("sSC", sc)]
        kppt_ev[c] = ctx.ctr["sSC"]

    def emit_mask(c):
        p = c % 2
        t, half = divmod(c, 2)
        for g in range(4):
            j, gh = divmod(g, 2)
            b = P_banks[c][g]
            w = banks.evac_wait(b) + ([("sMM", mm_after_o.get(c - 2, 0))] if g == 0 else [])
            ev = ctx.emit("vector", w,
                          lambda e, j=j, gh=gh, b=b, half=half, p=p:
                          e.tensor_tensor(abv[:, 4992 + p * 1024 + j * 512 + gh * 256:
                                              4992 + p * 1024 + j * 512 + (gh + 1) * 256],
                                          pb[b][:, 0:256],
                                          maskPh[half][:, :], ALU.mult), [("sEV", 1)])["sEV"]
            banks.add_free(b, "sEV", ev)
        mask_ev[c] = ctx.ctr["sEV"]

    def emit_tail_t(c):
        p = c % 2
        t, half = divmod(c, 2)
        obanks = []
        for j in range(2):
            b = banks.take()
            w = banks.mm_waits(b) + ([("sEV", mask_ev[c])] if j == 0 else [])
            for gh in range(2):
                g = 2 * j + gh
                vsl = vtm[:, t * 256 + g * 64: t * 256 + (g + 1) * 64]
                tp1 = (0, gh * 64) if gh else None
                ctx.emit("tensor", w if gh == 0 else [],
                         lambda e, b=b, vsl=vsl, j=j, gh=gh, tp1=tp1, p=p:
                         e.matmul(pb[b][gh * 64:(gh + 1) * 64, 0:256], vsl,
                                  abv[:, 4992 + p * 1024 + j * 512 + gh * 256:
                                       4992 + p * 1024 + j * 512 + (gh + 1) * 256],
                                  start=True, stop=True, tile_position=tp1),
                         [("sMM", 1)] if gh == 1 else [])
            banks.set_ready(b, "sMM", ctx.ctr["sMM"])
            obanks.append(b)
        sbanks = []
        for j in range(2):
            b = banks.take()
            w = banks.mm_waits(b) + ([("sSC", kppt_ev[c])] if j == 0 else [])
            for gh in range(2):
                g = 2 * j + gh
                vsl = vtm[:, t * 256 + g * 64: t * 256 + (g + 1) * 64]
                tp = (0, gh * 64) if gh else None
                ctx.emit("tensor", w if gh == 0 else [],
                         lambda e, b=b, j=j, gh=gh, vsl=vsl, tp=tp, p=p:
                         e.matmul(pb[b][gh * 64:(gh + 1) * 64, 0:64],
                                  kppt[p][j][:, gh * 64:(gh + 1) * 64],
                                  vsl, start=True, stop=True, tile_position=tp),
                         [("sMM", 1)] if gh == 1 else [])
            banks.set_ready(b, "sMM", ctx.ctr["sMM"])
            sbanks.append(b)
        mm_after_o[c] = ctx.ctr["sMM"]
        tail_banks[c] = (obanks, sbanks)

    def emit_tail_v(c):
        obanks, sbanks = tail_banks[c]
        for j in range(2):
            b = obanks[j]
            dst = og[j].rearrange("p (hp t) -> p hp t", hp=4)[:, :, c * 64:(c + 1) * 64]
            oeng, osem = (("scalar", "sSC") if j == 0 else ("vector", "sEV"))
            ov = ctx.emit(oeng, banks.evac_wait(b),
                          (lambda e, b=b, dst=dst:
                           e.copy(dst, pb[b][:, 0:256].rearrange("p (hp t) -> p hp t", hp=4)))
                          if j == 0 else
                          (lambda e, b=b, dst=dst:
                           e.tensor_copy(dst, pb[b][:, 0:256].rearrange("p (hp t) -> p hp t", hp=4))),
                          [(osem, 1)])[osem]
            banks.add_free(b, osem, ov)
        for j in range(2):
            b = sbanks[j]
            ev = ctx.emit("vector", banks.evac_wait(b) +
                          ([("sSC", sbf_sc.get(c - 1, 0))] if j == 0 else []),
                          lambda e, b=b, j=j, c=c:
                          e.scalar_tensor_tensor(sf32[j][:, :], sf32[j][:, :],
                                                 decays[:, c * 2 + j: c * 2 + j + 1],
                                                 pb[b][:, 0:64],
                                                 ALU.mult, ALU.add), [("sEV", 1)])["sEV"]
            banks.add_free(b, "sEV", ev)
        stt_ev[c] = ctx.ctr["sEV"]

    def emit_sbf_copy(c):
        ctx.emit("scalar", [("sEV", stt_ev[c])], lambda e, c=c:
                 e.copy(sbf_all[:, c * 128: (c + 1) * 128],
                        afv[:, 4360: 4360 + 128]), [("sSC", 1)])
        sbf_sc[c] = ctx.ctr["sSC"]

    sc_alls = {}
    sbf_sc = {}
    mm_after_P = {}
    kppt_ev = {}
    mask_ev = {}
    P_banks = {}
    pg_ready_t = {}
    tail_banks = {}
    stt_ev = {}
    q_cs = {}
    trans_mm = {}
    emit_gate_mm(0)
    for c in range(NCH):
        emit_pre_sv(c)
        if c >= 1:
            emit_kppt_evac(c - 1)
        if c >= 2:
            emit_sbf_copy(c - 2)
        if c >= 1:
            emit_tail_t(c - 1)
            emit_tail_v(c - 1)
        emit_pre_t(c)
        emit_mask(c)
    emit_kppt_evac(NCH - 1)
    emit_sbf_copy(NCH - 2)
    emit_tail_t(NCH - 1)
    emit_tail_v(NCH - 1)
    emit_sbf_copy(NCH - 1)

    chunk_done = ctx.barrier_waits()
    for e in ("vector", "gpsimd", "scalar"):
        ctx.guard(e, chunk_done)
    ctx.emit("vector", [], lambda e: e.memset(blob[:, :], 0.0), [("sEV", 1)])
    for j in range(2):
        ctx.emit("vector", [], lambda e, j=j: e.tensor_copy(blob[:, j * 64:(j + 1) * 64], sf32[j][:, :]),
                 [("sEV", 1)])
        ctx.emit("vector", [], lambda e, j=j, p=pp:
                 e.tensor_copy(blob[:, 128 + j:129 + j], pref[:, j * 2 + p: j * 2 + p + 1]), [("sEV", 1)])
    ev_blob = ctx.ctr["sEV"]
    xd = dma("gpsimd", "sXD0", dccin[:, :], blob[:, :], waits=[("sEV", ev_blob)])["sXD0"]
    cc = ctx.emit("gpsimd", [("sXD0", xd)],
                  lambda e: e.collective_compute("AllGather", ALU.bypass,
                                                 replica_groups=[list(range(NCORE))],
                                                 ins=[dccin[:, :]], outs=[dccout[:, :]]),
                  [("sCC", 1)], is_dma=True)["sCC"]
    xd = dma("gpsimd", "sXD0", gath[:, :].rearrange("p (r c) -> p r c", r=8),
             dccout.rearrange("(r p) c -> p r c", p=128), waits=[("sCC", cc)])["sXD0"]
    gath_xd = xd
    gath3 = gath[:, :].rearrange("p (r c) -> p r c", r=8)
    sumg_all = [gath3[:, :, 128 + j] for j in range(2)]
    for j in range(2):
        ctx.emit("vector", [("sXD0", gath_xd)],
                 lambda e, j=j: e.memset(sin[j * 2][:, :], 0.0), [("sEV", 1)])
    par = [0, 0]
    for i in range(NCORE):
        ctx.emit("vector", [("sXD0", gath_xd)],
                 lambda e, i=i: e.tensor_tensor(w8[:, 0:8], sumg_all[0],
                                                nuv[:, i * 8:(i + 1) * 8], ALU.mult), [("sEV", 1)])
        ctx.guard("vector", [("sEV", ctx.ctr["sEV"])])
        ctx.emit("vector", [],
                 lambda e: e.tensor_reduce(w8[:, 8:9], w8[:, 0:8], AX.X, ALU.add), [("sEV", 1)])
        ctx.guard("vector", [("sEV", ctx.ctr["sEV"])])
        ctx.emit("vector", [],
                 lambda e, i=i: e.tensor_tensor(w8[:, 0:8], sumg_all[1],
                                                nuv[:, i * 8:(i + 1) * 8], ALU.mult), [("sEV", 1)])
        ctx.guard("vector", [("sEV", ctx.ctr["sEV"])])
        ctx.emit("vector", [],
                 lambda e: e.tensor_reduce(w8[:, 9:10], w8[:, 0:8], AX.X, ALU.add), [("sEV", 1)])
        ctx.guard("vector", [("sEV", ctx.ctr["sEV"])])
        ev_z = ctx.ctr["sEV"]
        ctx.emit("scalar", [("sEV", ev_z)],
                 lambda e: e.activation(w8[:, 10:12], w8[:, 8:10], AF.Exp, scale=1.0 / GLN), [("sSC", 1)])
        sc_w = ctx.ctr["sSC"]
        ev_w = ctx.emit("vector", [("sSC", sc_w)],
                        lambda e, i=i: e.tensor_scalar_mul(w8[:, 12:14], w8[:, 10:12], muv[:, i:i + 1]),
                        [("sEV", 1)])["sEV"]
        ctx.guard("vector", [("sEV", ev_w)])
        for j in range(2):
            p = par[j]
            ctx.emit("vector", [],
                     lambda e, i=i, j=j, p=p:
                     e.scalar_tensor_tensor(sin[j * 2 + (1 - p)][:, :],
                                            gath[:, i * 132 + j * 64: i * 132 + (j + 1) * 64],
                                            w8[:, 12 + j:13 + j],
                                            sin[j * 2 + p][:, :], ALU.mult, ALU.add), [("sEV", 1)])
            ctx.guard("vector", [("sEV", ctx.ctr["sEV"])])
            par[j] = 1 - p
    sin_fin = [sin[0 + par[0]], sin[2 + par[1]]]
    if DEBUG:
        ctx.guard("scalar", ctx.barrier_waits())
        dma("scalar", "sVD0", ddbg["gath"][:, :], gath[:, :])
        dma("scalar", "sVD0", ddbg["w8"][:, :], w8[:, :])
        dma("scalar", "sVD0", ddbg["muv"][:, :], muv[:, :])

    comb_done = ctx.barrier_waits()

    for e in ("tensor", "vector", "gpsimd"):
        ctx.guard(e, comb_done)
    corr_mm = {}
    for c in range(NCH):
        p = c % 2
        ctx.emit("vector", [("sMM", corr_mm.get(c - 2, 0))],
                 lambda e, c=c, p=p:
                 e.tensor_tensor(apf(abv, 2432 + p * 512, [[256, 2], [64, 4], [1, 64]]),
                                 apf(hbv, QG0 + c * 64, [[SEGQ, 2], [seg, 4], [1, 64]]),
                                 apf(hbv, EGH0 + c * 64, [[seg, 2], [0, 4], [1, 64]]),
                                 ALU.mult), [("sEV", 1)])
        ev_q3 = ctx.ctr["sEV"]
        for j in range(2):
            if c == 0:
                ctx.emit("vector", [],
                         lambda e, j=j: e.tensor_scalar_mul(stot[0][j][:, :], sin_fin[j][:, :],
                                                            eprefs[:, j: j + 1]), [("sEV", 1)])
            else:
                ctx.emit("vector", [("sMM", corr_mm.get(c - 2, 0))] if j == 0 else [],
                         lambda e, j=j, c=c, p=p:
                         e.scalar_tensor_tensor(stot[p][j][:, :], sin_fin[j][:, :],
                                                eprefs[:, c * 2 + j: c * 2 + j + 1],
                                                sbf_all[:, (c - 1) * 128 + j * 64: (c - 1) * 128 + (j + 1) * 64],
                                                ALU.mult, ALU.add), [("sEV", 1)])
        ev_st = ctx.ctr["sEV"]
        for j in range(2):
            b = banks.take()
            w = banks.mm_waits(b) + ([("sEV", ev_st)] if j == 0 else [])
            for gh in range(2):
                tpc = (gh * 64, gh * 64) if gh else None
                ctx.emit("tensor", w if gh == 0 else [],
                         lambda e, b=b, j=j, gh=gh, tpc=tpc, p=p:
                         e.matmul(pb[b][gh * 64:(gh + 1) * 64, 0:256],
                                  stot[p][j][gh * 64:(gh + 1) * 64, :],
                                  qs[p][j][gh * 64:(gh + 1) * 64, :], start=True, stop=True,
                                  tile_position=tpc), [("sMM", 1)] if gh == 1 else [])
            banks.set_ready(b, "sMM", ctx.ctr["sMM"])
            dst = og[j].rearrange("p (hp t) -> p hp t", hp=4)[:, :, c * 64:(c + 1) * 64]
            ev = ctx.emit("vector", banks.evac_wait(b),
                          lambda e, b=b, dst=dst:
                          e.tensor_tensor(dst, dst, pb[b][:, 0:256].rearrange("p (hp t) -> p hp t", hp=4),
                                          ALU.add), [("sEV", 1)])["sEV"]
            banks.add_free(b, "sEV", ev)
        corr_mm[c] = ctx.ctr["sMM"]
    corr_done = ctx.barrier_waits()

    for e in ("tensor", "vector", "scalar"):
        ctx.guard(e, corr_done)
    sqo_s = [af[:, 2048:2560], af[:, 2560:3072]]
    lnrow = af[:, 3072:3584]
    exp_s = [af[:, 3584:4096], af[:, 4096:4608]]
    var_mm = {}
    bc_mm = {}
    on_banks = {}

    def on_sq(n):
        j, ti = divmod(n, 4 * NT)
        sc_sq = ctx.emit("scalar", [("sMM", var_mm.get(n - 2, 0))],
                         lambda e, j=j, ti=ti, n=n:
                         e.activation(sqo_s[n % 2][:, :], og[j][:, ti * 512:(ti + 1) * 512],
                                      AF.Square), [("sSC", 1)])["sSC"]
        b = banks.take()
        w = banks.mm_waits(b) + [("sSC", sc_sq)]
        ctx.emit("tensor", w,
                 lambda e, b=b, n=n: e.matmul(pb[b][0:2, :], onesv[:, 1:3], sqo_s[n % 2][:, :],
                                              start=True, stop=True), [("sMM", 1)])
        banks.set_ready(b, "sMM", ctx.ctr["sMM"])
        var_mm[n] = ctx.ctr["sMM"]
        on_banks[n] = b

    def on_fin(n):
        j, ti = divmod(n, 4 * NT)
        b = on_banks[n]
        ctx.emit("scalar", banks.evac_wait(b),
                 lambda e, b=b: e.activation(lnrow[0:2, 0:512], pb[b][0:2, :], AF.Ln,
                                             scale=1.0 / 64.0, bias=cst[0:2, 1:2]), [("sSC", 1)])
        sc_ln = ctx.ctr["sSC"]
        banks.add_free(b, "sSC", sc_ln)
        ctx.emit("scalar", [("sMM", bc_mm.get(n - 2, 0))],
                 lambda e, n=n: e.activation(exp_s[n % 2][0:2, 0:512], lnrow[0:2, 0:512],
                                             AF.Exp, scale=-0.5), [("sSC", 1)])
        sc_r = ctx.ctr["sSC"]
        b2 = banks.take()
        w2 = banks.mm_waits(b2) + [("sSC", sc_r)]
        ctx.emit("tensor", w2,
                 lambda e, b2=b2, n=n: e.matmul(pb[b2][:, :], hbc[:, :], exp_s[n % 2][0:2, 0:512],
                                                start=True, stop=True), [("sMM", 1)])
        banks.set_ready(b2, "sMM", ctx.ctr["sMM"])
        bc_mm[n] = ctx.ctr["sMM"]
        ev = ctx.emit("vector", banks.evac_wait(b2),
                      lambda e, j=j, ti=ti, b2=b2:
                      e.scalar_tensor_tensor(og[j][:, ti * 512:(ti + 1) * 512],
                                             og[j][:, ti * 512:(ti + 1) * 512],
                                             scl[:, 136:137], pb[b2][:, :],
                                             ALU.mult, ALU.mult), [("sEV", 1)])["sEV"]
        banks.add_free(b2, "sEV", ev)

    NON = 8 * NT
    for n in range(NON):
        on_sq(n)
        if n >= 1:
            on_fin(n - 1)
    on_fin(NON - 1)
    onorm_done = ctx.barrier_waits()

    for e in ("tensor", "vector", "gpsimd"):
        ctx.guard(e, onorm_done)
    xw_free = [[], []]
    yw_free = [[], []]
    for mb in range(8):
        for tt in range(NT):
            sl = (mb * NT + tt) % 2
            sem = "sXD%d" % sl
            xd = dma("gpsimd", sem, xwo[sl][:, :], dx[mb * 128:(mb + 1) * 128, tt * 512:(tt + 1) * 512],
                     waits=xw_free[sl])[sem]
            b = banks.take()
            parts = []
            for kc in range(8):
                j, hp = divmod(kc, 4)
                parts.append((wo_b[:, kc * 1024 + mb * 128: kc * 1024 + (mb + 1) * 128],
                              og[j][:, hp * seg + tt * 512: hp * seg + tt * 512 + 512]))
            mm_group(parts, b)
            ev = ctx.emit("vector", banks.evac_wait(b) + [(sem, xd)] + yw_free[sl],
                          lambda e, sl=sl, b=b: e.tensor_tensor(ysb[sl][:, :], xwo[sl][:, :], pb[b][:, :],
                                                                ALU.add), [("sEV", 1)])["sEV"]
            banks.add_free(b, "sEV", ev)
            xw_free[sl] = [("sEV", ev)]
            vsem = "sVD%d" % sl
            st = ctx.emit("scalar", [("sEV", ev)],
                          lambda e, sl=sl, mb=mb, tt=tt:
                          e.dma_start(out=dresid[mb * 128:(mb + 1) * 128, tt * 512:(tt + 1) * 512],
                                      in_=ysb[sl][:, :]), [(vsem, 16)], is_dma=True)[vsem]
            yw_free[sl] = [(vsem, st)]
    if DEBUG:
        ctx.guard("scalar", ctx.barrier_waits())
        dma("scalar", "sVD0", ddbg["resid"][:, :], dresid[:, :])
    wo_done = ctx.barrier_waits()
    resid_all_st = [("sVD0", ctx.ctr["sVD0"]), ("sVD1", ctx.ctr["sVD1"])]

    for e in ENGINES:
        ctx.guard(e, wo_done)

    wbf_free = [[], []]
    wbf_ready = [None, None]
    rs_free = [[], []]
    ysb_free = [list(resid_all_st), list(resid_all_st)]
    silu_free = [[], []]
    n_y = 0

    def stream_weight(dram, mb, nquart, slot):
        sem = "sWD%d" % slot
        wd = dma("gpsimd", sem, wbf[slot][:, 0:dram[mb].shape[1]], dram[mb],
                 waits=wbf_free[slot])[sem]
        wbf_free[slot] = []
        wbf_ready[slot] = (sem, wd)

    def stream_weight_pair(d1, d2, mb, slot):
        sem = "sWD%d" % slot
        dma("gpsimd", sem, wbf[slot][:, 0:1024], d1[mb], waits=wbf_free[slot])
        wd = dma("gpsimd", sem, wbf[slot][:, 1024:2048], d2[mb])[sem]
        wbf_free[slot] = []
        wbf_ready[slot] = (sem, wd)

    for p in range(2):
        tok0 = p * TP
        if p == 1:
            pass2 = [("sMM", ctx.ctr["sMM"])]
            ctx.guard("vector", pass2)
            ctx.guard("scalar", pass2)
        stA_ev = []
        stream_weight_pair(dw1, dw2, 0, 0)
        for mb in range(32):
            slot = mb % 2
            if mb + 1 < 32:
                stream_weight_pair(dw1, dw2, mb + 1, 1 - slot)
            last_mm_for_slot = None
            for t2 in range(NT2):
                bA = banks.take()
                wA = banks.mm_waits(bA) + ([wbf_ready[slot]] if t2 == 0 else [])
                for kc in range(8):
                    ctx.emit("tensor", wA if kc == 0 else [],
                             lambda e, bA=bA, kc=kc, t2=t2, slot=slot, tok0=tok0, st=(kc == 0):
                             e.matmul(pb[bA][:, :], wbf[slot][:, kc * 128:(kc + 1) * 128],
                                      xn[:, kc * seg + tok0 + t2 * 512: kc * seg + tok0 + t2 * 512 + 512],
                                      start=st, stop=(kc == 7)), [("sMM", 1)] if kc == 7 else [])
                banks.set_ready(bA, "sMM", ctx.ctr["sMM"])
                bB = banks.take()
                wB = banks.mm_waits(bB)
                for kc in range(8):
                    ctx.emit("tensor", wB if kc == 0 else [],
                             lambda e, bB=bB, kc=kc, t2=t2, slot=slot, tok0=tok0, st=(kc == 0):
                             e.matmul(pb[bB][:, :], wbf[slot][:, 1024 + kc * 128: 1024 + (kc + 1) * 128],
                                      xn[:, kc * seg + tok0 + t2 * 512: kc * seg + tok0 + t2 * 512 + 512],
                                      start=st, stop=(kc == 7)), [("sMM", 1)] if kc == 7 else [])
                banks.set_ready(bB, "sMM", ctx.ctr["sMM"])
                last_mm_for_slot = ctx.ctr["sMM"]
                ssl = (mb * NT2 + t2) % 2
                sc_si = ctx.emit("scalar", banks.evac_wait(bA) + silu_free[ssl],
                                 lambda e, bA=bA, mb=mb, ssl=ssl:
                                 e.activation(silu_s[ssl][:, :], pb[bA][:, :], AF.Silu,
                                              scale=scl[:, mb:mb + 1]), [("sSC", 1)])["sSC"]
                banks.add_free(bA, "sSC", sc_si)
                ev = ctx.emit("vector", banks.evac_wait(bB) + [("sSC", sc_si)],
                              lambda e, bB=bB, mb=mb, t2=t2, ssl=ssl:
                              e.scalar_tensor_tensor(ha[:, mb * TP + t2 * 512: mb * TP + t2 * 512 + 512],
                                                     pb[bB][:, :], scl[:, 32 + mb: 33 + mb],
                                                     silu_s[ssl][:, :], ALU.mult, ALU.mult),
                              [("sEV", 1)])["sEV"]
                banks.add_free(bB, "sEV", ev)
                silu_free[ssl] = [("sEV", ev)]
                stA_ev.append(ev)
            wbf_free[slot] = [("sMM", last_mm_for_slot)]
        evA = ctx.ctr["sEV"]
        stream_weight(dh1, 0, 4, 0)
        for mb in range(32):
            slot = mb % 2
            if mb + 1 < 32:
                stream_weight(dh1, mb + 1, 4, 1 - slot)
            for t2 in range(NT2):
                b = banks.take()
                w = banks.mm_waits(b) + ([wbf_ready[slot], ("sEV", evA)] if t2 == 0 else [])
                for kc in range(32):
                    ctx.emit("tensor", w if kc == 0 else [],
                             lambda e, b=b, kc=kc, t2=t2, slot=slot, st=(kc == 0):
                             e.matmul(pb[b][:, :], wbf[slot][:, kc * 128:(kc + 1) * 128],
                                      ha[:, kc * TP + t2 * 512: kc * TP + t2 * 512 + 512],
                                      start=st, stop=(kc == 31)), [("sMM", 1)] if kc == 31 else [])
                banks.set_ready(b, "sMM", ctx.ctr["sMM"])
                last_mm_for_slot = ctx.ctr["sMM"]
                sc_ = ctx.emit("scalar", banks.evac_wait(b),
                               lambda e, b=b, mb=mb, t2=t2:
                               e.activation(hb[:, mb * TP + t2 * 512: mb * TP + t2 * 512 + 512],
                                            pb[b][:, :], AF.Gelu, scale=scl[:, 64 + mb: 65 + mb]),
                               [("sSC", 1)])["sSC"]
                banks.add_free(b, "sSC", sc_)
            wbf_free[slot] = [("sMM", last_mm_for_slot)]
        scB = ctx.ctr["sSC"]
        stream_weight(dh2, 0, 4, 0)
        for mb in range(32):
            slot = mb % 2
            if mb + 1 < 32:
                stream_weight(dh2, mb + 1, 4, 1 - slot)
            for t2 in range(NT2):
                b = banks.take()
                w = banks.mm_waits(b) + ([wbf_ready[slot], ("sSC", scB)] if t2 == 0 else [])
                for kc in range(32):
                    ctx.emit("tensor", w if kc == 0 else [],
                             lambda e, b=b, kc=kc, t2=t2, slot=slot, st=(kc == 0):
                             e.matmul(pb[b][:, :], wbf[slot][:, kc * 128:(kc + 1) * 128],
                                      hb[:, kc * TP + t2 * 512: kc * TP + t2 * 512 + 512],
                                      start=st, stop=(kc == 31)), [("sMM", 1)] if kc == 31 else [])
                banks.set_ready(b, "sMM", ctx.ctr["sMM"])
                last_mm_for_slot = ctx.ctr["sMM"]
                sc_ = ctx.emit("scalar", banks.evac_wait(b),
                               lambda e, b=b, mb=mb, t2=t2:
                               e.activation(ha[:, mb * TP + t2 * 512: mb * TP + t2 * 512 + 512],
                                            pb[b][:, :], AF.Gelu, scale=scl[:, 96 + mb: 97 + mb]),
                               [("sSC", 1)])["sSC"]
                banks.add_free(b, "sSC", sc_)
            wbf_free[slot] = [("sMM", last_mm_for_slot)]
        scC = ctx.ctr["sSC"]
        stream_weight(dw3, 0, 4, 0)
        for mb in range(8):
            slot = mb % 2
            if mb + 1 < 8:
                stream_weight(dw3, mb + 1, 4, 1 - slot)
            for t2 in range(NT2):
                rsl = (mb * NT2 + t2) % 2
                rsem = "sXD%d" % rsl
                xd = dma("gpsimd", rsem, rstg[rsl][:, :],
                         dresid[mb * 128:(mb + 1) * 128, tok0 + t2 * 512: tok0 + t2 * 512 + 512],
                         waits=rs_free[rsl] + resid_all_st)[rsem]
                b = banks.take()
                w = banks.mm_waits(b) + ([wbf_ready[slot], ("sSC", scC)] if t2 == 0 else [])
                for kc in range(32):
                    ctx.emit("tensor", w if kc == 0 else [],
                             lambda e, b=b, kc=kc, t2=t2, slot=slot, st=(kc == 0):
                             e.matmul(pb[b][:, :], wbf[slot][:, kc * 128:(kc + 1) * 128],
                                      ha[:, kc * TP + t2 * 512: kc * TP + t2 * 512 + 512],
                                      start=st, stop=(kc == 31)), [("sMM", 1)] if kc == 31 else [])
                banks.set_ready(b, "sMM", ctx.ctr["sMM"])
                last_mm_for_slot = ctx.ctr["sMM"]
                ysl = (mb * NT2 + t2) % 2
                ev = ctx.emit("vector", banks.evac_wait(b) + [(rsem, xd)] + ysb_free[ysl],
                              lambda e, b=b, mb=mb, rsl=rsl, ysl=ysl:
                              e.scalar_tensor_tensor(ysb[ysl][:, :], pb[b][:, :],
                                                     scl[:, 128 + mb: 129 + mb], rstg[rsl][:, :],
                                                     ALU.mult, ALU.add), [("sEV", 1)])["sEV"]
                banks.add_free(b, "sEV", ev)
                rs_free[rsl] = [("sEV", ev)]
                ysem = "sVD%d" % ysl
                st = ctx.emit("scalar", [("sEV", ev)],
                              lambda e, ysl=ysl, mb=mb, t2=t2, tok0=tok0:
                              e.dma_start(out=dy[mb * 128:(mb + 1) * 128, tok0 + t2 * 512: tok0 + t2 * 512 + 512],
                                          in_=ysb[ysl][:, :]), [(ysem, 16)], is_dma=True)[ysem]
                n_y += 1
                ysb_free[ysl] = [("sEV", ev), (ysem, st)]
            wbf_free[slot] = [("sMM", last_mm_for_slot)]

    ctx.guard("scalar", [("sVD0", ctx.ctr["sVD0"]), ("sVD1", ctx.ctr["sVD1"])])

    with nc.Block() as blk:
        @blk.sync
        def _(e):
            for fn in ctx.q["sync"]:
                fn(e)

        @blk.gpsimd
        def _(e):
            for fn in ctx.q["gpsimd"]:
                fn(e)

        @blk.tensor
        def _(e):
            for fn in ctx.q["tensor"]:
                fn(e)

        @blk.vector
        def _(e):
            for fn in ctx.q["vector"]:
                fn(e)

        @blk.scalar
        def _(e):
            for fn in ctx.q["scalar"]:
                fn(e)

    es.close()
    return nc


def _prep_inputs(inputs, seg=2048):
    f = np.ascontiguousarray
    x = inputs["L0_attn_hidden_states"]
    Wq = inputs["L0_attn_Wq"]; Wk = inputs["L0_attn_Wk"]; Wv = inputs["L0_attn_Wv"]
    Wo = inputs["L0_attn_Wo"]; Wg1 = inputs["L0_attn_Wg1"]; Wg2 = inputs["L0_attn_Wg2"]
    bg2 = inputs["L0_attn_bg2"]; gnw = inputs["L0_attn_g_norm_weight"]
    w1 = inputs["L0_ffn_w1_w"]; s1 = inputs["L0_ffn_w1_s"]
    w2 = inputs["L0_ffn_w2_w"]; s2 = inputs["L0_ffn_w2_s"]
    h1 = inputs["L0_ffn_h1_w"]; hs1 = inputs["L0_ffn_h1_s"]
    h2 = inputs["L0_ffn_h2_w"]; hs2 = inputs["L0_ffn_h2_s"]
    w3 = inputs["L0_ffn_w3_w"]; s3 = inputs["L0_ffn_w3_s"]

    perm = []
    for j in range(2):
        for hp in range(4):
            for habs in (8 * j + hp, 8 * j + 4 + hp):
                perm.extend(range(habs * 64, habs * 64 + 64))
    perm = np.array(perm)

    wq_p = f(Wq[:, perm].reshape(8, 128, 1024))
    wo_p = f(Wo[perm, :].reshape(8, 128, 1024))
    wk_p = f(Wk.reshape(8, 128, 256))
    wv_p = f(Wv.reshape(8, 128, 256))
    wg1_p = f(Wg1.reshape(8, 128, 16))
    wg2a = f(np.vstack([Wg2, bg2[None, :]]))

    def tiles(wt, nmb):
        IN, OUT = wt.shape
        nkc = IN // 128
        return f(wt.reshape(nkc, 128, nmb, 128).transpose(2, 1, 0, 3).reshape(nmb, 128, nkc * 128))

    w1t = tiles(w1.T, 32)
    w2t = tiles(w2.T, 32)
    h1t = tiles(h1.T, 32)
    h2t = tiles(h2.T, 32)
    w3t = tiles(w3.T, 8)

    scl = np.zeros((128, 137), np.float32)
    scl[:, 0:32] = s1.reshape(32, 128).T
    scl[:, 32:64] = s2.reshape(32, 128).T
    scl[:, 64:96] = hs1.reshape(32, 128).T
    scl[:, 96:128] = hs2.reshape(32, 128).T
    scl[:, 128:136] = s3.reshape(8, 128).T
    scl[:, 136] = np.tile(gnw, 2)
    scl = f(scl)

    cstv = np.zeros((128, 8), np.float32)
    cstv[:, 0] = 1e-5
    cstv[:, 1] = 1e-6
    cstv[:, 2] = 1.0
    cstv = f(cstv)

    sidx = np.arange(C)
    m_incl = (sidx[:, None] <= sidx[None, :]).astype(np.float32)
    m_after = (sidx[:, None] > sidx[None, :]).astype(np.float32)
    z64 = np.zeros((C, C), np.float32)
    mcat = np.zeros((128, 256), np.float32)
    mcat[0:64, 0:64] = m_incl
    mcat[0:64, 64:128] = m_after
    mcat[64:128, 128:192] = m_incl
    mcat[64:128, 192:256] = m_after
    mcat = f(mcat)
    maskP0 = np.zeros((128, 256), np.float32)
    maskP0[0:64, :] = np.tile(m_incl, (1, 4))
    maskP1 = np.zeros((128, 256), np.float32)
    maskP1[64:128, :] = np.tile(m_incl, (1, 4))
    maskP0 = f(maskP0.astype(BF16))
    maskP1 = f(maskP1.astype(BF16))
    ident = f(np.eye(128, dtype=np.float32).astype(BF16))
    onesv = np.zeros((128, 4), np.float32)
    onesv[:, 0] = 1.0
    onesv[:64, 1] = 1.0
    onesv[64:, 2] = 1.0
    onesv = f(onesv)
    onesr = np.zeros((2, 128), np.float32)
    onesr[0, :] = 1.0
    onesr = f(onesr)
    hbcm = np.zeros((2, 128), np.float32)
    hbcm[0, 0:64] = 1.0
    hbcm[1, 64:128] = 1.0
    hbcm = f(hbcm)

    onesbv = np.ones((128, 1), BF16)
    shared = dict(onesb=onesbv,
                  wq=wq_p, wk=wk_p, wv=wv_p, wg1=wg1_p, wg2a=wg2a, wo=wo_p,
                  w1t=w1t, w2t=w2t, h1t=h1t, h2t=h2t, w3t=w3t,
                  scl=scl, cst=cstv, mcat=mcat, maskP0=maskP0, maskP1=maskP1, ident=ident,
                  onesv=onesv, onesr=onesr, hbc=hbcm)

    nseg = x.shape[1] // seg
    in_maps = []
    for core in range(NCORE):
        b, sgi = divmod(core, nseg)
        m = dict(shared)
        m["x"] = f(x[b, sgi * seg:(sgi + 1) * seg, :].T)
        mu = np.zeros(8, np.float32)
        nu = np.zeros((8, 8), np.float32)
        for i in range(8):
            if i // nseg == b and i < core:
                mu[i] = 1.0
                for l in range(8):
                    if l // nseg == b and i < l < core:
                        nu[i, l] = 1.0
        m["muv"] = f(np.tile(mu[None, :], (128, 1)))
        m["nuv"] = f(np.tile(nu.reshape(1, 64), (128, 1)))
        in_maps.append(m)
    return in_maps


def kernel(**inputs):
    seg = 2048
    in_maps = _prep_inputs(inputs, seg)
    nc = build_program(seg)
    res = run_bass_kernel_spmd(nc, in_maps, core_ids=list(range(NCORE)))
    out = np.empty((2, 8192, 1024), np.float32)
    nseg = 8192 // seg
    for core in range(NCORE):
        b, sgi = divmod(core, nseg)
        out[b, sgi * seg:(sgi + 1) * seg, :] = res.results[core]["y"].T
    return out


# version 20
# speedup vs baseline: 1.0943x; 1.0038x over previous
import contextlib
import numpy as np
import ml_dtypes

import concourse.bass as bass
import concourse.mybir as mybir
from concourse.bass_utils import run_bass_kernel_spmd

BF16 = ml_dtypes.bfloat16
F32 = mybir.dt.float32
BF = mybir.dt.bfloat16
AF = mybir.ActivationFunctionType
ALU = mybir.AluOpType
AX = mybir.AxisListType

NCORE = 8
D_MODEL = 1024
D_HID = 4096
H, HKV, D = 16, 4, 64
C = 64
GLN = 16.0

ENGINES = ("sync", "gpsimd", "tensor", "vector", "scalar")


SELF_EDGE_ENGINES = {"vector": "sEV", "scalar": "sSC", "gpsimd": "sCS"}


class Ctx:

    def __init__(self, self_edges=False):
        self.q = {e: [] for e in ENGINES}
        self.ctr = {s: 0 for s in ("sWD0", "sWD1", "sCN", "sXD0", "sXD1", "sVD0", "sVD1",
                                   "sCS", "sMM", "sSC", "sEV", "sCC")}
        self.sems = {}
        self.self_edges = self_edges

    def emit(self, eng, waits, build, incs, is_dma=False):
        waits = [(s, v) for (s, v) in waits if v > 0]
        incs = list(incs)
        if self.self_edges and eng in SELF_EDGE_ENGINES and not is_dma:
            sname = SELF_EDGE_ENGINES[eng]
            assert any(s == sname for s, _ in incs), (eng, incs)
            if self.ctr[sname] > 0:
                waits = waits + [(sname, self.ctr[sname])]
        for s, a in incs:
            self.ctr[s] += a
        sems = self.sems

        def fn(e, waits=tuple(waits), incs=tuple(incs), build=build):
            for s, v in waits:
                e.wait_ge(sems[s], v)
            ins = build(e)
            for s, a in incs:
                ins = ins.then_inc(sems[s], a)
        self.q[eng].append(fn)
        return {s: self.ctr[s] for s, _ in incs}

    def barrier_waits(self):
        return [(s, self.ctr[s]) for s in ("sMM", "sEV", "sSC", "sCS")]

    def guard(self, eng, waits):
        sems = self.sems

        def fn(e, waits=tuple([(s, v) for s, v in waits if v > 0])):
            for s, v in waits:
                e.wait_ge(sems[s], v)
        self.q[eng].append(fn)


class Banks:

    def __init__(self, ctx, tensors):
        self.ctx = ctx
        self.t = tensors
        self.free_w = [[] for _ in tensors]
        self.ready = [None] * len(tensors)
        self.in_use = [False] * len(tensors)
        self.rr = 0

    def take(self):
        for _ in range(len(self.t)):
            i = self.rr % len(self.t)
            self.rr += 1
            if not self.in_use[i]:
                self.in_use[i] = True
                return i
        raise RuntimeError("PSUM bank pool exhausted: all banks in use")

    def mm_waits(self, i):
        w = self.free_w[i]
        self.free_w[i] = []
        return w

    def set_ready(self, i, sem, val):
        self.ready[i] = (sem, val)

    def evac_wait(self, i):
        return [self.ready[i]] if self.ready[i] else []

    def add_free(self, i, sem, val):
        self.free_w[i].append((sem, val))
        self.in_use[i] = False


DEBUG = False
SIM_SELF_EDGES = True

def build_program(seg=2048):
    assert seg % 128 == 0
    NT = seg // 512
    NTT = seg // 128
    NCH = seg // C
    TP = seg // 2
    NT2 = TP // 512

    nc = bass.Bass("TRN2", target_bir_lowering=False)

    dx = nc.dram_tensor("x", [D_MODEL, seg], F32, kind="ExternalInput").ap()
    dwq = nc.dram_tensor("wq", [8, 128, 1024], F32, kind="ExternalInput").ap()
    dwk = nc.dram_tensor("wk", [8, 128, 256], F32, kind="ExternalInput").ap()
    dwv = nc.dram_tensor("wv", [8, 128, 256], F32, kind="ExternalInput").ap()
    dwg1 = nc.dram_tensor("wg1", [8, 128, 16], F32, kind="ExternalInput").ap()
    dwg2a = nc.dram_tensor("wg2a", [17, 256], F32, kind="ExternalInput").ap()
    dwo = nc.dram_tensor("wo", [8, 128, 1024], F32, kind="ExternalInput").ap()
    dw1 = nc.dram_tensor("w1t", [32, 128, 1024], F32, kind="ExternalInput").ap()
    dw2 = nc.dram_tensor("w2t", [32, 128, 1024], F32, kind="ExternalInput").ap()
    dh1 = nc.dram_tensor("h1t", [32, 128, 4096], F32, kind="ExternalInput").ap()
    dh2 = nc.dram_tensor("h2t", [32, 128, 4096], F32, kind="ExternalInput").ap()
    dw3 = nc.dram_tensor("w3t", [8, 128, 4096], F32, kind="ExternalInput").ap()
    dsc = nc.dram_tensor("scl", [128, 137], F32, kind="ExternalInput").ap()
    dcst = nc.dram_tensor("cst", [128, 8], F32, kind="ExternalInput").ap()
    dmcat = nc.dram_tensor("mcat", [128, 256], F32, kind="ExternalInput").ap()
    dmaskP0 = nc.dram_tensor("maskP0", [128, 256], BF, kind="ExternalInput").ap()
    dmaskP1 = nc.dram_tensor("maskP1", [128, 256], BF, kind="ExternalInput").ap()
    dident = nc.dram_tensor("ident", [128, 128], BF, kind="ExternalInput").ap()
    dones = nc.dram_tensor("onesv", [128, 4], F32, kind="ExternalInput").ap()
    donesb = nc.dram_tensor("onesb", [128, 1], BF, kind="ExternalInput").ap()
    donesr = nc.dram_tensor("onesr", [2, 128], F32, kind="ExternalInput").ap()
    dhbc = nc.dram_tensor("hbc", [2, 128], F32, kind="ExternalInput").ap()
    dmu = nc.dram_tensor("muv", [128, 8], F32, kind="ExternalInput").ap()
    dnu = nc.dram_tensor("nuv", [128, 64], F32, kind="ExternalInput").ap()

    dy = nc.dram_tensor("y", [D_MODEL, seg], F32, kind="ExternalOutput").ap()
    if DEBUG:
        ddbg = {n: nc.dram_tensor("dbg_" + n, s, d, kind="ExternalOutput").ap()
                for n, s, d in (
                    ("xn", [128, 8 * seg], BF), ("qg0", [128, 4 * seg], BF),
                    ("kfm", [128, 2 * seg], BF), ("vtm", [128, 2 * seg], BF),
                    ("g1aug", [32, seg], BF), ("og0", [128, 4 * seg], BF),
                    ("og1", [128, 4 * seg], BF),
                    ("sf", [128, 128], F32), ("eGg0", [128, seg], BF),
                    ("gath", [128, 8 * 132], F32), ("sin", [128, 128], BF),
                    ("g1e", [16, 2048], F32), ("g1a2", [32, seg], BF),
                    ("w8", [128, 16], F32), ("muv", [128, 8], F32),
                    ("w8h", [128, 16], F32), ("qg1", [128, 4 * seg], BF),
                    ("ch5", [128, 1152], F32),
                    ("resid", [D_MODEL, seg], F32), ("gtm0", [128, 256], F32))}

    dresid = nc.dram_tensor("resid", [D_MODEL, seg], F32, kind="Internal").ap()
    dccin = nc.dram_tensor("ccin", [128, 132], F32, kind="Internal").ap()
    dccout = nc.dram_tensor("ccout", [NCORE * 128, 132], F32, kind="Internal").ap()

    ctx = Ctx(self_edges=SIM_SELF_EDGES)
    es = contextlib.ExitStack()
    sb = lambda name, shape, dt: es.enter_context(nc.sbuf_tensor(name, shape, dt))

    xn = sb("xn", [128, 8 * seg], BF)
    SEGQ = seg * 4
    HW = max(32 * TP, 8 * 1024 + 2 * SEGQ + 2 * seg + 4096)
    ha = sb("ha", [128, HW], BF)
    hb = sb("hb", [128, HW], BF)
    wo_b = ha[:, 0:8 * 1024]
    og = [ha[:, 8 * 1024 + j * SEGQ: 8 * 1024 + (j + 1) * SEGQ] for j in range(2)]
    kfm = ha[:, 8 * 1024 + 2 * SEGQ: 8 * 1024 + 2 * SEGQ + 2 * seg]
    eGg = [ha[:, 8 * 1024 + 2 * SEGQ + 2 * seg + j * seg:
              8 * 1024 + 2 * SEGQ + 2 * seg + (j + 1) * seg] for j in range(2)]
    wq_b = hb[:, 0:8 * 1024]
    qg = [hb[:, 8 * 1024 + j * SEGQ: 8 * 1024 + (j + 1) * SEGQ] for j in range(2)]
    vtm = hb[:, 8 * 1024 + 2 * SEGQ: 8 * 1024 + 2 * SEGQ + 2 * seg]
    wk_b = hb[:, 8 * 1024 + 2 * SEGQ + 2 * seg: 8 * 1024 + 2 * SEGQ + 2 * seg + 2048]
    wv_b = hb[:, 8 * 1024 + 2 * SEGQ + 2 * seg + 2048: 8 * 1024 + 2 * SEGQ + 2 * seg + 4096]
    sbf_all = hb[:, 0:NCH * 128]
    EGH0 = NCH * 128
    eGhat = [hb[:, EGH0 + j * seg: EGH0 + (j + 1) * seg] for j in range(2)]

    af = sb("af", [128, 5120], F32)
    wst = [af[:, 0:1024], af[:, 1024:2048]]
    xstg = [af[:, 2048:2560], af[:, 2560:3072]]
    sqscr = af[:, 3072:3584]
    rowbuf = af[:, 3584:5120]
    gtm = [af[:, 2048 + i * 256: 2048 + (i + 1) * 256] for i in range(4)]
    gscr = [af[:, 3072 + i * 256: 3072 + (i + 1) * 256] for i in range(2)]
    exg = [[af[:, 3584 + p * 384 + i * 64: 3584 + p * 384 + (i + 1) * 64] for i in range(6)]
           for p in range(2)]
    pref = af[:, 4352:4360]
    sf32 = [af[:, 4360 + j * 64: 4360 + (j + 1) * 64] for j in range(2)]
    eprefs = af[:, 4488:4552]
    decays = af[:, 4552:4616]
    blob = af[:, 0:132]
    gath = af[:, 132:1188]
    sin = [af[:, 1188 + i * 64: 1188 + (i + 1) * 64] for i in range(4)]
    w8 = af[:, 1444:1460]
    sqo = af[:, 2048:2560]
    orow = af[:, 2560:3584]
    xwo = [af[:, 4096 + i * 512: 4096 + (i + 1) * 512] for i in range(2)]
    silu_s = [af[:, 2048 + i * 512: 2048 + (i + 1) * 512] for i in range(2)]
    ysb = [af[:, 3072 + i * 512: 3072 + (i + 1) * 512] for i in range(2)]
    rstg = [af[:, 4096 + i * 512: 4096 + (i + 1) * 512] for i in range(2)]

    ab = sb("ab", [128, 8192], BF)
    wbf = [ab[:, 0:4096], ab[:, 4096:8192]]
    g1aug = ab[0:32, 0:seg]
    wg1_b = ab[:, 2048:2176]
    wg2a_b = ab[0:32, 2176:2432]
    qs = [[ab[:, 2432 + p * 512 + j * 256: 2432 + p * 512 + (j + 1) * 256] for j in range(2)]
          for p in range(2)]
    ks = [[ab[:, 3456 + p * 256 + j * 128: 3456 + p * 256 + (j + 1) * 128] for j in range(2)]
          for p in range(2)]
    kppf = [[ab[:, 3968 + p * 256 + j * 128: 3968 + p * 256 + (j + 1) * 128] for j in range(2)]
            for p in range(2)]
    kppt = [[ab[:, 4480 + p * 256 + j * 128: 4480 + p * 256 + (j + 1) * 128] for j in range(2)]
            for p in range(2)]
    pscr = [[ab[:, 4992 + p * 1024 + g * 256: 4992 + p * 1024 + (g + 1) * 256] for g in range(4)]
            for p in range(2)]
    stot = [[ab[:, 7168 + p * 128 + j * 64: 7168 + p * 128 + (j + 1) * 64] for j in range(2)]
            for p in range(2)]
    maskPh = [ab[:, 7424 + h * 256: 7424 + (h + 1) * 256] for h in range(2)]

    scl = sb("scl_s", [128, 137], F32)
    cst = sb("cst_s", [128, 8], F32)
    mcat2 = sb("mcat2_s", [128, 256], F32)
    ident = sb("ident_s", [128, 128], BF)
    onesv = sb("onesv_s", [128, 4], F32)
    onesb = sb("onesb_s", [128, 1], BF)
    onesr = sb("onesr_s", [2, 128], F32)
    hbc = sb("hbc_s", [2, 128], F32)
    muv = sb("muv_s", [128, 8], F32)
    nuv = sb("nuv_s", [128, 64], F32)

    pb = [es.enter_context(nc.psum_tensor(f"p{i}", [128, 512], F32)) for i in range(6)]
    pTs = [es.enter_context(nc.psum_tensor(f"pT{i}", [128, 1024], BF)) for i in range(2)]

    for s in ("sWD0", "sWD1", "sCN", "sXD0", "sXD1", "sVD0", "sVD1",
              "sCS", "sMM", "sSC", "sEV", "sCC"):
        ctx.sems[s] = es.enter_context(nc.semaphore(s))

    banks = Banks(ctx, pb[:4])

    def dma(eng, sem, out, in_, waits=()):
        return ctx.emit(eng, list(waits), lambda e, o=out, i=in_: e.dma_start(out=o, in_=i), [(sem, 16)],
                        is_dma=True)

    def mm_group(parts, bank_i, extra_waits=(), region=None):
        w = banks.mm_waits(bank_i) + list(extra_waits)
        n = len(parts)
        for idx, (lh, rh) in enumerate(parts):
            incs = [("sMM", 1)] if idx == n - 1 else []
            out = region if region is not None else pb[bank_i][:, :]
            ctx.emit("tensor", w if idx == 0 else [],
                     lambda e, o=out, l=lh, r=rh, st=(idx == 0), sp=(idx == n - 1):
                     e.matmul(o, l, r, start=st, stop=sp), incs)
        banks.set_ready(bank_i, "sMM", ctx.ctr["sMM"])
        return ctx.ctr["sMM"]

    for out, i in ((scl, dsc), (cst, dcst), (mcat2, dmcat), (maskPh[0], dmaskP0),
                   (maskPh[1], dmaskP1), (ident, dident), (onesv, dones), (onesb, donesb),
                   (onesr, donesr), (hbc, dhbc), (muv, dmu), (nuv, dnu)):
        dma("sync", "sCN", out[:, :], i[:, :])
    consts_wd = ctx.ctr["sCN"]

    for kc in range(8):
        dma("gpsimd", "sWD0", wq_b[:, kc * 1024:(kc + 1) * 1024], dwq[kc, :, :])
        dma("gpsimd", "sWD0", wo_b[:, kc * 1024:(kc + 1) * 1024], dwo[kc, :, :])
        dma("gpsimd", "sWD0", wk_b[:, kc * 256:(kc + 1) * 256], dwk[kc, :, :])
        dma("gpsimd", "sWD0", wv_b[:, kc * 256:(kc + 1) * 256], dwv[kc, :, :])
        dma("gpsimd", "sWD0", wg1_b[:, kc * 16:(kc + 1) * 16], dwg1[kc, :, :])
    dma("gpsimd", "sWD0", wg2a_b[0:17, :], dwg2a[:, :])
    attn_w_wd = ctx.ctr["sWD0"]

    ctx.emit("vector", [], lambda e: e.memset(g1aug[:, :], 1.0), [("sEV", 1)])
    g1_memset_ev = ctx.ctr["sEV"]

    xb = [ha[:, 8 * 1024 + q * 4096: 8 * 1024 + (q + 1) * 4096] for q in range(2)]
    sqs = [sqscr, af[:, 640:1152]]
    xb_free = [[], []]
    last_xd = [0, 0]
    xdmas = {}
    S1 = {}
    sq_mm_prev = [0, 0]
    prev_tt_mm = [0]
    rows_guard = [0]

    def p1_dma(tt):
        q = tt % 2
        xds = []
        for kc in range(8):
            sl = kc % 2
            sem = "sXD%d" % sl
            xd = dma("gpsimd", sem, xb[q][:, kc * 512:(kc + 1) * 512],
                     dx[kc * 128:(kc + 1) * 128, tt * 512:(tt + 1) * 512],
                     waits=[(sem, last_xd[sl])] + (xb_free[q] if kc < 2 else []) +
                           ([("sCN", consts_wd)] if (tt == 0 and kc == 0) else []))[sem]
            last_xd[sl] = xd
            xds.append((sem, xd))
        xdmas[tt] = xds

    def p1_stats(tt):
        q = tt % 2
        first = True
        for kc in range(8):
            sl = kc % 2
            sem, xd = xdmas[tt][kc]
            ctx.emit("tensor", [(sem, xd)] + ([("sSC", rows_guard[0])] if first else []),
                     lambda e, q=q, kc=kc, st=first, sp=(kc == 7):
                     e.matmul(pb[4][0:1, :], onesb[:, 0:1], xb[q][:, kc * 512:(kc + 1) * 512],
                              start=st, stop=sp), [])
            sc_ = ctx.emit("scalar", [(sem, xd), ("sMM", sq_mm_prev[sl])],
                           lambda e, q=q, kc=kc, sl=sl:
                           e.activation(sqs[sl][:, :], xb[q][:, kc * 512:(kc + 1) * 512], AF.Square),
                           [("sSC", 1)])["sSC"]
            mmres = ctx.emit("tensor", [("sSC", sc_)],
                             lambda e, sl=sl, st=first, sp=(kc == 7):
                             e.matmul(pb[5][0:1, :], onesv[:, 0:1], sqs[sl][:, :], start=st, stop=sp),
                             [("sMM", 1)])["sMM"]
            sq_mm_prev[sl] = mmres
            first = False
        S1[tt] = dict(mm_stats=ctx.ctr["sMM"])

    def p1_rows(tt):
        d = S1[tt]
        ctx.emit("scalar", [("sMM", d["mm_stats"])],
                 lambda e: e.mul(rowbuf[0:1, 0:512], pb[4][0:1, :], 1.0 / 1024.0), [("sSC", 1)])
        sc_mu = ctx.ctr["sSC"]
        ctx.emit("scalar", [],
                 lambda e: e.mul(rowbuf[0:1, 512:1024], pb[5][0:1, :], 1.0 / 1024.0), [("sSC", 1)])
        sc_ex2 = ctx.ctr["sSC"]
        rows_guard[0] = sc_ex2
        ctx.emit("vector", [("sSC", sc_ex2), ("sMM", prev_tt_mm[0])],
                 lambda e: e.tensor_mul(rowbuf[0:1, 1024:1536], rowbuf[0:1, 0:512], rowbuf[0:1, 0:512]),
                 [("sEV", 1)])
        ctx.emit("vector", [],
                 lambda e: e.tensor_sub(rowbuf[0:1, 512:1024], rowbuf[0:1, 512:1024], rowbuf[0:1, 1024:1536]),
                 [("sEV", 1)])
        ev_var = ctx.ctr["sEV"]
        ctx.emit("scalar", [("sEV", ev_var)],
                 lambda e: e.activation(rowbuf[0:1, 512:1024], rowbuf[0:1, 512:1024], AF.Ln,
                                        bias=cst[0:1, 0:1]), [("sSC", 1)])
        ctx.emit("scalar", [],
                 lambda e: e.activation(rowbuf[0:1, 1024:1536], rowbuf[0:1, 512:1024], AF.Exp, scale=-0.5),
                 [("sSC", 1)])
        d["sc_a"] = ctx.ctr["sSC"]
        ctx.emit("vector", [("sSC", d["sc_a"])],
                 lambda e: e.tensor_mul(rowbuf[0:1, 0:512], rowbuf[0:1, 0:512], rowbuf[0:1, 1024:1536]),
                 [("sEV", 1)])
        d["ev_b"] = ctx.ctr["sEV"]

    def p1_bcast(tt):
        d = S1[tt]
        bA2 = banks.take()
        d["mm_a"] = mm_group([(onesr[0:1, :], rowbuf[0:1, 1024:1536])], bA2, extra_waits=[("sSC", d["sc_a"])])
        bB2 = banks.take()
        d["mm_b"] = mm_group([(onesr[0:1, :], rowbuf[0:1, 0:512])], bB2, extra_waits=[("sEV", d["ev_b"])])
        d["bA2"], d["bB2"] = bA2, bB2
        prev_tt_mm[0] = d["mm_b"]

    def p1_pass2(tt):
        q = tt % 2
        d = S1[tt]
        bA2, bB2 = d["bA2"], d["bB2"]
        ptmp = af[:, 4096:4608]
        for kc in range(8):
            ctx.emit("vector", [("sMM", d["mm_a"])] if kc == 0 else [],
                     lambda e, q=q, kc=kc, bA2=bA2, ptmp=ptmp:
                     e.tensor_tensor(ptmp[:, :], xb[q][:, kc * 512:(kc + 1) * 512],
                                     pb[bA2][:, :], ALU.mult), [("sEV", 1)])
            ctx.emit("vector", [("sMM", d["mm_b"])] if kc == 0 else [],
                     lambda e, kc=kc, tt=tt, bB2=bB2, ptmp=ptmp:
                     e.tensor_tensor(xn[:, kc * seg + tt * 512: kc * seg + tt * 512 + 512],
                                     ptmp[:, :], pb[bB2][:, :], ALU.subtract), [("sEV", 1)])
        d["p2ev"] = ctx.ctr["sEV"]
        xb_free[q] = [("sEV", d["p2ev"])]
        banks.add_free(bA2, "sEV", ctx.ctr["sEV"])
        banks.add_free(bB2, "sEV", ctx.ctr["sEV"])

    def p1_proj(tt):
        pw = [("sEV", S1[tt]["p2ev"]), ("sWD0", attn_w_wd), ("sEV", g1_memset_ev)]
        for m in range(8):
            j, hp = divmod(m, 4)
            b = banks.take()
            mm_group([(wq_b[:, kc * 1024 + m * 128: kc * 1024 + (m + 1) * 128],
                       xn[:, kc * seg + tt * 512: kc * seg + (tt) * 512 + 512]) for kc in range(8)], b,
                     extra_waits=pw)
            sc_ = ctx.emit("scalar", banks.evac_wait(b),
                           lambda e, j=j, hp=hp, tt=tt, b=b:
                           e.activation(qg[j][:, hp * seg + tt * 512: hp * seg + tt * 512 + 512],
                                        pb[b][:, :], AF.Relu, scale=0.125), [("sSC", 1)])["sSC"]
            banks.add_free(b, "sSC", sc_)
        for m in range(2):
            b = banks.take()
            mm_group([(wk_b[:, kc * 256 + m * 128: kc * 256 + (m + 1) * 128],
                       xn[:, kc * seg + tt * 512: kc * seg + tt * 512 + 512]) for kc in range(8)], b,
                     extra_waits=pw)
            sc_ = ctx.emit("scalar", banks.evac_wait(b),
                           lambda e, m=m, tt=tt, b=b:
                           e.activation(kfm[:, m * seg + tt * 512: m * seg + tt * 512 + 512],
                                        pb[b][:, :], AF.Relu), [("sSC", 1)])["sSC"]
            banks.add_free(b, "sSC", sc_)
        for ti in range(4 * tt, 4 * tt + 4):
            b = banks.take()
            mm_group([(xn[:, kc * seg + ti * 128: kc * seg + (ti + 1) * 128],
                       wv_b[:, kc * 256:(kc + 1) * 256]) for kc in range(8)], b,
                     region=pb[b][:, 0:256], extra_waits=pw)
            ev = ctx.emit("vector", banks.evac_wait(b),
                          lambda e, ti=ti, b=b: e.tensor_copy(vtm[:, ti * 256:(ti + 1) * 256], pb[b][:, 0:256]),
                          [("sEV", 1)])["sEV"]
            banks.add_free(b, "sEV", ev)
        b = banks.take()
        mm_group([(wg1_b[:, kc * 16:(kc + 1) * 16],
                   xn[:, kc * seg + tt * 512: kc * seg + tt * 512 + 512]) for kc in range(8)], b,
                 region=pb[b][0:16, :], extra_waits=pw)
        ev = ctx.emit("vector", banks.evac_wait(b),
                      lambda e, tt=tt, b=b: e.tensor_copy(g1aug[0:16, tt * 512:(tt + 1) * 512], pb[b][0:16, :]),
                      [("sEV", 1)])["sEV"]
        banks.add_free(b, "sEV", ev)

    p1_dma(0)
    for i in range(-1, NT):
        if i + 1 < NT:
            p1_stats(i + 1)
        if i + 2 < NT:
            p1_dma(i + 2)
        if i >= 0:
            p1_proj(i)
        if i + 1 < NT:
            p1_rows(i + 1)
            p1_bcast(i + 1)
            p1_pass2(i + 1)
    if DEBUG:
        ctx.guard("scalar", ctx.barrier_waits())
        dma("scalar", "sVD0", ddbg["xn"][:, :], xn[:, :])
    ln_done = ctx.barrier_waits()

    proj_done = ctx.barrier_waits()


    for e in ("tensor", "vector", "scalar", "gpsimd"):
        ctx.guard(e, proj_done)

    for j in range(2):
        ctx.emit("vector", [], lambda e, j=j: e.memset(sf32[j][:, :], 0.0), [("sEV", 1)])
    for p in range(2):
        for j in range(2):
            ctx.emit("vector", [], lambda e, p=p, j=j: e.memset(ks[p][j][:, :], 0.0), [("sEV", 1)])
    ctx.emit("scalar", [], lambda e: e.memzero(pref[:, :]), [("sSC", 1)])
    init_ev = ctx.ctr["sEV"]
    init_sc = ctx.ctr["sSC"]
    ctx.guard("tensor", [("sEV", init_ev)])
    ctx.guard("scalar", [("sEV", init_ev)])

    gslot_free = [[] for _ in range(4)]
    gslot_ready = [None] * 4
    glog_emitted = 0
    prev_gate_ev = 0

    def emit_gate_ttile(t):
        nonlocal glog_emitted, prev_gate_ev
        sl = t % 4
        b = banks.take()
        mm_group([(g1aug[0:17, t * 128:(t + 1) * 128], wg2a_b[0:17, :])], b,
                 region=pb[b][:, 0:256])
        w0 = banks.evac_wait(b) + gslot_free[sl] + [("sEV", prev_gate_ev)]
        ctx.emit("scalar", w0, lambda e, b=b: e.activation(gscr[0][:, :], pb[b][:, 0:256], AF.Abs),
                 [("sSC", 1)])
        ctx.emit("scalar", [], lambda e: e.activation(gscr[1][:, :], gscr[0][:, :], AF.Exp, scale=-1.0),
                 [("sSC", 1)])
        ctx.emit("scalar", [], lambda e: e.activation(gscr[0][:, :], gscr[1][:, :], AF.Ln, bias=cst[:, 2:3]),
                 [("sSC", 1)])
        sc1 = ctx.ctr["sSC"]
        ev = ctx.emit("vector", [("sSC", sc1)] + banks.evac_wait(b) + gslot_free[sl],
                      lambda e, b=b: e.tensor_scalar_min(gscr[1][:, :], pb[b][:, 0:256], 0.0),
                      [("sEV", 1)])["sEV"]
        banks.add_free(b, "sEV", ev)
        ev2 = ctx.emit("vector", [],
                       lambda e, sl=sl: e.tensor_sub(gtm[sl][:, :], gscr[1][:, :], gscr[0][:, :]),
                       [("sEV", 1)])["sEV"]
        gslot_ready[sl] = ("sEV", ev2)
        prev_gate_ev = ev2
        glog_emitted = t + 1

    for t in range(min(3, NTT)):
        emit_gate_ttile(t)

    factor_ev = {}
    factor_sc = {}
    q_cs = {}
    mm_after_o = {}
    pT_free = [[], []]
    pp = 0
    pgbs = [pb[4], pb[5]]
    pg_free = [[("sSC", rows_guard[0])], [("sSC", rows_guard[0])]]
    pg_ready = [None, None]

    def apf(view, extra_off, dims):
        base = view[:, :] if hasattr(view, "tensor") is False else view
        return bass.AP(tensor=base.tensor, offset=base.offset + extra_off,
                       ap=[list(base.ap[0])] + [list(d) for d in dims])

    hbv = hb[:, :]
    hav = ha[:, :]
    abv = ab[:, :]
    afv = af[:, :]
    QG0 = 8 * 1024
    KFM0 = 8 * 1024 + 2 * SEGQ
    def emit_gate_mm(t):
        nonlocal glog_emitted
        while glog_emitted < min(t + 3, NTT):
            emit_gate_ttile(glog_emitted)
        tb = t % 2
        sl2 = t % 4
        pgb = pgbs[tb]
        for j in range(2):
            w = (pg_free[tb] if j == 0 else []) + [gslot_ready[sl2]]
            pg_free[tb] = []
            ctx.emit("tensor", w,
                     lambda e, j=j, sl2=sl2, pgb=pgb:
                     e.matmul(pgb[:, j * 256:(j + 1) * 256],
                              gtm[sl2][:, j * 128:(j + 1) * 128],
                              mcat2[:, :], start=True, stop=True), [("sMM", 1)])
        pg_ready_t[t] = ("sMM", ctx.ctr["sMM"])
        gslot_free[sl2] = [("sMM", ctx.ctr["sMM"])]

    def emit_pre_sv(c):
        nonlocal pp
        p = c % 2
        t, half = divmod(c, 2)
        tb = t % 2
        pgb = pgbs[tb]
        co = half * 128
        EX0 = 3584 + p * 384
        w_war = [pg_ready_t[t], ("sEV", factor_ev.get(c - 2, 0))]
        if c == 0:
            w_war = w_war + [("sSC", init_sc)]
        pgv = pgb[:, :]
        ctx.emit("scalar", w_war,
                 lambda e, co=co, EX0=EX0, pgv=pgv:
                 e.activation(afv[:, EX0: EX0 + 128], apf(pgv, co, [[256, 2], [1, 64]]),
                              AF.Exp, scale=1.0 / GLN), [("sSC", 1)])
        ctx.emit("scalar", [],
                 lambda e, c=c, EX0=EX0:
                 e.copy(decays[:, c * 2: c * 2 + 2],
                        apf(afv, EX0 + 63, [[64, 2], [1, 1]])), [("sSC", 1)])
        ctx.emit("scalar", [],
                 lambda e, co=co, EX0=EX0, pgv=pgv:
                 e.activation(afv[:, EX0 + 128: EX0 + 256], apf(pgv, co, [[256, 2], [1, 64]]),
                              AF.Exp, scale=-1.0 / GLN), [("sSC", 1)])
        ctx.emit("scalar", [],
                 lambda e, co=co, EX0=EX0, pgv=pgv:
                 e.activation(afv[:, EX0 + 256: EX0 + 384], apf(pgv, co + 64, [[256, 2], [1, 64]]),
                              AF.Exp, scale=1.0 / GLN), [("sSC", 1)])
        ctx.emit("scalar", [],
                 lambda e, c=c, pq=pp:
                 e.activation(eprefs[:, c * 2: c * 2 + 2], apf(afv, 4352 + pq, [[2, 2], [1, 1]]),
                              AF.Exp, scale=1.0 / GLN), [("sSC", 1)])
        for j in range(2):
            ctx.emit("scalar", [], lambda e, j=j, co=co, pq=pp, pgb=pgb:
                     e.add(pref[:, j * 2 + (1 - pq): j * 2 + (1 - pq) + 1],
                           pgb[:, j * 256 + co + 63: j * 256 + co + 64],
                           pref[:, j * 2 + pq: j * 2 + pq + 1]), [("sSC", 1)])
        ctx.emit("scalar", [], lambda e, c=c, EX0=EX0:
                 e.copy(apf(hbv, EGH0 + c * 64, [[seg, 2], [1, 64]]),
                        afv[:, EX0: EX0 + 128]), [("sSC", 1)])
        sc_all = ctx.ctr["sSC"]
        sc_alls[c] = sc_all
        if half == 1:
            pg_free[tb] = [("sSC", sc_all)]
        pp = 1 - pp
        for j in range(2):
            ctx.emit("vector", [("sSC", sc_all), ("sMM", mm_after_P.get(c - 2, 0))] if j == 0 else [],
                     lambda e, j=j, c=c, p=p, EX0=EX0:
                     e.tensor_tensor(apf(abv, 2432 + p * 512 + j * 256, [[64, 4], [1, 64]]),
                                     apf(hbv, QG0 + j * SEGQ + c * 64, [[seg, 4], [1, 64]]),
                                     apf(afv, EX0 + j * 64, [[0, 4], [1, 64]]),
                                     ALU.mult), [("sEV", 1)])
        ctx.emit("vector", [],
                 lambda e, c=c, half=half, p=p, EX0=EX0:
                 e.tensor_tensor(apf(abv, 3456 + p * 256 + half * 64, [[128, 2], [1, 64]]),
                                 apf(hav, KFM0 + c * 64, [[seg, 2], [1, 64]]),
                                 apf(afv, EX0 + 128, [[64, 2], [1, 64]]), ALU.mult),
                 [("sEV", 1)])
        ctx.emit("vector", [], lambda e, half=half, p=p:
                 e.memset(apf(abv, 3968 + p * 256 + (1 - half) * 64, [[128, 2], [1, 64]]), 0.0),
                 [("sEV", 1)])
        ctx.emit("vector", [], lambda e, c=c, half=half, p=p, EX0=EX0:
                 e.tensor_tensor(apf(abv, 3968 + p * 256 + half * 64, [[128, 2], [1, 64]]),
                                 apf(hav, KFM0 + c * 64, [[seg, 2], [1, 64]]),
                                 apf(afv, EX0 + 256, [[64, 2], [1, 64]]), ALU.mult),
                 [("sEV", 1)])
        factor_ev[c] = ctx.ctr["sEV"]

    def emit_pre_t(c):
        p = c % 2
        t, half = divmod(c, 2)
        if half == 0 and t + 1 < NTT:
            emit_gate_mm(t + 1)
        for j in range(2):
            ctx.emit("tensor", ([("sEV", factor_ev[c])] if j == 0 else []) + pT_free[j],
                     lambda e, j=j, p=p: e.transpose(pTs[j][:, 0:128], kppf[p][j][:, :], ident[:, :]),
                     [("sMM", 1)])
            trans_mm[c * 2 + j] = ctx.ctr["sMM"]
        pbanks = []
        for g in range(4):
            j, gh = divmod(g, 2)
            b = banks.take()
            w = banks.mm_waits(b)
            ctx.emit("tensor", w,
                     lambda e, j=j, gh=gh, b=b, p=p:
                     e.matmul(pb[b][:, 0:256], ks[p][j][gh * 64:(gh + 1) * 64, :],
                              qs[p][j][gh * 64:(gh + 1) * 64, :], start=True, stop=True),
                     [("sMM", 1)])
            banks.set_ready(b, "sMM", ctx.ctr["sMM"])
            pbanks.append(b)
        P_banks[c] = pbanks
        mm_after_P[c] = ctx.ctr["sMM"]

    def emit_kppt_evac(c):
        p = c % 2
        for j in range(2):
            sc = ctx.emit("scalar", [("sMM", trans_mm[c * 2 + j])],
                          lambda e, j=j, p=p: e.copy(kppt[p][j][:, :], pTs[j][:, 0:128]),
                          [("sSC", 1)])["sSC"]
            pT_free[j] = [("sSC", sc)]
        kppt_ev[c] = ctx.ctr["sSC"]

    def emit_mask(c):
        p = c % 2
        t, half = divmod(c, 2)
        for g in range(4):
            j, gh = divmod(g, 2)
            b = P_banks[c][g]
            w = banks.evac_wait(b) + ([("sMM", mm_after_o.get(c - 2, 0))] if g == 0 else [])
            ev = ctx.emit("vector", w,
                          lambda e, j=j, gh=gh, b=b, half=half, p=p:
                          e.tensor_tensor(abv[:, 4992 + p * 1024 + j * 512 + gh * 256:
                                              4992 + p * 1024 + j * 512 + (gh + 1) * 256],
                                          pb[b][:, 0:256],
                                          maskPh[half][:, :], ALU.mult), [("sEV", 1)])["sEV"]
            banks.add_free(b, "sEV", ev)
        mask_ev[c] = ctx.ctr["sEV"]

    def emit_tail_t(c):
        p = c % 2
        t, half = divmod(c, 2)
        obanks = []
        for j in range(2):
            b = banks.take()
            w = banks.mm_waits(b) + ([("sEV", mask_ev[c])] if j == 0 else [])
            for gh in range(2):
                g = 2 * j + gh
                vsl = vtm[:, t * 256 + g * 64: t * 256 + (g + 1) * 64]
                tp1 = (0, gh * 64) if gh else None
                ctx.emit("tensor", w if gh == 0 else [],
                         lambda e, b=b, vsl=vsl, j=j, gh=gh, tp1=tp1, p=p:
                         e.matmul(pb[b][gh * 64:(gh + 1) * 64, 0:256], vsl,
                                  abv[:, 4992 + p * 1024 + j * 512 + gh * 256:
                                       4992 + p * 1024 + j * 512 + (gh + 1) * 256],
                                  start=True, stop=True, tile_position=tp1),
                         [("sMM", 1)] if gh == 1 else [])
            banks.set_ready(b, "sMM", ctx.ctr["sMM"])
            obanks.append(b)
        sbanks = []
        for j in range(2):
            b = banks.take()
            w = banks.mm_waits(b) + ([("sSC", kppt_ev[c])] if j == 0 else [])
            for gh in range(2):
                g = 2 * j + gh
                vsl = vtm[:, t * 256 + g * 64: t * 256 + (g + 1) * 64]
                tp = (0, gh * 64) if gh else None
                ctx.emit("tensor", w if gh == 0 else [],
                         lambda e, b=b, j=j, gh=gh, vsl=vsl, tp=tp, p=p:
                         e.matmul(pb[b][gh * 64:(gh + 1) * 64, 0:64],
                                  kppt[p][j][:, gh * 64:(gh + 1) * 64],
                                  vsl, start=True, stop=True, tile_position=tp),
                         [("sMM", 1)] if gh == 1 else [])
            banks.set_ready(b, "sMM", ctx.ctr["sMM"])
            sbanks.append(b)
        mm_after_o[c] = ctx.ctr["sMM"]
        tail_banks[c] = (obanks, sbanks)

    def emit_tail_v(c):
        obanks, sbanks = tail_banks[c]
        for j in range(2):
            b = obanks[j]
            dst = og[j].rearrange("p (hp t) -> p hp t", hp=4)[:, :, c * 64:(c + 1) * 64]
            oeng, osem = (("scalar", "sSC") if j == 0 else ("vector", "sEV"))
            ov = ctx.emit(oeng, banks.evac_wait(b),
                          (lambda e, b=b, dst=dst:
                           e.copy(dst, pb[b][:, 0:256].rearrange("p (hp t) -> p hp t", hp=4)))
                          if j == 0 else
                          (lambda e, b=b, dst=dst:
                           e.tensor_copy(dst, pb[b][:, 0:256].rearrange("p (hp t) -> p hp t", hp=4))),
                          [(osem, 1)])[osem]
            banks.add_free(b, osem, ov)
        for j in range(2):
            b = sbanks[j]
            ev = ctx.emit("vector", banks.evac_wait(b) +
                          ([("sSC", sbf_sc.get(c - 1, 0))] if j == 0 else []),
                          lambda e, b=b, j=j, c=c:
                          e.scalar_tensor_tensor(sf32[j][:, :], sf32[j][:, :],
                                                 decays[:, c * 2 + j: c * 2 + j + 1],
                                                 pb[b][:, 0:64],
                                                 ALU.mult, ALU.add), [("sEV", 1)])["sEV"]
            banks.add_free(b, "sEV", ev)
        stt_ev[c] = ctx.ctr["sEV"]

    def emit_sbf_copy(c):
        ctx.emit("scalar", [("sEV", stt_ev[c])], lambda e, c=c:
                 e.copy(sbf_all[:, c * 128: (c + 1) * 128],
                        afv[:, 4360: 4360 + 128]), [("sSC", 1)])
        sbf_sc[c] = ctx.ctr["sSC"]

    sc_alls = {}
    sbf_sc = {}
    mm_after_P = {}
    kppt_ev = {}
    mask_ev = {}
    P_banks = {}
    pg_ready_t = {}
    tail_banks = {}
    stt_ev = {}
    q_cs = {}
    trans_mm = {}
    emit_gate_mm(0)
    for c in range(NCH):
        emit_pre_sv(c)
        if c >= 1:
            emit_kppt_evac(c - 1)
        if c >= 2:
            emit_sbf_copy(c - 2)
        if c >= 1:
            emit_tail_t(c - 1)
            emit_tail_v(c - 1)
        emit_pre_t(c)
        emit_mask(c)
    emit_kppt_evac(NCH - 1)
    emit_sbf_copy(NCH - 2)
    emit_tail_t(NCH - 1)
    emit_tail_v(NCH - 1)
    emit_sbf_copy(NCH - 1)

    chunk_done = ctx.barrier_waits()
    for e in ("vector", "gpsimd", "scalar"):
        ctx.guard(e, chunk_done)
    ctx.emit("vector", [], lambda e: e.memset(blob[:, :], 0.0), [("sEV", 1)])
    for j in range(2):
        ctx.emit("vector", [], lambda e, j=j: e.tensor_copy(blob[:, j * 64:(j + 1) * 64], sf32[j][:, :]),
                 [("sEV", 1)])
        ctx.emit("vector", [], lambda e, j=j, p=pp:
                 e.tensor_copy(blob[:, 128 + j:129 + j], pref[:, j * 2 + p: j * 2 + p + 1]), [("sEV", 1)])
    ev_blob = ctx.ctr["sEV"]
    xd = dma("gpsimd", "sXD0", dccin[:, :], blob[:, :], waits=[("sEV", ev_blob)])["sXD0"]
    cc = ctx.emit("gpsimd", [("sXD0", xd)],
                  lambda e: e.collective_compute("AllGather", ALU.bypass,
                                                 replica_groups=[list(range(NCORE))],
                                                 ins=[dccin[:, :]], outs=[dccout[:, :]]),
                  [("sCC", 1)], is_dma=True)["sCC"]
    xd = dma("gpsimd", "sXD0", gath[:, :].rearrange("p (r c) -> p r c", r=8),
             dccout.rearrange("(r p) c -> p r c", p=128), waits=[("sCC", cc)])["sXD0"]
    gath_xd = xd
    gath3 = gath[:, :].rearrange("p (r c) -> p r c", r=8)
    sumg_all = [gath3[:, :, 128 + j] for j in range(2)]
    for j in range(2):
        ctx.emit("vector", [("sXD0", gath_xd)],
                 lambda e, j=j: e.memset(sin[j * 2][:, :], 0.0), [("sEV", 1)])
    par = [0, 0]
    for i in range(NCORE):
        ctx.emit("vector", [("sXD0", gath_xd)],
                 lambda e, i=i: e.tensor_tensor(w8[:, 0:8], sumg_all[0],
                                                nuv[:, i * 8:(i + 1) * 8], ALU.mult), [("sEV", 1)])
        ctx.guard("vector", [("sEV", ctx.ctr["sEV"])])
        ctx.emit("vector", [],
                 lambda e: e.tensor_reduce(w8[:, 8:9], w8[:, 0:8], AX.X, ALU.add), [("sEV", 1)])
        ctx.guard("vector", [("sEV", ctx.ctr["sEV"])])
        ctx.emit("vector", [],
                 lambda e, i=i: e.tensor_tensor(w8[:, 0:8], sumg_all[1],
                                                nuv[:, i * 8:(i + 1) * 8], ALU.mult), [("sEV", 1)])
        ctx.guard("vector", [("sEV", ctx.ctr["sEV"])])
        ctx.emit("vector", [],
                 lambda e: e.tensor_reduce(w8[:, 9:10], w8[:, 0:8], AX.X, ALU.add), [("sEV", 1)])
        ctx.guard("vector", [("sEV", ctx.ctr["sEV"])])
        ev_z = ctx.ctr["sEV"]
        ctx.emit("scalar", [("sEV", ev_z)],
                 lambda e: e.activation(w8[:, 10:12], w8[:, 8:10], AF.Exp, scale=1.0 / GLN), [("sSC", 1)])
        sc_w = ctx.ctr["sSC"]
        ev_w = ctx.emit("vector", [("sSC", sc_w)],
                        lambda e, i=i: e.tensor_scalar_mul(w8[:, 12:14], w8[:, 10:12], muv[:, i:i + 1]),
                        [("sEV", 1)])["sEV"]
        ctx.guard("vector", [("sEV", ev_w)])
        for j in range(2):
            p = par[j]
            ctx.emit("vector", [],
                     lambda e, i=i, j=j, p=p:
                     e.scalar_tensor_tensor(sin[j * 2 + (1 - p)][:, :],
                                            gath[:, i * 132 + j * 64: i * 132 + (j + 1) * 64],
                                            w8[:, 12 + j:13 + j],
                                            sin[j * 2 + p][:, :], ALU.mult, ALU.add), [("sEV", 1)])
            ctx.guard("vector", [("sEV", ctx.ctr["sEV"])])
            par[j] = 1 - p
    sin_fin = [sin[0 + par[0]], sin[2 + par[1]]]
    if DEBUG:
        ctx.guard("scalar", ctx.barrier_waits())
        dma("scalar", "sVD0", ddbg["gath"][:, :], gath[:, :])
        dma("scalar", "sVD0", ddbg["w8"][:, :], w8[:, :])
        dma("scalar", "sVD0", ddbg["muv"][:, :], muv[:, :])

    comb_done = ctx.barrier_waits()

    for e in ("tensor", "vector", "gpsimd"):
        ctx.guard(e, comb_done)
    corr_mm = {}
    corr_banks = {}

    def corr_chunk(c):
        p = c % 2
        ctx.emit("vector", [("sMM", corr_mm.get(c - 2, 0))],
                 lambda e, c=c, p=p:
                 e.tensor_tensor(apf(abv, 2432 + p * 512, [[256, 2], [64, 4], [1, 64]]),
                                 apf(hbv, QG0 + c * 64, [[SEGQ, 2], [seg, 4], [1, 64]]),
                                 apf(hbv, EGH0 + c * 64, [[seg, 2], [0, 4], [1, 64]]),
                                 ALU.mult), [("sEV", 1)])
        ev_q3 = ctx.ctr["sEV"]
        for j in range(2):
            if c == 0:
                ctx.emit("vector", [],
                         lambda e, j=j: e.tensor_scalar_mul(stot[0][j][:, :], sin_fin[j][:, :],
                                                            eprefs[:, j: j + 1]), [("sEV", 1)])
            else:
                ctx.emit("vector", [("sMM", corr_mm.get(c - 2, 0))] if j == 0 else [],
                         lambda e, j=j, c=c, p=p:
                         e.scalar_tensor_tensor(stot[p][j][:, :], sin_fin[j][:, :],
                                                eprefs[:, c * 2 + j: c * 2 + j + 1],
                                                sbf_all[:, (c - 1) * 128 + j * 64: (c - 1) * 128 + (j + 1) * 64],
                                                ALU.mult, ALU.add), [("sEV", 1)])
        ev_st = ctx.ctr["sEV"]
        cbanks = []
        for j in range(2):
            b = banks.take()
            w = banks.mm_waits(b) + ([("sEV", ev_st)] if j == 0 else [])
            for gh in range(2):
                tpc = (gh * 64, gh * 64) if gh else None
                ctx.emit("tensor", w if gh == 0 else [],
                         lambda e, b=b, j=j, gh=gh, tpc=tpc, p=p:
                         e.matmul(pb[b][gh * 64:(gh + 1) * 64, 0:256],
                                  stot[p][j][gh * 64:(gh + 1) * 64, :],
                                  qs[p][j][gh * 64:(gh + 1) * 64, :], start=True, stop=True,
                                  tile_position=tpc), [("sMM", 1)] if gh == 1 else [])
            banks.set_ready(b, "sMM", ctx.ctr["sMM"])
            cbanks.append(b)
        corr_mm[c] = ctx.ctr["sMM"]
        corr_banks[c] = cbanks

    def _corr_add(c):
        for j in range(2):
            b = corr_banks[c][j]
            dst = og[j].rearrange("p (hp t) -> p hp t", hp=4)[:, :, c * 64:(c + 1) * 64]
            ev = ctx.emit("vector", banks.evac_wait(b),
                          lambda e, b=b, dst=dst:
                          e.tensor_tensor(dst, dst, pb[b][:, 0:256].rearrange("p (hp t) -> p hp t", hp=4),
                                          ALU.add), [("sEV", 1)])["sEV"]
            banks.add_free(b, "sEV", ev)

    for c in range(NCH):
        corr_chunk(c)
        if c >= 1:
            _corr_add(c - 1)
    _corr_add(NCH - 1)
    corr_done = ctx.barrier_waits()

    for e in ("tensor", "vector", "scalar"):
        ctx.guard(e, corr_done)
    sqo_s = [af[:, 2048:2560], af[:, 2560:3072]]
    lnrow = af[:, 3072:3584]
    exp_s = [af[:, 3584:4096], af[:, 4096:4608]]
    var_mm = {}
    bc_mm = {}
    on_banks = {}

    def on_sq(n):
        j, ti = divmod(n, 4 * NT)
        sc_sq = ctx.emit("scalar", [("sMM", var_mm.get(n - 2, 0))],
                         lambda e, j=j, ti=ti, n=n:
                         e.activation(sqo_s[n % 2][:, :], og[j][:, ti * 512:(ti + 1) * 512],
                                      AF.Square), [("sSC", 1)])["sSC"]
        b = banks.take()
        w = banks.mm_waits(b) + [("sSC", sc_sq)]
        ctx.emit("tensor", w,
                 lambda e, b=b, n=n: e.matmul(pb[b][0:2, :], onesv[:, 1:3], sqo_s[n % 2][:, :],
                                              start=True, stop=True), [("sMM", 1)])
        banks.set_ready(b, "sMM", ctx.ctr["sMM"])
        var_mm[n] = ctx.ctr["sMM"]
        on_banks[n] = b

    def on_fin(n):
        j, ti = divmod(n, 4 * NT)
        b = on_banks[n]
        ctx.emit("scalar", banks.evac_wait(b),
                 lambda e, b=b: e.activation(lnrow[0:2, 0:512], pb[b][0:2, :], AF.Ln,
                                             scale=1.0 / 64.0, bias=cst[0:2, 1:2]), [("sSC", 1)])
        sc_ln = ctx.ctr["sSC"]
        banks.add_free(b, "sSC", sc_ln)
        ctx.emit("scalar", [("sMM", bc_mm.get(n - 2, 0))],
                 lambda e, n=n: e.activation(exp_s[n % 2][0:2, 0:512], lnrow[0:2, 0:512],
                                             AF.Exp, scale=-0.5), [("sSC", 1)])
        sc_r = ctx.ctr["sSC"]
        b2 = banks.take()
        w2 = banks.mm_waits(b2) + [("sSC", sc_r)]
        ctx.emit("tensor", w2,
                 lambda e, b2=b2, n=n: e.matmul(pb[b2][:, :], hbc[:, :], exp_s[n % 2][0:2, 0:512],
                                                start=True, stop=True), [("sMM", 1)])
        banks.set_ready(b2, "sMM", ctx.ctr["sMM"])
        bc_mm[n] = ctx.ctr["sMM"]
        ev = ctx.emit("vector", banks.evac_wait(b2),
                      lambda e, j=j, ti=ti, b2=b2:
                      e.scalar_tensor_tensor(og[j][:, ti * 512:(ti + 1) * 512],
                                             og[j][:, ti * 512:(ti + 1) * 512],
                                             scl[:, 136:137], pb[b2][:, :],
                                             ALU.mult, ALU.mult), [("sEV", 1)])["sEV"]
        banks.add_free(b2, "sEV", ev)

    NON = 8 * NT
    for n in range(NON):
        on_sq(n)
        if n >= 1:
            on_fin(n - 1)
    on_fin(NON - 1)
    onorm_done = ctx.barrier_waits()

    for e in ("tensor", "vector", "gpsimd"):
        ctx.guard(e, onorm_done)
    xw_free = [[], []]
    yw_free = [[], []]
    for mb in range(8):
        for tt in range(NT):
            sl = (mb * NT + tt) % 2
            sem = "sXD%d" % sl
            xd = dma("gpsimd", sem, xwo[sl][:, :], dx[mb * 128:(mb + 1) * 128, tt * 512:(tt + 1) * 512],
                     waits=xw_free[sl])[sem]
            b = banks.take()
            parts = []
            for kc in range(8):
                j, hp = divmod(kc, 4)
                parts.append((wo_b[:, kc * 1024 + mb * 128: kc * 1024 + (mb + 1) * 128],
                              og[j][:, hp * seg + tt * 512: hp * seg + tt * 512 + 512]))
            mm_group(parts, b)
            ev = ctx.emit("vector", banks.evac_wait(b) + [(sem, xd)] + yw_free[sl],
                          lambda e, sl=sl, b=b: e.tensor_tensor(ysb[sl][:, :], xwo[sl][:, :], pb[b][:, :],
                                                                ALU.add), [("sEV", 1)])["sEV"]
            banks.add_free(b, "sEV", ev)
            xw_free[sl] = [("sEV", ev)]
            vsem = "sVD%d" % sl
            st = ctx.emit("scalar", [("sEV", ev)],
                          lambda e, sl=sl, mb=mb, tt=tt:
                          e.dma_start(out=dresid[mb * 128:(mb + 1) * 128, tt * 512:(tt + 1) * 512],
                                      in_=ysb[sl][:, :]), [(vsem, 16)], is_dma=True)[vsem]
            yw_free[sl] = [(vsem, st)]
    if DEBUG:
        ctx.guard("scalar", ctx.barrier_waits())
        dma("scalar", "sVD0", ddbg["resid"][:, :], dresid[:, :])
    wo_done = ctx.barrier_waits()
    resid_all_st = [("sVD0", ctx.ctr["sVD0"]), ("sVD1", ctx.ctr["sVD1"])]

    for e in ENGINES:
        ctx.guard(e, wo_done)

    wbf_free = [[], []]
    wbf_ready = [None, None]
    rs_free = [[], []]
    ysb_free = [list(resid_all_st), list(resid_all_st)]
    silu_free = [[], []]
    n_y = 0

    def stream_weight(dram, mb, nquart, slot):
        sem = "sWD%d" % slot
        wd = dma("gpsimd", sem, wbf[slot][:, 0:dram[mb].shape[1]], dram[mb],
                 waits=wbf_free[slot])[sem]
        wbf_free[slot] = []
        wbf_ready[slot] = (sem, wd)

    def stream_weight_pair(d1, d2, mb, slot):
        sem = "sWD%d" % slot
        dma("gpsimd", sem, wbf[slot][:, 0:1024], d1[mb], waits=wbf_free[slot])
        wd = dma("gpsimd", sem, wbf[slot][:, 1024:2048], d2[mb])[sem]
        wbf_free[slot] = []
        wbf_ready[slot] = (sem, wd)

    for p in range(2):
        tok0 = p * TP
        if p == 1:
            pass2 = [("sMM", ctx.ctr["sMM"])]
            ctx.guard("vector", pass2)
            ctx.guard("scalar", pass2)
        stA_ev = []
        stream_weight_pair(dw1, dw2, 0, 0)
        for mb in range(32):
            slot = mb % 2
            if mb + 1 < 32:
                stream_weight_pair(dw1, dw2, mb + 1, 1 - slot)
            last_mm_for_slot = None
            for t2 in range(NT2):
                bA = banks.take()
                wA = banks.mm_waits(bA) + ([wbf_ready[slot]] if t2 == 0 else [])
                for kc in range(8):
                    ctx.emit("tensor", wA if kc == 0 else [],
                             lambda e, bA=bA, kc=kc, t2=t2, slot=slot, tok0=tok0, st=(kc == 0):
                             e.matmul(pb[bA][:, :], wbf[slot][:, kc * 128:(kc + 1) * 128],
                                      xn[:, kc * seg + tok0 + t2 * 512: kc * seg + tok0 + t2 * 512 + 512],
                                      start=st, stop=(kc == 7)), [("sMM", 1)] if kc == 7 else [])
                banks.set_ready(bA, "sMM", ctx.ctr["sMM"])
                bB = banks.take()
                wB = banks.mm_waits(bB)
                for kc in range(8):
                    ctx.emit("tensor", wB if kc == 0 else [],
                             lambda e, bB=bB, kc=kc, t2=t2, slot=slot, tok0=tok0, st=(kc == 0):
                             e.matmul(pb[bB][:, :], wbf[slot][:, 1024 + kc * 128: 1024 + (kc + 1) * 128],
                                      xn[:, kc * seg + tok0 + t2 * 512: kc * seg + tok0 + t2 * 512 + 512],
                                      start=st, stop=(kc == 7)), [("sMM", 1)] if kc == 7 else [])
                banks.set_ready(bB, "sMM", ctx.ctr["sMM"])
                last_mm_for_slot = ctx.ctr["sMM"]
                ssl = (mb * NT2 + t2) % 2
                sc_si = ctx.emit("scalar", banks.evac_wait(bA) + silu_free[ssl],
                                 lambda e, bA=bA, mb=mb, ssl=ssl:
                                 e.activation(silu_s[ssl][:, :], pb[bA][:, :], AF.Silu,
                                              scale=scl[:, mb:mb + 1]), [("sSC", 1)])["sSC"]
                banks.add_free(bA, "sSC", sc_si)
                ev = ctx.emit("vector", banks.evac_wait(bB) + [("sSC", sc_si)],
                              lambda e, bB=bB, mb=mb, t2=t2, ssl=ssl:
                              e.scalar_tensor_tensor(ha[:, mb * TP + t2 * 512: mb * TP + t2 * 512 + 512],
                                                     pb[bB][:, :], scl[:, 32 + mb: 33 + mb],
                                                     silu_s[ssl][:, :], ALU.mult, ALU.mult),
                              [("sEV", 1)])["sEV"]
                banks.add_free(bB, "sEV", ev)
                silu_free[ssl] = [("sEV", ev)]
                stA_ev.append(ev)
            wbf_free[slot] = [("sMM", last_mm_for_slot)]
        evA = ctx.ctr["sEV"]
        stream_weight(dh1, 0, 4, 0)
        for mb in range(32):
            slot = mb % 2
            if mb + 1 < 32:
                stream_weight(dh1, mb + 1, 4, 1 - slot)
            for t2 in range(NT2):
                b = banks.take()
                w = banks.mm_waits(b) + ([wbf_ready[slot], ("sEV", evA)] if t2 == 0 else [])
                for kc in range(32):
                    ctx.emit("tensor", w if kc == 0 else [],
                             lambda e, b=b, kc=kc, t2=t2, slot=slot, st=(kc == 0):
                             e.matmul(pb[b][:, :], wbf[slot][:, kc * 128:(kc + 1) * 128],
                                      ha[:, kc * TP + t2 * 512: kc * TP + t2 * 512 + 512],
                                      start=st, stop=(kc == 31)), [("sMM", 1)] if kc == 31 else [])
                banks.set_ready(b, "sMM", ctx.ctr["sMM"])
                last_mm_for_slot = ctx.ctr["sMM"]
                sc_ = ctx.emit("scalar", banks.evac_wait(b),
                               lambda e, b=b, mb=mb, t2=t2:
                               e.activation(hb[:, mb * TP + t2 * 512: mb * TP + t2 * 512 + 512],
                                            pb[b][:, :], AF.Gelu, scale=scl[:, 64 + mb: 65 + mb]),
                               [("sSC", 1)])["sSC"]
                banks.add_free(b, "sSC", sc_)
            wbf_free[slot] = [("sMM", last_mm_for_slot)]
        scB = ctx.ctr["sSC"]
        stream_weight(dh2, 0, 4, 0)
        for mb in range(32):
            slot = mb % 2
            if mb + 1 < 32:
                stream_weight(dh2, mb + 1, 4, 1 - slot)
            for t2 in range(NT2):
                b = banks.take()
                w = banks.mm_waits(b) + ([wbf_ready[slot], ("sSC", scB)] if t2 == 0 else [])
                for kc in range(32):
                    ctx.emit("tensor", w if kc == 0 else [],
                             lambda e, b=b, kc=kc, t2=t2, slot=slot, st=(kc == 0):
                             e.matmul(pb[b][:, :], wbf[slot][:, kc * 128:(kc + 1) * 128],
                                      hb[:, kc * TP + t2 * 512: kc * TP + t2 * 512 + 512],
                                      start=st, stop=(kc == 31)), [("sMM", 1)] if kc == 31 else [])
                banks.set_ready(b, "sMM", ctx.ctr["sMM"])
                last_mm_for_slot = ctx.ctr["sMM"]
                sc_ = ctx.emit("scalar", banks.evac_wait(b),
                               lambda e, b=b, mb=mb, t2=t2:
                               e.activation(ha[:, mb * TP + t2 * 512: mb * TP + t2 * 512 + 512],
                                            pb[b][:, :], AF.Gelu, scale=scl[:, 96 + mb: 97 + mb]),
                               [("sSC", 1)])["sSC"]
                banks.add_free(b, "sSC", sc_)
            wbf_free[slot] = [("sMM", last_mm_for_slot)]
        scC = ctx.ctr["sSC"]
        stream_weight(dw3, 0, 4, 0)
        for mb in range(8):
            slot = mb % 2
            if mb + 1 < 8:
                stream_weight(dw3, mb + 1, 4, 1 - slot)
            for t2 in range(NT2):
                rsl = (mb * NT2 + t2) % 2
                rsem = "sXD%d" % rsl
                xd = dma("gpsimd", rsem, rstg[rsl][:, :],
                         dresid[mb * 128:(mb + 1) * 128, tok0 + t2 * 512: tok0 + t2 * 512 + 512],
                         waits=rs_free[rsl] + resid_all_st)[rsem]
                b = banks.take()
                w = banks.mm_waits(b) + ([wbf_ready[slot], ("sSC", scC)] if t2 == 0 else [])
                for kc in range(32):
                    ctx.emit("tensor", w if kc == 0 else [],
                             lambda e, b=b, kc=kc, t2=t2, slot=slot, st=(kc == 0):
                             e.matmul(pb[b][:, :], wbf[slot][:, kc * 128:(kc + 1) * 128],
                                      ha[:, kc * TP + t2 * 512: kc * TP + t2 * 512 + 512],
                                      start=st, stop=(kc == 31)), [("sMM", 1)] if kc == 31 else [])
                banks.set_ready(b, "sMM", ctx.ctr["sMM"])
                last_mm_for_slot = ctx.ctr["sMM"]
                ysl = (mb * NT2 + t2) % 2
                ev = ctx.emit("vector", banks.evac_wait(b) + [(rsem, xd)] + ysb_free[ysl],
                              lambda e, b=b, mb=mb, rsl=rsl, ysl=ysl:
                              e.scalar_tensor_tensor(ysb[ysl][:, :], pb[b][:, :],
                                                     scl[:, 128 + mb: 129 + mb], rstg[rsl][:, :],
                                                     ALU.mult, ALU.add), [("sEV", 1)])["sEV"]
                banks.add_free(b, "sEV", ev)
                rs_free[rsl] = [("sEV", ev)]
                ysem = "sVD%d" % ysl
                st = ctx.emit("scalar", [("sEV", ev)],
                              lambda e, ysl=ysl, mb=mb, t2=t2, tok0=tok0:
                              e.dma_start(out=dy[mb * 128:(mb + 1) * 128, tok0 + t2 * 512: tok0 + t2 * 512 + 512],
                                          in_=ysb[ysl][:, :]), [(ysem, 16)], is_dma=True)[ysem]
                n_y += 1
                ysb_free[ysl] = [("sEV", ev), (ysem, st)]
            wbf_free[slot] = [("sMM", last_mm_for_slot)]

    ctx.guard("scalar", [("sVD0", ctx.ctr["sVD0"]), ("sVD1", ctx.ctr["sVD1"])])

    with nc.Block() as blk:
        @blk.sync
        def _(e):
            for fn in ctx.q["sync"]:
                fn(e)

        @blk.gpsimd
        def _(e):
            for fn in ctx.q["gpsimd"]:
                fn(e)

        @blk.tensor
        def _(e):
            for fn in ctx.q["tensor"]:
                fn(e)

        @blk.vector
        def _(e):
            for fn in ctx.q["vector"]:
                fn(e)

        @blk.scalar
        def _(e):
            for fn in ctx.q["scalar"]:
                fn(e)

    es.close()
    return nc


def _prep_inputs(inputs, seg=2048):
    f = np.ascontiguousarray
    x = inputs["L0_attn_hidden_states"]
    Wq = inputs["L0_attn_Wq"]; Wk = inputs["L0_attn_Wk"]; Wv = inputs["L0_attn_Wv"]
    Wo = inputs["L0_attn_Wo"]; Wg1 = inputs["L0_attn_Wg1"]; Wg2 = inputs["L0_attn_Wg2"]
    bg2 = inputs["L0_attn_bg2"]; gnw = inputs["L0_attn_g_norm_weight"]
    w1 = inputs["L0_ffn_w1_w"]; s1 = inputs["L0_ffn_w1_s"]
    w2 = inputs["L0_ffn_w2_w"]; s2 = inputs["L0_ffn_w2_s"]
    h1 = inputs["L0_ffn_h1_w"]; hs1 = inputs["L0_ffn_h1_s"]
    h2 = inputs["L0_ffn_h2_w"]; hs2 = inputs["L0_ffn_h2_s"]
    w3 = inputs["L0_ffn_w3_w"]; s3 = inputs["L0_ffn_w3_s"]

    perm = []
    for j in range(2):
        for hp in range(4):
            for habs in (8 * j + hp, 8 * j + 4 + hp):
                perm.extend(range(habs * 64, habs * 64 + 64))
    perm = np.array(perm)

    wq_p = f(Wq[:, perm].reshape(8, 128, 1024))
    wo_p = f(Wo[perm, :].reshape(8, 128, 1024))
    wk_p = f(Wk.reshape(8, 128, 256))
    wv_p = f(Wv.reshape(8, 128, 256))
    wg1_p = f(Wg1.reshape(8, 128, 16))
    wg2a = f(np.vstack([Wg2, bg2[None, :]]))

    def tiles(wt, nmb):
        IN, OUT = wt.shape
        nkc = IN // 128
        return f(wt.reshape(nkc, 128, nmb, 128).transpose(2, 1, 0, 3).reshape(nmb, 128, nkc * 128))

    w1t = tiles(w1.T, 32)
    w2t = tiles(w2.T, 32)
    h1t = tiles(h1.T, 32)
    h2t = tiles(h2.T, 32)
    w3t = tiles(w3.T, 8)

    scl = np.zeros((128, 137), np.float32)
    scl[:, 0:32] = s1.reshape(32, 128).T
    scl[:, 32:64] = s2.reshape(32, 128).T
    scl[:, 64:96] = hs1.reshape(32, 128).T
    scl[:, 96:128] = hs2.reshape(32, 128).T
    scl[:, 128:136] = s3.reshape(8, 128).T
    scl[:, 136] = np.tile(gnw, 2)
    scl = f(scl)

    cstv = np.zeros((128, 8), np.float32)
    cstv[:, 0] = 1e-5
    cstv[:, 1] = 1e-6
    cstv[:, 2] = 1.0
    cstv = f(cstv)

    sidx = np.arange(C)
    m_incl = (sidx[:, None] <= sidx[None, :]).astype(np.float32)
    m_after = (sidx[:, None] > sidx[None, :]).astype(np.float32)
    z64 = np.zeros((C, C), np.float32)
    mcat = np.zeros((128, 256), np.float32)
    mcat[0:64, 0:64] = m_incl
    mcat[0:64, 64:128] = m_after
    mcat[64:128, 128:192] = m_incl
    mcat[64:128, 192:256] = m_after
    mcat = f(mcat)
    maskP0 = np.zeros((128, 256), np.float32)
    maskP0[0:64, :] = np.tile(m_incl, (1, 4))
    maskP1 = np.zeros((128, 256), np.float32)
    maskP1[64:128, :] = np.tile(m_incl, (1, 4))
    maskP0 = f(maskP0.astype(BF16))
    maskP1 = f(maskP1.astype(BF16))
    ident = f(np.eye(128, dtype=np.float32).astype(BF16))
    onesv = np.zeros((128, 4), np.float32)
    onesv[:, 0] = 1.0
    onesv[:64, 1] = 1.0
    onesv[64:, 2] = 1.0
    onesv = f(onesv)
    onesr = np.zeros((2, 128), np.float32)
    onesr[0, :] = 1.0
    onesr = f(onesr)
    hbcm = np.zeros((2, 128), np.float32)
    hbcm[0, 0:64] = 1.0
    hbcm[1, 64:128] = 1.0
    hbcm = f(hbcm)

    onesbv = np.ones((128, 1), BF16)
    shared = dict(onesb=onesbv,
                  wq=wq_p, wk=wk_p, wv=wv_p, wg1=wg1_p, wg2a=wg2a, wo=wo_p,
                  w1t=w1t, w2t=w2t, h1t=h1t, h2t=h2t, w3t=w3t,
                  scl=scl, cst=cstv, mcat=mcat, maskP0=maskP0, maskP1=maskP1, ident=ident,
                  onesv=onesv, onesr=onesr, hbc=hbcm)

    nseg = x.shape[1] // seg
    in_maps = []
    for core in range(NCORE):
        b, sgi = divmod(core, nseg)
        m = dict(shared)
        m["x"] = f(x[b, sgi * seg:(sgi + 1) * seg, :].T)
        mu = np.zeros(8, np.float32)
        nu = np.zeros((8, 8), np.float32)
        for i in range(8):
            if i // nseg == b and i < core:
                mu[i] = 1.0
                for l in range(8):
                    if l // nseg == b and i < l < core:
                        nu[i, l] = 1.0
        m["muv"] = f(np.tile(mu[None, :], (128, 1)))
        m["nuv"] = f(np.tile(nu.reshape(1, 64), (128, 1)))
        in_maps.append(m)
    return in_maps


def kernel(**inputs):
    seg = 2048
    in_maps = _prep_inputs(inputs, seg)
    nc = build_program(seg)
    res = run_bass_kernel_spmd(nc, in_maps, core_ids=list(range(NCORE)))
    out = np.empty((2, 8192, 1024), np.float32)
    nseg = 8192 // seg
    for core in range(NCORE):
        b, sgi = divmod(core, nseg)
        out[b, sgi * seg:(sgi + 1) * seg, :] = res.results[core]["y"].T
    return out
